# version 12
# baseline (speedup 1.0000x reference)
import numpy as np
import ml_dtypes

import concourse.bass as bass
import concourse.mybir as mybir
from concourse.bass_utils import run_bass_kernel_spmd

F32 = mybir.dt.float32
BF16 = mybir.dt.bfloat16
AF = mybir.ActivationFunctionType
ALU = mybir.AluOpType
AX = mybir.AxisListType
bfdt = ml_dtypes.bfloat16

N_CORES = 8
QW = 512
KW = 128


class Prog:

    NQ, NG_ = 4, 8
    ENGS = (["T", "S", "V"] + [f"Q{i}" for i in range(NQ)]
            + [f"G{i}" for i in range(NG_)])
    DMA = {e for e in ENGS if e[0] in "QG"}

    def __init__(self):
        self.ops = {"T": [], "S": [], "V": [], "Q": [], "G": []}
        self.seq = {e: 0 for e in self.ENGS}
        self.pend_r = {e: [] for e in self.ENGS}
        self.pend_w = {e: [] for e in self.ENGS}
        self.bufw = {}
        self.bufr = {}
        self.waited = {e: {e2: 0 for e2 in self.ENGS} for e in self.ENGS}
        self.rr = {"Q": 0, "G": 0}
        self.n_waits = 0

    def op(self, eng, fn, reads=(), writes=(), inc=True):
        stream = eng
        if eng == "Q":
            eng = f"Q{self.rr['Q'] % self.NQ}"
            self.rr["Q"] += 1
        elif eng == "G":
            eng = f"G{self.rr['G'] % self.NG_}"
            self.rr["G"] += 1
        waits = {}

        def need(dep_eng, dep_seq, buf):
            if dep_eng == eng == "T" and buf.startswith("P"):
                return
            waits[dep_eng] = max(waits.get(dep_eng, 0), dep_seq)

        if eng in self.DMA and self.seq[eng] > 0:
            waits[eng] = self.seq[eng]
        for b in reads:
            w = self.bufw.get(b)
            if w:
                need(w[0], w[1], b)
        for b in writes:
            w = self.bufw.get(b)
            if w:
                need(w[0], w[1], b)
            for re, rs in self.bufr.get(b, {}).items():
                need(re, rs, b)
        todo = []
        for de, ds in waits.items():
            if ds > self.waited[eng][de]:
                self.waited[eng][de] = ds
                todo.append((de, ds))
        self.n_waits += len(todo)
        self.pend_r[eng].extend(reads)
        self.pend_w[eng].extend(writes)
        if inc:
            self.seq[eng] += 1
            s = self.seq[eng]
            for b in self.pend_w[eng]:
                self.bufw[b] = (eng, s)
                self.bufr.pop(b, None)
            for b in self.pend_r[eng]:
                self.bufr.setdefault(b, {})[eng] = s
            self.pend_r[eng] = []
            self.pend_w[eng] = []
        mult = 16 if eng in self.DMA else 1
        myeng = eng
        dmaset = self.DMA

        def go(p, sems):
            for de, ds in todo:
                p.wait_ge(sems[de], ds * (16 if de in dmaset else 1))
            inst = fn(p)
            if inc and inst is not None:
                inst.then_inc(sems[myeng], mult)

        self.ops[stream].append(go)

    def final_wait(self, eng, dep_stream):
        subs = [e for e in self.ENGS if e.startswith(dep_stream) and e != dep_stream] \
            if dep_stream in ("Q", "G") else [dep_stream]
        vals = [(de, self.seq[de] * (16 if de in self.DMA else 1)) for de in subs]

        def go(p, sems):
            for de, v in vals:
                if v:
                    p.wait_ge(sems[de], v)

        self.ops[eng].append(go)


def rope_tables_fm(T, base):
    i = np.arange(0, 128, 2, dtype=np.float32)
    inv = (np.float32(1.0) / (np.float32(base) ** (i / np.float32(128)))).astype(np.float32)
    t = np.arange(T, dtype=np.float32)
    f = t[None, :] * inv[:, None]
    cos = np.cos(f).astype(np.float32)
    sin = np.sin(f).astype(np.float32)
    return (np.ascontiguousarray(np.concatenate([cos, cos], axis=0)),
            np.ascontiguousarray(np.concatenate([sin, sin], axis=0)))


def host_constants(T, E):
    c = {}
    c["cosA"], c["sinA"] = rope_tables_fm(T, 1000000.0)
    c["cosB"], c["sinB"] = rope_tables_fm(T, 10000.0)
    S = np.zeros((128, 128), np.float32)
    for d in range(64):
        S[d + 64, d] = -1.0
        S[d, d + 64] = 1.0
    c["Smat"] = S
    c["ident"] = np.eye(128, dtype=np.float32)
    c["identb"] = np.eye(128, dtype=np.float32).astype(bfdt)
    n_c = QW // KW
    kk = np.arange(KW)[:, None]
    qq = np.arange(QW)[None, :]
    masks = np.stack([(kk <= qq - KW * cc).astype(np.float32) for cc in range(n_c)])
    c["masks"] = np.ascontiguousarray(masks.transpose(1, 0, 2).reshape(KW, n_c * QW)).astype(bfdt)
    return c


def layout_weights(inp, T, E):
    NH = E // 128
    NEC = E // 128
    NG = np.asarray(inp["L0_attn_b_wk"]).shape[1] // 128
    w = {}

    def heads_img(A, nh, hw):
        return np.ascontiguousarray(
            A.reshape(NEC, 128, nh, hw).transpose(2, 1, 0, 3).reshape(nh, 128, NEC * hw))

    def fchunk_img(A):
        return np.ascontiguousarray(
            A.reshape(NH, 128, NEC, 128).transpose(2, 1, 0, 3).reshape(NEC, 128, NH * 128))

    wq_a = np.asarray(inp["L0_attn_a_wq"], np.float32)
    wk_a = np.asarray(inp["L0_attn_a_wk"], np.float32)
    wv_a = np.asarray(inp["L0_attn_a_wv"], np.float32)
    wo_a = np.asarray(inp["L0_attn_a_wo"], np.float32)
    w["wqa"] = heads_img(wq_a.T, NH, 128)
    w["wka"] = heads_img(wk_a.T, NH, 128)
    w["woa"] = fchunk_img(wo_a.T)
    w["wva"] = heads_img(wv_a.T, NH // 2, 256)

    wq_b = np.asarray(inp["L0_attn_b_wq"], np.float32)
    wk_b = np.asarray(inp["L0_attn_b_wk"], np.float32)
    wv_b = np.asarray(inp["L0_attn_b_wv"], np.float32)
    wo_b = np.asarray(inp["L0_attn_b_wo"], np.float32)
    w["wqb"] = heads_img(wq_b, NH, 128)
    w["wkb"] = heads_img(wk_b, NG, 128)
    nvp = max(1, NG // 2)
    vwb = wv_b.shape[1] // nvp
    w["wvb"] = heads_img(wv_b, nvp, vwb)
    w["wob"] = fchunk_img(wo_b)

    rw = np.asarray(inp["L0_ffn_router_w"], np.float32)
    dw = np.asarray(inp["L0_ffn_down_w"], np.float32)
    uw = np.asarray(inp["L0_ffn_up_w"], np.float32)
    D_all = dw.transpose(2, 0, 1).reshape(E, 128)
    C = np.concatenate([D_all, rw.T], axis=1)
    w["dr"] = np.ascontiguousarray(
        C.reshape(NEC, 128, 136).transpose(1, 0, 2).reshape(128, NEC * 136))
    w["u"] = np.ascontiguousarray(uw.transpose(0, 2, 1).reshape(128, E))

    w["bq"] = np.ascontiguousarray(np.asarray(inp["L0_attn_b_bq"], np.float32).reshape(NH, 128).T)
    w["bk"] = np.ascontiguousarray(np.asarray(inp["L0_attn_b_bk"], np.float32).reshape(NG, 128).T)
    w["bo"] = np.ascontiguousarray(np.asarray(inp["L0_attn_b_bo"], np.float32).reshape(NH, 128).T)
    w["bv"] = np.ascontiguousarray(np.asarray(inp["L0_attn_b_bv"], np.float32).reshape(1, NG * 128))
    w["qns"] = np.ascontiguousarray(np.asarray(inp["L0_attn_b_q_norm_scale"], np.float32).reshape(128, 1))
    w["kns"] = np.ascontiguousarray(np.asarray(inp["L0_attn_b_k_norm_scale"], np.float32).reshape(128, 1))
    return w


def build_program(T, E, NG):
    NH = E // 128
    NEC = E // 128
    NTC = T // 128
    NQC = T // QW
    n_c = QW // KW
    rep = NH // NG
    nvp = max(1, NG // 2)
    vwb = (NG * 128) // nvp
    scale_a = float(np.float32(1.0) / np.sqrt(np.float32(128)))
    scale_b = float((np.float64(128.0) ** -0.5) ** 2)

    nc = bass.Bass("TRN2", target_bir_lowering=False)

    xin = nc.dram_tensor("xin", [E, T], F32, kind="ExternalInput").ap()
    out = nc.dram_tensor("out", [E, T], F32, kind="ExternalOutput").ap()
    d_cosA = nc.dram_tensor("cosA", [128, T], F32, kind="ExternalInput").ap()
    d_sinA = nc.dram_tensor("sinA", [128, T], F32, kind="ExternalInput").ap()
    d_cosB = nc.dram_tensor("cosB", [128, T], F32, kind="ExternalInput").ap()
    d_sinB = nc.dram_tensor("sinB", [128, T], F32, kind="ExternalInput").ap()
    d_S = nc.dram_tensor("Smat", [128, 128], F32, kind="ExternalInput").ap()
    d_ident = nc.dram_tensor("ident", [128, 128], F32, kind="ExternalInput").ap()
    d_identb = nc.dram_tensor("identb", [128, 128], BF16, kind="ExternalInput").ap()
    d_masks = nc.dram_tensor("masks", [KW, n_c * QW], BF16, kind="ExternalInput").ap()
    d_wqa = nc.dram_tensor("wqa", [NH, 128, NEC * 128], F32, kind="ExternalInput").ap()
    d_wka = nc.dram_tensor("wka", [NH, 128, NEC * 128], F32, kind="ExternalInput").ap()
    d_wva = nc.dram_tensor("wva", [NH // 2, 128, NEC * 256], F32, kind="ExternalInput").ap()
    d_woa = nc.dram_tensor("woa", [NEC, 128, NH * 128], F32, kind="ExternalInput").ap()
    d_wqb = nc.dram_tensor("wqb", [NH, 128, NEC * 128], F32, kind="ExternalInput").ap()
    d_wkb = nc.dram_tensor("wkb", [NG, 128, NEC * 128], F32, kind="ExternalInput").ap()
    d_wvb = nc.dram_tensor("wvb", [nvp, 128, NEC * vwb], F32, kind="ExternalInput").ap()
    d_wob = nc.dram_tensor("wob", [NEC, 128, NH * 128], F32, kind="ExternalInput").ap()
    d_dr = nc.dram_tensor("dr", [128, NEC * 136], F32, kind="ExternalInput").ap()
    d_u = nc.dram_tensor("u", [128, E], F32, kind="ExternalInput").ap()
    d_bq = nc.dram_tensor("bq", [128, NH], F32, kind="ExternalInput").ap()
    d_bk = nc.dram_tensor("bk", [128, NG], F32, kind="ExternalInput").ap()
    d_bo = nc.dram_tensor("bo", [128, NH], F32, kind="ExternalInput").ap()
    d_bv = nc.dram_tensor("bv", [1, NG * 128], F32, kind="ExternalInput").ap()
    d_qns = nc.dram_tensor("qns", [128, 1], F32, kind="ExternalInput").ap()
    d_kns = nc.dram_tensor("kns", [128, 1], F32, kind="ExternalInput").ap()
    osp = nc.dram_tensor("osp", [NH * 128, T], BF16, kind="Internal").ap()

    A = nc.alloc_sbuf_tensor
    x_t = [A(f"x{i}", [128, T], BF16) for i in range(NEC)]
    y_t = [A(f"y{i}", [128, T], BF16) for i in range(NEC)]
    WSW = max(NEC * 256, NEC * 136 + E, NEC * vwb, NH * 128)
    ws = [A("ws0", [128, WSW], BF16), A("ws1", [128, WSW], BF16)]
    cos_s = A("cos_s", [128, T], BF16)
    sin_s = A("sin_s", [128, T], BF16)
    q_sb = A("q_sb", [128, T], BF16)
    k_sb = A("k_sb", [128, T], BF16)
    tmp1 = A("tmp1", [128, T], BF16)
    v_tm = [A(f"vtm{i}", [128, 256], BF16) for i in range(NTC)]
    expT = [A("expT0", [128, 2 * QW], BF16), A("expT1", [128, 2 * QW], BF16)]
    recipZ = A("recipZ", [128, QW], BF16)
    outn = A("outn", [128, QW], BF16)
    mB = A("mB", [128, T], BF16)
    rstdB = A("rstdB", [128, T], BF16)
    m2s = A("m2s", [128, QW], BF16)
    vars_ = A("vars", [128, QW], F32)
    sqs = [A("sqs0", [128, QW], BF16), A("sqs1", [128, QW], BF16)]
    stage = [A("stage0", [128, QW], F32), A("stage1", [128, QW], F32)]
    agT = A("agT", [128, T], BF16)
    a_sb = [A("a_sb0", [128, 128], BF16), A("a_sb1", [128, 128], BF16)]
    ag_sb = [A("ag_sb0", [128, 128], F32), A("ag_sb1", [128, 128], F32)]
    w8 = A("w8", [128, 8], F32)
    g8 = A("g8", [128, 16], F32)
    smalls = A("smalls", [128, 16], F32)
    ident_s = A("ident_s", [128, 128], F32)
    identb_s = A("identb_s", [128, 128], BF16)
    allones = A("allones", [128, 128], BF16)
    Smat_s = A("Smat_s", [128, 128], BF16)
    masks_s = A("masks_s", [128, n_c * QW], BF16)
    bq_s = A("bq_s", [128, NH], F32)
    bk_s = A("bk_s", [128, NG], F32)
    bo_s = A("bo_s", [128, NH], F32)
    bv_pad = A("bv_pad", [128, vwb], BF16)
    qns_s = A("qns_s", [128, 1], F32)
    kns_s = A("kns_s", [128, 1], F32)
    ebias = A("ebias", [128, NQC], F32)
    eps5 = A("eps5", [128, 1], F32)
    eps6 = A("eps6", [128, 1], F32)

    PT = [nc.alloc_psum_tensor(f"PT{i}", [128, 1024], F32) for i in range(4)]

    def bank(b, n=512):
        return PT[b // 2][:, (b % 2) * 512:(b % 2) * 512 + n]

    PS = [bank(i) for i in range(8)]

    P = Prog()

    def dma(eng, dst_ap, src_ap, reads, writes):
        P.op(eng, lambda p, d=dst_ap, s=src_ap: p.dma_start(out=d, in_=s),
             reads=reads, writes=writes)

    def mmN(bi, n, lhsT, rhs, reads, start, stop, inc=True):
        P.op("T", lambda p, l=lhsT, r=rhs, st=start, sp=stop, b=bi, nn=n:
             p.matmul(bank(b, nn), l, r, start=st, stop=sp),
             reads=reads, writes=[f"P{bi}"], inc=inc)

    def mm2(bi, lhsT, rhs, reads, start, stop, inc=True):
        mmN(bi, QW, lhsT, rhs, reads, start, stop, inc)

    P.op("V", lambda p: p.memset(allones[:], 1.0), writes=["allones"])
    P.op("V", lambda p: p.memset(eps5[:], 1e-5), writes=["eps5"])
    P.op("V", lambda p: p.memset(eps6[:], 1e-6), writes=["eps6"])
    P.op("V", lambda p: p.memset(bv_pad[:], 0.0), writes=["bv_pad"])
    dma("Q", ident_s[:], d_ident[:, :], [], ["ident_s"])
    dma("Q", identb_s[:], d_identb[:, :], [], ["identb_s"])
    dma("Q", masks_s[:], d_masks[:, :], [], ["masks_s"])
    dma("G", Smat_s[:], d_S[:, :], [], ["Smat_s"])
    dma("G", cos_s[:], d_cosA[:, :], [], ["cos_s"])
    dma("G", sin_s[:], d_sinA[:, :], [], ["sin_s"])
    dma("G", bq_s[:], d_bq[:, :], [], ["bq_s"])
    dma("G", bk_s[:], d_bk[:, :], [], ["bk_s"])
    dma("G", bo_s[:], d_bo[:, :], [], ["bo_s"])
    dma("G", qns_s[:], d_qns[:, :], [], ["qns_s"])
    dma("G", kns_s[:], d_kns[:, :], [], ["kns_s"])
    for i in range(NEC):
        dma("G", x_t[i][:], xin[i * 128:(i + 1) * 128, :], [], [f"x{i}"])

    phases = []

    def add_phase(dmas):
        phases.append(dmas)
        return len(phases) - 1

    emitted = set()

    def emit_phase(idx):
        if idx in emitted or idx >= len(phases):
            return
        emitted.add(idx)
        slot = idx % 2
        name = f"ws{slot}"
        for (off, width, src) in phases[idx]:
            dma("G", ws[slot][:, off:off + width], src, [name], [name])

    def slot_of(idx):
        return idx % 2, f"ws{idx % 2}"

    def layer_norm():
        steps = [(ec, tc) for ec in range(NEC) for tc in range(NQC)]

        def square(i):
            ec, tc = steps[i]
            sq, sqn = sqs[i % 2], f"sqs{i % 2}"
            sl = slice(tc * QW, (tc + 1) * QW)
            P.op("V", lambda p, e=ec, s=sl, q=sq: p.tensor_tensor(
                q[:], x_t[e][:, s], x_t[e][:, s], ALU.mult),
                reads=[f"x{ec}"], writes=[sqn])

        square(0)
        for i, (ec, tc) in enumerate(steps):
            if i + 1 < len(steps):
                square(i + 1)
            sqn = f"sqs{i % 2}"
            sl = slice(tc * QW, (tc + 1) * QW)
            mm2(tc, allones[:], x_t[ec][:, sl], ["allones", f"x{ec}"],
                start=(ec == 0), stop=(ec == NEC - 1))
            mm2(4 + tc, allones[:], sqs[i % 2][:], ["allones", sqn],
                start=(ec == 0), stop=(ec == NEC - 1))
        for tc in range(NQC):
            sl = slice(tc * QW, (tc + 1) * QW)
            P.op("S", lambda p, t=tc, s=sl: p.activation(
                mB[:, s], PS[t][:, 0:QW], AF.Identity, scale=1.0 / E),
                reads=[f"P{tc}"], writes=["mB"])
            P.op("S", lambda p, s=sl: p.activation(m2s[:], mB[:, s], AF.Square),
                 reads=["mB"], writes=["m2s"])
            P.op("V", lambda p, t=tc: p.scalar_tensor_tensor(
                vars_[:], PS[4 + t][:, 0:QW], 1.0 / E, m2s[:], ALU.mult, ALU.subtract),
                reads=[f"P{4 + tc}", "m2s"], writes=["vars"])
            P.op("S", lambda p: p.activation(vars_[:], vars_[:], AF.Ln, bias=eps5[:]),
                 reads=["vars", "eps5"], writes=["vars"])
            P.op("S", lambda p, s=sl: p.activation(rstdB[:, s], vars_[:], AF.Exp, scale=-0.5),
                 reads=["vars"], writes=["rstdB"])
            P.op("V", lambda p, s=sl: p.tensor_tensor(mB[:, s], mB[:, s], rstdB[:, s], ALU.mult),
                 reads=["rstdB"], writes=["mB"])
        for ec in range(NEC):
            P.op("V", lambda p, e=ec: p.tensor_tensor(y_t[e][:], x_t[e][:], rstdB[:], ALU.mult),
                 reads=[f"x{ec}", "rstdB"], writes=[f"y{ec}"])
            P.op("V", lambda p, e=ec: p.tensor_tensor(y_t[e][:], y_t[e][:], mB[:], ALU.subtract),
                 reads=["mB"], writes=[f"y{ec}"])

    def proj(dst, dstname, sname, slot_ap, woff, bias_ap, bias_name, after_tc=None):
        for tc in range(NQC):
            b = 6 + (tc % 2)
            for ec in range(NEC):
                lhs = slot_ap[:, woff + ec * 128: woff + (ec + 1) * 128]
                mm2(b, lhs, y_t[ec][:, tc * QW:(tc + 1) * QW],
                    [sname, f"y{ec}"], start=(ec == 0), stop=(ec == NEC - 1),
                    inc=(ec == NEC - 1))
            if bias_ap is None:
                P.op("S", lambda p, bb=b, t=tc: p.activation(
                    dst[:, t * QW:(t + 1) * QW], PS[bb][:, 0:QW], AF.Identity),
                    reads=[f"P{b}"], writes=[f"{dstname}.{tc}"])
            else:
                P.op("S", lambda p, bb=b, t=tc, ba=bias_ap: p.activation(
                    dst[:, t * QW:(t + 1) * QW], PS[bb][:, 0:QW], AF.Identity, bias=ba),
                    reads=[f"P{b}", bias_name], writes=[f"{dstname}.{tc}"])
            if after_tc is not None:
                after_tc(tc)

    def rope_tc(buf, bufname, tc, bank0=4):
        b = bank0 + (tc % 2)
        sl = slice(tc * QW, (tc + 1) * QW)
        nm = f"{bufname}.{tc}"
        P.op("V", lambda p, s=sl: p.tensor_tensor(
            tmp1[:, s], buf[:, s], sin_s[:, s], ALU.mult),
            reads=[nm, "sin_s"], writes=[f"tmp1.{tc}"])
        P.op("V", lambda p, s=sl: p.tensor_tensor(
            buf[:, s], buf[:, s], cos_s[:, s], ALU.mult),
            reads=["cos_s", nm], writes=[nm])
        mm2(b, identb_s[:], buf[:, sl], ["identb_s", nm], start=True, stop=False)
        mm2(b, Smat_s[:], tmp1[:, sl], ["Smat_s", f"tmp1.{tc}"], start=False, stop=True)
        P.op("S", lambda p, bb=b, s=sl: p.activation(
            buf[:, s], PS[bb][:, 0:QW], AF.Identity),
            reads=[f"P{b}"], writes=[nm])

    def rms_tc(buf, bufname, tc, scale_ap, scale_name):
        sq, sqn = sqs[tc % 2], f"sqs{tc % 2}"
        b = 4 + (tc % 2)
        sl = slice(tc * QW, (tc + 1) * QW)
        nm = f"{bufname}.{tc}"
        P.op("V", lambda p, s=sl, q=sq: p.tensor_tensor(
            q[:], buf[:, s], buf[:, s], ALU.mult),
            reads=[nm], writes=[sqn])
        mm2(b, allones[:], sq[:], ["allones", sqn], start=True, stop=True)
        P.op("S", lambda p, bb=b: p.activation(vars_[:], PS[bb][:, 0:QW], AF.Ln,
                                               scale=1.0 / 128.0, bias=eps6[:]),
             reads=[f"P{b}", "eps6"], writes=["vars"])
        P.op("S", lambda p, s=sl: p.activation(
            rstdB[:, s], vars_[:], AF.Exp, scale=-0.5),
            reads=["vars"], writes=["rstdB"])
        P.op("V", lambda p, s=sl: p.tensor_tensor(buf[:, s], buf[:, s], rstdB[:, s], ALU.mult),
             reads=["rstdB", nm], writes=[nm])
        P.op("V", lambda p, sa=scale_ap, s=sl: p.tensor_scalar(
            buf[:, s], buf[:, s], sa, None, ALU.mult),
            reads=[scale_name, nm], writes=[nm])

    def norm_tc(buf, bufname, tc, scol, bank=4):
        b = bank + (tc % 2)
        sq, sqn = sqs[tc % 2], f"sqs{tc % 2}"
        sl = slice(tc * QW, (tc + 1) * QW)
        P.op("V", lambda p, s=sl, q=sq: p.tensor_tensor(
            q[:], buf[:, s], buf[:, s], ALU.mult),
            reads=[f"{bufname}.{tc}"], writes=[sqn])
        mm2(b, allones[:], sq[:], ["allones", sqn], start=True, stop=True)
        P.op("V", lambda p, bb=b, c=scol: p.reduce_max(
            smalls[:, c:c + 1], PS[bb][:, 0:QW], axis=AX.X),
            reads=[f"P{b}"], writes=["smalls"])

    def exp_bias(e_scale, qc):
        P.op("V", lambda p, c=qc: p.tensor_tensor(smalls[:, 2:3], smalls[:, 12 + c:13 + c],
                                                  smalls[:, 1:2], ALU.mult),
             reads=["smalls"], writes=["smalls"])
        P.op("S", lambda p: p.activation(smalls[:, 3:4], smalls[:, 2:3], AF.Ln),
             reads=["smalls"], writes=["smalls"])
        P.op("S", lambda p: p.activation(smalls[:, 3:4], smalls[:, 3:4], AF.Exp, scale=0.5),
             reads=["smalls"], writes=["smalls"])
        P.op("V", lambda p, es=e_scale: p.tensor_scalar(
            smalls[:, 4:5], smalls[:, 3:4], -es, 40.0, ALU.mult, ALU.add),
            reads=["smalls"], writes=["smalls"])
        P.op("V", lambda p, c=qc: p.tensor_scalar(ebias[:, c:c + 1], smalls[:, 4:5], 0.0, None, ALU.min),
             reads=["smalls"], writes=[f"ebias.{qc}"])

    def attn_head_qc(h, vslice, e_scale, qc):
        if True:
            qsl = slice(qc * QW, (qc + 1) * QW)
            nkc = (qc + 1) * n_c
            npair = nkc // 2

            def sc_exp(j):
                par = j % 2
                for half in range(2):
                    kc = 2 * j + half
                    mm2(2 * par + half, k_sb[:, kc * KW:(kc + 1) * KW], q_sb[:, qsl],
                        [f"k_sb.{kc // n_c}", f"q_sb.{qc}"], start=True, stop=True)
                P.op("S", lambda p, pa=par: p.activation(
                    expT[pa][:], PT[pa][:, 0:2 * QW], AF.Exp, bias=ebias[:, qc:qc + 1], scale=e_scale),
                    reads=[f"P{2 * par}", f"P{2 * par + 1}", f"ebias.{qc}"], writes=[f"expT{par}"])
                for half in range(2):
                    kc = 2 * j + half
                    cc = kc - qc * n_c
                    if cc >= 0:
                        P.op("V", lambda p, pa=par, c=cc, hf=half: p.tensor_tensor(
                            expT[pa][:, hf * QW:(hf + 1) * QW],
                            expT[pa][:, hf * QW:(hf + 1) * QW],
                            masks_s[:, c * QW:(c + 1) * QW], ALU.mult),
                            reads=["masks_s"], writes=[f"expT{par}"])

            def zav(j):
                par = j % 2
                for half in range(2):
                    kc = 2 * j + half
                    e_ap = expT[par][:, half * QW:(half + 1) * QW]
                    mm2(4, allones[:], e_ap, ["allones", f"expT{par}"],
                        start=(kc == 0), stop=(kc == nkc - 1))
                    mm2(5, v_tm[kc][:, vslice], e_ap,
                        [f"vtm{kc}", f"expT{par}"],
                        start=(kc == 0), stop=(kc == nkc - 1))

            sc_exp(0)
            for j in range(1, npair):
                sc_exp(j)
                zav(j - 1)
            zav(npair - 1)
            P.op("S", lambda p: p.activation(vars_[:], PS[4][:, 0:QW], AF.Ln),
                 reads=["P4"], writes=["vars"])
            P.op("S", lambda p: p.activation(recipZ[:], vars_[:], AF.Exp, scale=-1.0),
                 reads=["vars"], writes=["recipZ"])
            P.op("V", lambda p: p.tensor_tensor(outn[:], PS[5][:, 0:QW], recipZ[:], ALU.mult),
                 reads=["P5", "recipZ"], writes=["outn"])
            dma("Q", osp[h * 128:(h + 1) * 128, qc * QW:(qc + 1) * QW], outn[:],
                ["outn"], [f"osp{h}"])

    def v_proj(sname, slot_ap, vwidth, bvslice=None):
        for tg in range((NTC + 3) // 4):
            tchs = list(range(tg * 4, min((tg + 1) * 4, NTC)))
            for ec in range(NEC):
                for i, t in enumerate(tchs):
                    bank = 4 + i
                    last = (ec == NEC - 1) and bvslice is None
                    mmN(bank, vwidth, y_t[ec][:, t * 128:(t + 1) * 128],
                        slot_ap[:, ec * vwidth:(ec + 1) * vwidth],
                        [f"y{ec}", sname], start=(ec == 0), stop=last,
                        inc=(ec == NEC - 1))
            if bvslice is not None:
                for i, t in enumerate(tchs):
                    mmN(4 + i, vwidth, allones[:], bv_pad[:, 0:vwidth],
                        ["allones", "bv_pad"], start=False, stop=True)
            for i, t in enumerate(tchs):
                bank = 4 + i
                P.op("V", lambda p, b=bank, tt=t, vw=vwidth: p.tensor_copy(
                    v_tm[tt][:, 0:vw], PS[b][:, 0:vw]),
                    reads=[f"P{bank}"], writes=[f"vtm{t}"])

    def wo_phase(phase_base, with_bo):
        for h in range(NH):
            dma("Q", y_t[h][:], osp[h * 128:(h + 1) * 128, :], [f"osp{h}"], [f"y{h}"])
        emit_phase(phase_base)
        for fc in range(NEC):
            if fc + 1 < NEC:
                emit_phase(phase_base + fc + 1)
            slot, sname = slot_of(phase_base + fc)
            for tc in range(NQC):
                b = tc % 2
                for h in range(NH):
                    mm2(b, ws[slot][:, h * 128:(h + 1) * 128],
                        y_t[h][:, tc * QW:(tc + 1) * QW],
                        [sname, f"y{h}"], start=(h == 0), stop=(h == NH - 1),
                        inc=(h == NH - 1))
                if with_bo:
                    P.op("V", lambda p, bb=b, f=fc, t=tc: p.scalar_tensor_tensor(
                        x_t[f][:, t * QW:(t + 1) * QW], PS[bb][:, 0:QW],
                        bo_s[:, f:f + 1], x_t[f][:, t * QW:(t + 1) * QW],
                        ALU.add, ALU.add),
                        reads=[f"P{b}", "bo_s", f"x{fc}"], writes=[f"x{fc}"])
                else:
                    P.op("V", lambda p, bb=b, f=fc, t=tc: p.tensor_tensor(
                        x_t[f][:, t * QW:(t + 1) * QW],
                        x_t[f][:, t * QW:(t + 1) * QW], PS[bb][:, 0:QW], ALU.add),
                        reads=[f"P{b}", f"x{fc}"], writes=[f"x{fc}"])

    def ffn(phase_idx, last=False):
        layer_norm()
        emit_phase(phase_idx)
        slot, sname = slot_of(phase_idx)
        droff = 0
        uoff = NEC * 136
        def gate_chain(t):
            b = t % 2
            par = t % 2
            P.op("V", lambda p, bb=b: p.reduce_max(
                smalls[:, 0:1], PS[bb][:, 128:136], axis=AX.X),
                reads=[f"P{b}"], writes=["smalls"])
            P.op("V", lambda p: p.tensor_scalar(
                smalls[:, 1:2], smalls[:, 0:1], -1.0, None, ALU.mult),
                reads=["smalls"], writes=["smalls"])
            P.op("S", lambda p, bb=b: p.activation(
                w8[:], PS[bb][:, 128:136], AF.Exp, bias=smalls[:, 1:2]),
                reads=[f"P{b}", "smalls"], writes=["w8"])
            P.op("V", lambda p: p.reduce_sum(smalls[:, 2:3], w8[:], axis=AX.X),
                 reads=["w8"], writes=["smalls"])
            P.op("V", lambda p: p.reciprocal(smalls[:, 3:4], smalls[:, 2:3]),
                 reads=["smalls"], writes=["smalls"])
            P.op("V", lambda p: p.tensor_scalar(
                g8[:, 0:8], w8[:], 1.0, -1e9, ALU.is_ge, ALU.mult),
                reads=["w8"], writes=["g8"])
            P.op("V", lambda p: p.tensor_tensor(g8[:, 8:16], w8[:], g8[:, 0:8], ALU.add),
                 reads=["w8"], writes=["g8"])
            P.op("V", lambda p: p.reduce_max(smalls[:, 4:5], g8[:, 8:16], axis=AX.X),
                 reads=["g8"], writes=["smalls"])
            P.op("V", lambda p: p.tensor_scalar(
                g8[:, 0:8], w8[:], smalls[:, 4:5], None, ALU.is_ge),
                reads=["w8", "smalls"], writes=["g8"])
            P.op("V", lambda p: p.tensor_tensor(w8[:], w8[:], g8[:, 0:8], ALU.mult),
                 reads=["g8"], writes=["w8"])
            P.op("V", lambda p: p.tensor_scalar(
                w8[:], w8[:], smalls[:, 3:4], None, ALU.mult),
                reads=["smalls"], writes=["w8"])
            P.op("S", lambda p, bb=b: p.activation(
                vars_[:, 0:128], PS[bb][:, 0:128], AF.Exp, scale=-1.0),
                reads=[f"P{b}"], writes=["vars"])
            P.op("V", lambda p: p.tensor_scalar(vars_[:, 0:128], vars_[:, 0:128], 1.0, None, ALU.add),
                 reads=["vars"], writes=["vars"])
            P.op("V", lambda p: p.reciprocal(vars_[:, 128:256], vars_[:, 0:128]),
                 reads=["vars"], writes=["vars"])
            P.op("V", lambda p, bb=b, pa=par: p.tensor_tensor(
                a_sb[pa][:], PS[bb][:, 0:128], vars_[:, 128:256], ALU.mult),
                reads=[f"P{b}", "vars"], writes=[f"a_sb{par}"])
            for ex in range(8):
                P.op("V", lambda p, e=ex, pa=par: p.tensor_scalar(
                    ag_sb[pa][:, e * 16:(e + 1) * 16], a_sb[pa][:, e * 16:(e + 1) * 16],
                    w8[:, e:e + 1], None, ALU.mult),
                    reads=[f"a_sb{par}", "w8"], writes=[f"ag_sb{par}"], inc=(ex == 7))

        def transp(t):
            par = t % 2
            tb = 2 + (t % 2)
            P.op("T", lambda p, bb=tb, pa=par: p.transpose(
                PS[bb][:, 0:128], ag_sb[pa][:], ident_s[:]),
                reads=[f"ag_sb{par}", "ident_s"], writes=[f"P{tb}"])
            P.op("S", lambda p, bb=tb, tt=t: p.activation(
                agT[:, tt * 128:(tt + 1) * 128], PS[bb][:, 0:128], AF.Identity),
                reads=[f"P{tb}"], writes=["agT"])

        for t in range(NTC):
            b = t % 2
            for ec in range(NEC):
                mmN(b, 136, y_t[ec][:, t * 128:(t + 1) * 128],
                    ws[slot][:, droff + ec * 136: droff + (ec + 1) * 136],
                    [f"y{ec}", sname], start=(ec == 0), stop=(ec == NEC - 1),
                    inc=(ec == NEC - 1))
            if t > 0:
                transp(t - 1)
            gate_chain(t)
        transp(NTC - 1)
        for fc in range(NEC):
            for tc in range(NQC):
                b = 4 + ((fc * NQC + tc) % 2)
                mm2(b, ws[slot][:, uoff + fc * 128: uoff + (fc + 1) * 128],
                    agT[:, tc * QW:(tc + 1) * QW], [sname, "agT"],
                    start=True, stop=True)
                if last:
                    par = (fc * NQC + tc) % 2
                    P.op("V", lambda p, bb=b, f=fc, t=tc, pa=par: p.tensor_tensor(
                        stage[pa][:], x_t[f][:, t * QW:(t + 1) * QW],
                        PS[bb][:, 0:QW], ALU.add),
                        reads=[f"P{b}", f"x{fc}"], writes=[f"stage{par}"])
                    dma("Q", out[fc * 128:(fc + 1) * 128, tc * QW:(tc + 1) * QW],
                        stage[par][:], [f"stage{par}"], ["outhbm"])
                else:
                    P.op("V", lambda p, bb=b, f=fc, t=tc: p.tensor_tensor(
                        x_t[f][:, t * QW:(t + 1) * QW],
                        x_t[f][:, t * QW:(t + 1) * QW], PS[bb][:, 0:QW], ALU.add),
                        reads=[f"P{b}", f"x{fc}"], writes=[f"x{fc}"])

    gpv = NG // nvp
    ph_wva, ph_wqk = {}, {}
    for pair in range(NH // 2):
        ph_wva[pair] = add_phase([(0, NEC * 256, d_wva[pair, :, :])])
        for hh in range(2):
            h = pair * 2 + hh
            ph_wqk[h] = add_phase([(0, NEC * 128, d_wqa[h, :, :]),
                                   (NEC * 128, NEC * 128, d_wka[h, :, :])])
    ph_woa = [add_phase([(0, NH * 128, d_woa[fc, :, :])]) for fc in range(NEC)]
    PH_FFN1 = add_phase([(0, NEC * 136, d_dr[:, :]), (NEC * 136, E, d_u[:, :])])
    ph_wvb, ph_wkb, ph_wqb = {}, {}, {}
    for vp in range(nvp):
        ph_wvb[vp] = add_phase([(0, NEC * vwb, d_wvb[vp, :, :])])
        for gg in range(gpv):
            g = vp * gpv + gg
            ph_wkb[g] = add_phase([(0, NEC * 128, d_wkb[g, :, :])])
            for r in range(rep):
                h = g * rep + r
                ph_wqb[h] = add_phase([(0, NEC * 128, d_wqb[h, :, :])])
    ph_wob = [add_phase([(0, NH * 128, d_wob[fc, :, :])]) for fc in range(NEC)]
    PH_FFN2 = add_phase([(0, NEC * 136, d_dr[:, :]), (NEC * 136, E, d_u[:, :])])

    layer_norm()
    emit_phase(0)
    for pair in range(NH // 2):
        emit_phase(ph_wva[pair])
        emit_phase(ph_wva[pair] + 1)
        vslot, vsname = slot_of(ph_wva[pair])
        v_proj(vsname, ws[vslot], 256)
        for hh in range(2):
            h = pair * 2 + hh
            phq = ph_wqk[h]
            emit_phase(phq)
            emit_phase(phq + 1)
            qslot, qsname = slot_of(phq)
            proj(q_sb, "q_sb", qsname, ws[qslot], 0, None, None)

            def chains_a(tc):
                rope_tc(k_sb, "k_sb", tc)
                norm_tc(k_sb, "k_sb", tc, 8 + tc)
                rope_tc(q_sb, "q_sb", tc)
                norm_tc(q_sb, "q_sb", tc, 12 + tc)
            proj(k_sb, "k_sb", qsname, ws[qslot], NEC * 128, None, None,
                 after_tc=chains_a)
            P.op("V", lambda p: p.reduce_max(
                smalls[:, 1:2], smalls[:, 8:8 + NQC], axis=AX.X),
                reads=["smalls"], writes=["smalls"])
            exp_bias(scale_a, 0)
            for qc in range(NQC):
                if qc + 1 < NQC:
                    exp_bias(scale_a, qc + 1)
                attn_head_qc(h, slice(hh * 128, (hh + 1) * 128), scale_a, qc)
    wo_phase(ph_woa[0], with_bo=False)

    ffn(PH_FFN1)

    layer_norm()
    dma("G", cos_s[:], d_cosB[:, :], ["cos_s"], ["cos_s"])
    dma("G", sin_s[:], d_sinB[:, :], ["sin_s"], ["sin_s"])
    emit_phase(ph_wvb[0])
    for vp in range(nvp):
        emit_phase(ph_wvb[vp])
        emit_phase(ph_wvb[vp] + 1)
        vslot, vsname = slot_of(ph_wvb[vp])
        dma("G", bv_pad[0:1, :], d_bv[:, vp * vwb:(vp + 1) * vwb], ["bv_pad"], ["bv_pad"])
        v_proj(vsname, ws[vslot], vwb, bvslice=slice(vp * vwb, (vp + 1) * vwb))
        for gg in range(gpv):
            g = vp * gpv + gg
            phk = ph_wkb[g]
            emit_phase(phk)
            emit_phase(phk + 1)
            kslot, ksname = slot_of(phk)
            def kchain_b(tc):
                rms_tc(k_sb, "k_sb", tc, kns_s[:], "kns_s")
                rope_tc(k_sb, "k_sb", tc)
                norm_tc(k_sb, "k_sb", tc, 8 + tc)
            proj(k_sb, "k_sb", ksname, ws[kslot], 0, bk_s[:, g:g + 1], "bk_s",
                 after_tc=kchain_b)
            P.op("V", lambda p: p.reduce_max(
                smalls[:, 1:2], smalls[:, 8:8 + NQC], axis=AX.X),
                reads=["smalls"], writes=["smalls"])
            for r in range(rep):
                h = g * rep + r
                phq = ph_wqb[h]
                emit_phase(phq)
                emit_phase(phq + 1)
                qslot, qsname = slot_of(phq)
                def qchain_b(tc):
                    rms_tc(q_sb, "q_sb", tc, qns_s[:], "qns_s")
                    rope_tc(q_sb, "q_sb", tc)
                    norm_tc(q_sb, "q_sb", tc, 12 + tc)
                proj(q_sb, "q_sb", qsname, ws[qslot], 0, bq_s[:, h:h + 1], "bq_s",
                     after_tc=qchain_b)
                exp_bias(scale_b, 0)
                for qc in range(NQC):
                    if qc + 1 < NQC:
                        exp_bias(scale_b, qc + 1)
                    attn_head_qc(h, slice(gg * 128, (gg + 1) * 128), scale_b, qc)
    wo_phase(ph_wob[0], with_bo=True)

    ffn(PH_FFN2, last=True)

    P.final_wait("Q", "Q")
    P.final_wait("G", "G")
    assert len(emitted) == len(phases), (len(emitted), len(phases))

    import contextlib
    with contextlib.ExitStack() as stack:
        sems = {e: stack.enter_context(nc.semaphore(f"sem{e}")) for e in Prog.ENGS}
        block = stack.enter_context(nc.Block())

        @block.tensor
        def _(t):
            for f in P.ops["T"]:
                f(t, sems)

        @block.scalar
        def _(s):
            for f in P.ops["S"]:
                f(s, sems)

        @block.vector
        def _(v):
            for f in P.ops["V"]:
                f(v, sems)

        @block.sync
        def _(q):
            for f in P.ops["Q"]:
                f(q, sems)

        @block.gpsimd
        def _(g):
            for f in P.ops["G"]:
                f(g, sems)

    return nc


def make_in_maps(inputs):
    x = np.asarray(inputs["L0_attn_a_x"], np.float32)
    B, T, E = x.shape
    NG = np.asarray(inputs["L0_attn_b_wk"]).shape[1] // 128
    consts = host_constants(T, E)
    w = layout_weights(inputs, T, E)
    shared = {
        "cosA": consts["cosA"], "sinA": consts["sinA"],
        "cosB": consts["cosB"], "sinB": consts["sinB"],
        "Smat": consts["Smat"], "ident": consts["ident"], "identb": consts["identb"], "masks": consts["masks"],
        "wqa": w["wqa"], "wka": w["wka"], "wva": w["wva"], "woa": w["woa"],
        "wqb": w["wqb"], "wkb": w["wkb"], "wvb": w["wvb"], "wob": w["wob"],
        "dr": w["dr"], "u": w["u"],
        "bq": w["bq"], "bk": w["bk"], "bo": w["bo"], "bv": w["bv"],
        "qns": w["qns"], "kns": w["kns"],
    }
    in_maps = []
    for b in range(B):
        m = dict(shared)
        m["xin"] = np.ascontiguousarray(x[b].T)
        in_maps.append(m)
    return in_maps, T, E, NG


def kernel(**inputs):
    x = np.asarray(inputs["L0_attn_a_x"], np.float32)
    B, T, E = x.shape
    assert B == N_CORES, (B, N_CORES)
    in_maps, T, E, NG = make_in_maps(inputs)
    nc = build_program(T, E, NG)
    res = run_bass_kernel_spmd(nc, in_maps, core_ids=list(range(N_CORES)))
    outs = [np.ascontiguousarray(res.results[b]["out"].T) for b in range(B)]
    return np.stack(outs).astype(np.float32)


# version 13
# speedup vs baseline: 1.0525x; 1.0525x over previous
import numpy as np
import ml_dtypes

import concourse.bass as bass
import concourse.mybir as mybir
from concourse.bass_utils import run_bass_kernel_spmd

F32 = mybir.dt.float32
BF16 = mybir.dt.bfloat16
AF = mybir.ActivationFunctionType
ALU = mybir.AluOpType
AX = mybir.AxisListType
bfdt = ml_dtypes.bfloat16

N_CORES = 8
QW = 512
KW = 128


class Prog:

    NQ, NG_ = 4, 8
    ENGS = (["T", "S", "V"] + [f"Q{i}" for i in range(NQ)]
            + [f"G{i}" for i in range(NG_)])
    DMA = {e for e in ENGS if e[0] in "QG"}

    def __init__(self):
        self.ops = {"T": [], "S": [], "V": [], "Q": [], "G": []}
        self.seq = {e: 0 for e in self.ENGS}
        self.pend_r = {e: [] for e in self.ENGS}
        self.pend_w = {e: [] for e in self.ENGS}
        self.bufw = {}
        self.bufr = {}
        self.waited = {e: {e2: 0 for e2 in self.ENGS} for e in self.ENGS}
        self.rr = {"Q": 0, "G": 0}
        self.n_waits = 0

    def op(self, eng, fn, reads=(), writes=(), inc=True):
        stream = eng
        if eng == "Q":
            eng = f"Q{self.rr['Q'] % self.NQ}"
            self.rr["Q"] += 1
        elif eng == "G":
            eng = f"G{self.rr['G'] % self.NG_}"
            self.rr["G"] += 1
        waits = {}

        def need(dep_eng, dep_seq, buf):
            if dep_eng == eng == "T" and buf.startswith("P"):
                return
            waits[dep_eng] = max(waits.get(dep_eng, 0), dep_seq)

        if eng in self.DMA and self.seq[eng] > 0:
            waits[eng] = self.seq[eng]
        for b in reads:
            w = self.bufw.get(b)
            if w:
                need(w[0], w[1], b)
        for b in writes:
            w = self.bufw.get(b)
            if w:
                need(w[0], w[1], b)
            for re, rs in self.bufr.get(b, {}).items():
                need(re, rs, b)
        todo = []
        for de, ds in waits.items():
            if ds > self.waited[eng][de]:
                self.waited[eng][de] = ds
                todo.append((de, ds))
        self.n_waits += len(todo)
        self.pend_r[eng].extend(reads)
        self.pend_w[eng].extend(writes)
        if inc:
            self.seq[eng] += 1
            s = self.seq[eng]
            for b in self.pend_w[eng]:
                self.bufw[b] = (eng, s)
                self.bufr.pop(b, None)
            for b in self.pend_r[eng]:
                self.bufr.setdefault(b, {})[eng] = s
            self.pend_r[eng] = []
            self.pend_w[eng] = []
        mult = 16 if eng in self.DMA else 1
        myeng = eng
        dmaset = self.DMA

        def go(p, sems):
            for de, ds in todo:
                p.wait_ge(sems[de], ds * (16 if de in dmaset else 1))
            inst = fn(p)
            if inc and inst is not None:
                inst.then_inc(sems[myeng], mult)

        self.ops[stream].append(go)

    def final_wait(self, eng, dep_stream):
        subs = [e for e in self.ENGS if e.startswith(dep_stream) and e != dep_stream] \
            if dep_stream in ("Q", "G") else [dep_stream]
        vals = [(de, self.seq[de] * (16 if de in self.DMA else 1)) for de in subs]

        def go(p, sems):
            for de, v in vals:
                if v:
                    p.wait_ge(sems[de], v)

        self.ops[eng].append(go)


def rope_tables_fm(T, base):
    i = np.arange(0, 128, 2, dtype=np.float32)
    inv = (np.float32(1.0) / (np.float32(base) ** (i / np.float32(128)))).astype(np.float32)
    t = np.arange(T, dtype=np.float32)
    f = t[None, :] * inv[:, None]
    cos = np.cos(f).astype(np.float32)
    sin = np.sin(f).astype(np.float32)
    return (np.ascontiguousarray(np.concatenate([cos, cos], axis=0)),
            np.ascontiguousarray(np.concatenate([sin, sin], axis=0)))


def host_constants(T, E):
    c = {}
    c["cosA"], c["sinA"] = rope_tables_fm(T, 1000000.0)
    c["cosB"], c["sinB"] = rope_tables_fm(T, 10000.0)
    S = np.zeros((128, 128), np.float32)
    for d in range(64):
        S[d + 64, d] = -1.0
        S[d, d + 64] = 1.0
    c["Smat"] = S
    c["ident"] = np.eye(128, dtype=np.float32)
    c["identb"] = np.eye(128, dtype=np.float32).astype(bfdt)
    n_c = QW // KW
    kk = np.arange(KW)[:, None]
    qq = np.arange(QW)[None, :]
    masks = np.stack([(kk <= qq - KW * cc).astype(np.float32) for cc in range(n_c)])
    c["masks"] = np.ascontiguousarray(masks.transpose(1, 0, 2).reshape(KW, n_c * QW)).astype(bfdt)
    return c


def layout_weights(inp, T, E):
    NH = E // 128
    NEC = E // 128
    NG = np.asarray(inp["L0_attn_b_wk"]).shape[1] // 128
    w = {}

    def heads_img(A, nh, hw):
        return np.ascontiguousarray(
            A.reshape(NEC, 128, nh, hw).transpose(2, 1, 0, 3).reshape(nh, 128, NEC * hw))

    def fchunk_img(A):
        return np.ascontiguousarray(
            A.reshape(NH, 128, NEC, 128).transpose(2, 1, 0, 3).reshape(NEC, 128, NH * 128))

    wq_a = np.asarray(inp["L0_attn_a_wq"], np.float32)
    wk_a = np.asarray(inp["L0_attn_a_wk"], np.float32)
    wv_a = np.asarray(inp["L0_attn_a_wv"], np.float32)
    wo_a = np.asarray(inp["L0_attn_a_wo"], np.float32)
    w["wqa"] = heads_img(wq_a.T, NH, 128)
    w["wka"] = heads_img(wk_a.T, NH, 128)
    w["woa"] = fchunk_img(wo_a.T)
    w["wva"] = heads_img(wv_a.T, NH // 2, 256)

    wq_b = np.asarray(inp["L0_attn_b_wq"], np.float32)
    wk_b = np.asarray(inp["L0_attn_b_wk"], np.float32)
    wv_b = np.asarray(inp["L0_attn_b_wv"], np.float32)
    wo_b = np.asarray(inp["L0_attn_b_wo"], np.float32)
    w["wqb"] = heads_img(wq_b, NH, 128)
    w["wkb"] = heads_img(wk_b, NG, 128)
    nvp = max(1, NG // 2)
    vwb = wv_b.shape[1] // nvp
    w["wvb"] = heads_img(wv_b, nvp, vwb)
    w["wob"] = fchunk_img(wo_b)

    rw = np.asarray(inp["L0_ffn_router_w"], np.float32)
    dw = np.asarray(inp["L0_ffn_down_w"], np.float32)
    uw = np.asarray(inp["L0_ffn_up_w"], np.float32)
    D_all = dw.transpose(2, 0, 1).reshape(E, 128)
    C = np.concatenate([D_all, rw.T], axis=1)
    w["dr"] = np.ascontiguousarray(
        C.reshape(NEC, 128, 136).transpose(1, 0, 2).reshape(128, NEC * 136))
    w["u"] = np.ascontiguousarray(uw.transpose(0, 2, 1).reshape(128, E))

    w["bq"] = np.ascontiguousarray(np.asarray(inp["L0_attn_b_bq"], np.float32).reshape(NH, 128).T)
    w["bk"] = np.ascontiguousarray(np.asarray(inp["L0_attn_b_bk"], np.float32).reshape(NG, 128).T)
    w["bo"] = np.ascontiguousarray(np.asarray(inp["L0_attn_b_bo"], np.float32).reshape(NH, 128).T)
    w["bv"] = np.ascontiguousarray(np.asarray(inp["L0_attn_b_bv"], np.float32).reshape(1, NG * 128))
    w["qns"] = np.ascontiguousarray(np.asarray(inp["L0_attn_b_q_norm_scale"], np.float32).reshape(128, 1))
    w["kns"] = np.ascontiguousarray(np.asarray(inp["L0_attn_b_k_norm_scale"], np.float32).reshape(128, 1))
    return w


def build_program(T, E, NG):
    NH = E // 128
    NEC = E // 128
    NTC = T // 128
    NQC = T // QW
    n_c = QW // KW
    rep = NH // NG
    nvp = max(1, NG // 2)
    vwb = (NG * 128) // nvp
    scale_a = float(np.float32(1.0) / np.sqrt(np.float32(128)))
    scale_b = float((np.float64(128.0) ** -0.5) ** 2)

    nc = bass.Bass("TRN2", target_bir_lowering=False)

    xin = nc.dram_tensor("xin", [E, T], F32, kind="ExternalInput").ap()
    out = nc.dram_tensor("out", [E, T], F32, kind="ExternalOutput").ap()
    d_cosA = nc.dram_tensor("cosA", [128, T], F32, kind="ExternalInput").ap()
    d_sinA = nc.dram_tensor("sinA", [128, T], F32, kind="ExternalInput").ap()
    d_cosB = nc.dram_tensor("cosB", [128, T], F32, kind="ExternalInput").ap()
    d_sinB = nc.dram_tensor("sinB", [128, T], F32, kind="ExternalInput").ap()
    d_S = nc.dram_tensor("Smat", [128, 128], F32, kind="ExternalInput").ap()
    d_ident = nc.dram_tensor("ident", [128, 128], F32, kind="ExternalInput").ap()
    d_identb = nc.dram_tensor("identb", [128, 128], BF16, kind="ExternalInput").ap()
    d_masks = nc.dram_tensor("masks", [KW, n_c * QW], BF16, kind="ExternalInput").ap()
    d_wqa = nc.dram_tensor("wqa", [NH, 128, NEC * 128], F32, kind="ExternalInput").ap()
    d_wka = nc.dram_tensor("wka", [NH, 128, NEC * 128], F32, kind="ExternalInput").ap()
    d_wva = nc.dram_tensor("wva", [NH // 2, 128, NEC * 256], F32, kind="ExternalInput").ap()
    d_woa = nc.dram_tensor("woa", [NEC, 128, NH * 128], F32, kind="ExternalInput").ap()
    d_wqb = nc.dram_tensor("wqb", [NH, 128, NEC * 128], F32, kind="ExternalInput").ap()
    d_wkb = nc.dram_tensor("wkb", [NG, 128, NEC * 128], F32, kind="ExternalInput").ap()
    d_wvb = nc.dram_tensor("wvb", [nvp, 128, NEC * vwb], F32, kind="ExternalInput").ap()
    d_wob = nc.dram_tensor("wob", [NEC, 128, NH * 128], F32, kind="ExternalInput").ap()
    d_dr = nc.dram_tensor("dr", [128, NEC * 136], F32, kind="ExternalInput").ap()
    d_u = nc.dram_tensor("u", [128, E], F32, kind="ExternalInput").ap()
    d_bq = nc.dram_tensor("bq", [128, NH], F32, kind="ExternalInput").ap()
    d_bk = nc.dram_tensor("bk", [128, NG], F32, kind="ExternalInput").ap()
    d_bo = nc.dram_tensor("bo", [128, NH], F32, kind="ExternalInput").ap()
    d_bv = nc.dram_tensor("bv", [1, NG * 128], F32, kind="ExternalInput").ap()
    d_qns = nc.dram_tensor("qns", [128, 1], F32, kind="ExternalInput").ap()
    d_kns = nc.dram_tensor("kns", [128, 1], F32, kind="ExternalInput").ap()
    osp = nc.dram_tensor("osp", [NH * 128, T], BF16, kind="Internal").ap()

    A = nc.alloc_sbuf_tensor
    x_t = [A(f"x{i}", [128, T], BF16) for i in range(NEC)]
    y_t = [A(f"y{i}", [128, T], BF16) for i in range(NEC)]
    WSW = max(NEC * 256, NEC * 136 + E, NEC * vwb, NH * 128)
    ws = [A("ws0", [128, WSW], BF16), A("ws1", [128, WSW], BF16)]
    cos_s = A("cos_s", [128, T], BF16)
    sin_s = A("sin_s", [128, T], BF16)
    q_sb = A("q_sb", [128, T], BF16)
    k_sb = A("k_sb", [128, T], BF16)
    tmp1 = A("tmp1", [128, T], BF16)
    v_tm = [A(f"vtm{i}", [128, 256], BF16) for i in range(NTC)]
    expT = [A("expT0", [128, 2 * QW], BF16), A("expT1", [128, 2 * QW], BF16)]
    recipZ = A("recipZ", [128, QW], BF16)
    outn = A("outn", [128, QW], BF16)
    mB = A("mB", [128, T], BF16)
    rstdB = A("rstdB", [128, T], BF16)
    m2s = A("m2s", [128, QW], BF16)
    vars_ = A("vars", [128, QW], F32)
    sqs = [A("sqs0", [128, QW], BF16), A("sqs1", [128, QW], BF16)]
    stage = [A("stage0", [128, QW], F32), A("stage1", [128, QW], F32)]
    agT = A("agT", [128, T], BF16)
    a_sb = [A("a_sb0", [128, 128], BF16), A("a_sb1", [128, 128], BF16)]
    ag_sb = [A("ag_sb0", [128, 128], F32), A("ag_sb1", [128, 128], F32)]
    w8 = A("w8", [128, 8], F32)
    g8 = A("g8", [128, 16], F32)
    smalls = A("smalls", [128, 16], F32)
    ident_s = A("ident_s", [128, 128], F32)
    identb_s = A("identb_s", [128, 128], BF16)
    allones = A("allones", [128, 128], BF16)
    Smat_s = A("Smat_s", [128, 128], BF16)
    masks_s = A("masks_s", [128, n_c * QW], BF16)
    bq_s = A("bq_s", [128, NH], F32)
    bk_s = A("bk_s", [128, NG], F32)
    bo_s = A("bo_s", [128, NH], F32)
    bv_pad = A("bv_pad", [128, vwb], BF16)
    qns_s = A("qns_s", [128, 1], F32)
    kns_s = A("kns_s", [128, 1], F32)
    ebias = A("ebias", [128, NQC], F32)
    eps5 = A("eps5", [128, 1], F32)
    eps6 = A("eps6", [128, 1], F32)

    PT = [nc.alloc_psum_tensor(f"PT{i}", [128, 1024], F32) for i in range(4)]

    def bank(b, n=512):
        return PT[b // 2][:, (b % 2) * 512:(b % 2) * 512 + n]

    PS = [bank(i) for i in range(8)]

    P = Prog()

    def dma(eng, dst_ap, src_ap, reads, writes):
        P.op(eng, lambda p, d=dst_ap, s=src_ap: p.dma_start(out=d, in_=s),
             reads=reads, writes=writes)

    def mmN(bi, n, lhsT, rhs, reads, start, stop, inc=True):
        P.op("T", lambda p, l=lhsT, r=rhs, st=start, sp=stop, b=bi, nn=n:
             p.matmul(bank(b, nn), l, r, start=st, stop=sp),
             reads=reads, writes=[f"P{bi}"], inc=inc)

    def mm2(bi, lhsT, rhs, reads, start, stop, inc=True):
        mmN(bi, QW, lhsT, rhs, reads, start, stop, inc)

    P.op("V", lambda p: p.memset(allones[:], 1.0), writes=["allones"])
    P.op("V", lambda p: p.memset(eps5[:], 1e-5), writes=["eps5"])
    P.op("V", lambda p: p.memset(eps6[:], 1e-6), writes=["eps6"])
    P.op("V", lambda p: p.memset(bv_pad[:], 0.0), writes=["bv_pad"])
    dma("Q", ident_s[:], d_ident[:, :], [], ["ident_s"])
    dma("Q", identb_s[:], d_identb[:, :], [], ["identb_s"])
    dma("Q", masks_s[:], d_masks[:, :], [], ["masks_s"])
    dma("G", Smat_s[:], d_S[:, :], [], ["Smat_s"])
    dma("G", cos_s[:], d_cosA[:, :], [], ["cos_s"])
    dma("G", sin_s[:], d_sinA[:, :], [], ["sin_s"])
    dma("G", bq_s[:], d_bq[:, :], [], ["bq_s"])
    dma("G", bk_s[:], d_bk[:, :], [], ["bk_s"])
    dma("G", bo_s[:], d_bo[:, :], [], ["bo_s"])
    dma("G", qns_s[:], d_qns[:, :], [], ["qns_s"])
    dma("G", kns_s[:], d_kns[:, :], [], ["kns_s"])
    for i in range(NEC):
        dma("G", x_t[i][:], xin[i * 128:(i + 1) * 128, :], [], [f"x{i}"])

    phases = []

    def add_phase(dmas):
        phases.append(dmas)
        return len(phases) - 1

    emitted = set()

    def emit_phase(idx):
        if idx in emitted or idx >= len(phases):
            return
        emitted.add(idx)
        slot = idx % 2
        name = f"ws{slot}"
        for (off, width, src) in phases[idx]:
            dma("G", ws[slot][:, off:off + width], src, [name], [name])

    def slot_of(idx):
        return idx % 2, f"ws{idx % 2}"

    def layer_norm():
        steps = [(ec, tc) for ec in range(NEC) for tc in range(NQC)]

        def square(i):
            ec, tc = steps[i]
            sq, sqn = sqs[i % 2], f"sqs{i % 2}"
            sl = slice(tc * QW, (tc + 1) * QW)
            P.op("V", lambda p, e=ec, s=sl, q=sq: p.tensor_tensor(
                q[:], x_t[e][:, s], x_t[e][:, s], ALU.mult),
                reads=[f"x{ec}"], writes=[sqn])

        square(0)
        for i, (ec, tc) in enumerate(steps):
            if i + 1 < len(steps):
                square(i + 1)
            sqn = f"sqs{i % 2}"
            sl = slice(tc * QW, (tc + 1) * QW)
            mm2(tc, allones[:], x_t[ec][:, sl], ["allones", f"x{ec}"],
                start=(ec == 0), stop=(ec == NEC - 1))
            mm2(4 + tc, allones[:], sqs[i % 2][:], ["allones", sqn],
                start=(ec == 0), stop=(ec == NEC - 1))
        for tc in range(NQC):
            sl = slice(tc * QW, (tc + 1) * QW)
            P.op("S", lambda p, t=tc, s=sl: p.activation(
                mB[:, s], PS[t][:, 0:QW], AF.Identity, scale=1.0 / E),
                reads=[f"P{tc}"], writes=["mB"])
            P.op("S", lambda p, s=sl: p.activation(m2s[:], mB[:, s], AF.Square),
                 reads=["mB"], writes=["m2s"])
            P.op("V", lambda p, t=tc: p.scalar_tensor_tensor(
                vars_[:], PS[4 + t][:, 0:QW], 1.0 / E, m2s[:], ALU.mult, ALU.subtract),
                reads=[f"P{4 + tc}", "m2s"], writes=["vars"])
            P.op("S", lambda p: p.activation(vars_[:], vars_[:], AF.Ln, bias=eps5[:]),
                 reads=["vars", "eps5"], writes=["vars"])
            P.op("S", lambda p, s=sl: p.activation(rstdB[:, s], vars_[:], AF.Exp, scale=-0.5),
                 reads=["vars"], writes=["rstdB"])
            P.op("V", lambda p, s=sl: p.tensor_tensor(mB[:, s], mB[:, s], rstdB[:, s], ALU.mult),
                 reads=["rstdB"], writes=["mB"])
        for ec in range(NEC):
            P.op("V", lambda p, e=ec: p.tensor_tensor(y_t[e][:], x_t[e][:], rstdB[:], ALU.mult),
                 reads=[f"x{ec}", "rstdB"], writes=[f"y{ec}"])
            P.op("V", lambda p, e=ec: p.tensor_tensor(y_t[e][:], y_t[e][:], mB[:], ALU.subtract),
                 reads=["mB"], writes=[f"y{ec}"])

    def proj(dst, dstname, sname, slot_ap, woff, bias_ap, bias_name, after_tc=None):
        for tc in range(NQC):
            b = 6 + (tc % 2)
            for ec in range(NEC):
                lhs = slot_ap[:, woff + ec * 128: woff + (ec + 1) * 128]
                mm2(b, lhs, y_t[ec][:, tc * QW:(tc + 1) * QW],
                    [sname, f"y{ec}"], start=(ec == 0), stop=(ec == NEC - 1),
                    inc=(ec == NEC - 1))
            if bias_ap is None:
                P.op("S", lambda p, bb=b, t=tc: p.activation(
                    dst[:, t * QW:(t + 1) * QW], PS[bb][:, 0:QW], AF.Identity),
                    reads=[f"P{b}"], writes=[f"{dstname}.{tc}"])
            else:
                P.op("S", lambda p, bb=b, t=tc, ba=bias_ap: p.activation(
                    dst[:, t * QW:(t + 1) * QW], PS[bb][:, 0:QW], AF.Identity, bias=ba),
                    reads=[f"P{b}", bias_name], writes=[f"{dstname}.{tc}"])
            if after_tc is not None and tc > 0:
                after_tc(tc - 1)
        if after_tc is not None:
            after_tc(NQC - 1)

    def rope_tc(buf, bufname, tc, bank0=4):
        b = bank0 + (tc % 2)
        sl = slice(tc * QW, (tc + 1) * QW)
        nm = f"{bufname}.{tc}"
        P.op("V", lambda p, s=sl: p.tensor_tensor(
            tmp1[:, s], buf[:, s], sin_s[:, s], ALU.mult),
            reads=[nm, "sin_s"], writes=[f"tmp1.{tc}"])
        P.op("V", lambda p, s=sl: p.tensor_tensor(
            buf[:, s], buf[:, s], cos_s[:, s], ALU.mult),
            reads=["cos_s", nm], writes=[nm])
        mm2(b, identb_s[:], buf[:, sl], ["identb_s", nm], start=True, stop=False)
        mm2(b, Smat_s[:], tmp1[:, sl], ["Smat_s", f"tmp1.{tc}"], start=False, stop=True)
        P.op("S", lambda p, bb=b, s=sl: p.activation(
            buf[:, s], PS[bb][:, 0:QW], AF.Identity),
            reads=[f"P{b}"], writes=[nm])

    def rms_tc(buf, bufname, tc, scale_ap, scale_name):
        sq, sqn = sqs[tc % 2], f"sqs{tc % 2}"
        b = 4 + (tc % 2)
        sl = slice(tc * QW, (tc + 1) * QW)
        nm = f"{bufname}.{tc}"
        P.op("V", lambda p, s=sl, q=sq: p.tensor_tensor(
            q[:], buf[:, s], buf[:, s], ALU.mult),
            reads=[nm], writes=[sqn])
        mm2(b, allones[:], sq[:], ["allones", sqn], start=True, stop=True)
        P.op("S", lambda p, bb=b: p.activation(vars_[:], PS[bb][:, 0:QW], AF.Ln,
                                               scale=1.0 / 128.0, bias=eps6[:]),
             reads=[f"P{b}", "eps6"], writes=["vars"])
        P.op("S", lambda p, s=sl: p.activation(
            rstdB[:, s], vars_[:], AF.Exp, scale=-0.5),
            reads=["vars"], writes=["rstdB"])
        P.op("V", lambda p, s=sl: p.tensor_tensor(buf[:, s], buf[:, s], rstdB[:, s], ALU.mult),
             reads=["rstdB", nm], writes=[nm])
        P.op("V", lambda p, sa=scale_ap, s=sl: p.tensor_scalar(
            buf[:, s], buf[:, s], sa, None, ALU.mult),
            reads=[scale_name, nm], writes=[nm])

    def norm_tc(buf, bufname, tc, scol, bank=4):
        b = bank + (tc % 2)
        sq, sqn = sqs[tc % 2], f"sqs{tc % 2}"
        sl = slice(tc * QW, (tc + 1) * QW)
        P.op("V", lambda p, s=sl, q=sq: p.tensor_tensor(
            q[:], buf[:, s], buf[:, s], ALU.mult),
            reads=[f"{bufname}.{tc}"], writes=[sqn])
        mm2(b, allones[:], sq[:], ["allones", sqn], start=True, stop=True)
        P.op("V", lambda p, bb=b, c=scol: p.reduce_max(
            smalls[:, c:c + 1], PS[bb][:, 0:QW], axis=AX.X),
            reads=[f"P{b}"], writes=["smalls"])

    def exp_bias(e_scale, qc):
        P.op("V", lambda p, c=qc: p.tensor_tensor(smalls[:, 2:3], smalls[:, 12 + c:13 + c],
                                                  smalls[:, 1:2], ALU.mult),
             reads=["smalls"], writes=["smalls"])
        P.op("S", lambda p: p.activation(smalls[:, 3:4], smalls[:, 2:3], AF.Ln),
             reads=["smalls"], writes=["smalls"])
        P.op("S", lambda p: p.activation(smalls[:, 3:4], smalls[:, 3:4], AF.Exp, scale=0.5),
             reads=["smalls"], writes=["smalls"])
        P.op("V", lambda p, es=e_scale: p.tensor_scalar(
            smalls[:, 4:5], smalls[:, 3:4], -es, 40.0, ALU.mult, ALU.add),
            reads=["smalls"], writes=["smalls"])
        P.op("V", lambda p, c=qc: p.tensor_scalar(ebias[:, c:c + 1], smalls[:, 4:5], 0.0, None, ALU.min),
             reads=["smalls"], writes=[f"ebias.{qc}"])

    def attn_head_qc(h, vslice, e_scale, qc):
        if True:
            qsl = slice(qc * QW, (qc + 1) * QW)
            nkc = (qc + 1) * n_c
            npair = nkc // 2

            def sc_exp(j):
                par = j % 2
                for half in range(2):
                    kc = 2 * j + half
                    mm2(2 * par + half, k_sb[:, kc * KW:(kc + 1) * KW], q_sb[:, qsl],
                        [f"k_sb.{kc // n_c}", f"q_sb.{qc}"], start=True, stop=True)
                P.op("S", lambda p, pa=par: p.activation(
                    expT[pa][:], PT[pa][:, 0:2 * QW], AF.Exp, bias=ebias[:, qc:qc + 1], scale=e_scale),
                    reads=[f"P{2 * par}", f"P{2 * par + 1}", f"ebias.{qc}"], writes=[f"expT{par}"])
                for half in range(2):
                    kc = 2 * j + half
                    cc = kc - qc * n_c
                    if cc >= 0:
                        P.op("V", lambda p, pa=par, c=cc, hf=half: p.tensor_tensor(
                            expT[pa][:, hf * QW:(hf + 1) * QW],
                            expT[pa][:, hf * QW:(hf + 1) * QW],
                            masks_s[:, c * QW:(c + 1) * QW], ALU.mult),
                            reads=["masks_s"], writes=[f"expT{par}"])

            def zav(j):
                par = j % 2
                for half in range(2):
                    kc = 2 * j + half
                    e_ap = expT[par][:, half * QW:(half + 1) * QW]
                    mm2(4, allones[:], e_ap, ["allones", f"expT{par}"],
                        start=(kc == 0), stop=(kc == nkc - 1))
                    mm2(5, v_tm[kc][:, vslice], e_ap,
                        [f"vtm{kc}", f"expT{par}"],
                        start=(kc == 0), stop=(kc == nkc - 1))

            sc_exp(0)
            for j in range(1, npair):
                sc_exp(j)
                zav(j - 1)
            zav(npair - 1)
            P.op("S", lambda p: p.activation(vars_[:], PS[4][:, 0:QW], AF.Ln),
                 reads=["P4"], writes=["vars"])
            P.op("S", lambda p: p.activation(recipZ[:], vars_[:], AF.Exp, scale=-1.0),
                 reads=["vars"], writes=["recipZ"])
            P.op("V", lambda p: p.tensor_tensor(outn[:], PS[5][:, 0:QW], recipZ[:], ALU.mult),
                 reads=["P5", "recipZ"], writes=["outn"])
            dma("Q", osp[h * 128:(h + 1) * 128, qc * QW:(qc + 1) * QW], outn[:],
                ["outn"], [f"osp{h}"])

    def v_proj(sname, slot_ap, vwidth, bvslice=None):
        for tg in range((NTC + 3) // 4):
            tchs = list(range(tg * 4, min((tg + 1) * 4, NTC)))
            for ec in range(NEC):
                for i, t in enumerate(tchs):
                    bank = 4 + i
                    last = (ec == NEC - 1) and bvslice is None
                    mmN(bank, vwidth, y_t[ec][:, t * 128:(t + 1) * 128],
                        slot_ap[:, ec * vwidth:(ec + 1) * vwidth],
                        [f"y{ec}", sname], start=(ec == 0), stop=last,
                        inc=(ec == NEC - 1))
            if bvslice is not None:
                for i, t in enumerate(tchs):
                    mmN(4 + i, vwidth, allones[:], bv_pad[:, 0:vwidth],
                        ["allones", "bv_pad"], start=False, stop=True)
            for i, t in enumerate(tchs):
                bank = 4 + i
                P.op("V", lambda p, b=bank, tt=t, vw=vwidth: p.tensor_copy(
                    v_tm[tt][:, 0:vw], PS[b][:, 0:vw]),
                    reads=[f"P{bank}"], writes=[f"vtm{t}"])

    def wo_phase(phase_base, with_bo):
        for h in range(NH):
            dma("Q", y_t[h][:], osp[h * 128:(h + 1) * 128, :], [f"osp{h}"], [f"y{h}"])
        emit_phase(phase_base)
        for fc in range(NEC):
            if fc + 1 < NEC:
                emit_phase(phase_base + fc + 1)
            slot, sname = slot_of(phase_base + fc)
            for tc in range(NQC):
                b = tc % 2
                for h in range(NH):
                    mm2(b, ws[slot][:, h * 128:(h + 1) * 128],
                        y_t[h][:, tc * QW:(tc + 1) * QW],
                        [sname, f"y{h}"], start=(h == 0), stop=(h == NH - 1),
                        inc=(h == NH - 1))
                if with_bo:
                    P.op("V", lambda p, bb=b, f=fc, t=tc: p.scalar_tensor_tensor(
                        x_t[f][:, t * QW:(t + 1) * QW], PS[bb][:, 0:QW],
                        bo_s[:, f:f + 1], x_t[f][:, t * QW:(t + 1) * QW],
                        ALU.add, ALU.add),
                        reads=[f"P{b}", "bo_s", f"x{fc}"], writes=[f"x{fc}"])
                else:
                    P.op("V", lambda p, bb=b, f=fc, t=tc: p.tensor_tensor(
                        x_t[f][:, t * QW:(t + 1) * QW],
                        x_t[f][:, t * QW:(t + 1) * QW], PS[bb][:, 0:QW], ALU.add),
                        reads=[f"P{b}", f"x{fc}"], writes=[f"x{fc}"])

    def ffn(phase_idx, last=False):
        layer_norm()
        emit_phase(phase_idx)
        slot, sname = slot_of(phase_idx)
        droff = 0
        uoff = NEC * 136
        def gate_chain(t):
            b = t % 2
            par = t % 2
            P.op("V", lambda p, bb=b: p.reduce_max(
                smalls[:, 0:1], PS[bb][:, 128:136], axis=AX.X),
                reads=[f"P{b}"], writes=["smalls"])
            P.op("V", lambda p: p.tensor_scalar(
                smalls[:, 1:2], smalls[:, 0:1], -1.0, None, ALU.mult),
                reads=["smalls"], writes=["smalls"])
            P.op("S", lambda p, bb=b: p.activation(
                w8[:], PS[bb][:, 128:136], AF.Exp, bias=smalls[:, 1:2]),
                reads=[f"P{b}", "smalls"], writes=["w8"])
            P.op("V", lambda p: p.reduce_sum(smalls[:, 2:3], w8[:], axis=AX.X),
                 reads=["w8"], writes=["smalls"])
            P.op("V", lambda p: p.reciprocal(smalls[:, 3:4], smalls[:, 2:3]),
                 reads=["smalls"], writes=["smalls"])
            P.op("V", lambda p: p.tensor_scalar(
                g8[:, 0:8], w8[:], 1.0, -1e9, ALU.is_ge, ALU.mult),
                reads=["w8"], writes=["g8"])
            P.op("V", lambda p: p.tensor_tensor(g8[:, 8:16], w8[:], g8[:, 0:8], ALU.add),
                 reads=["w8"], writes=["g8"])
            P.op("V", lambda p: p.reduce_max(smalls[:, 4:5], g8[:, 8:16], axis=AX.X),
                 reads=["g8"], writes=["smalls"])
            P.op("V", lambda p: p.tensor_scalar(
                g8[:, 0:8], w8[:], smalls[:, 4:5], None, ALU.is_ge),
                reads=["w8", "smalls"], writes=["g8"])
            P.op("V", lambda p: p.tensor_tensor(w8[:], w8[:], g8[:, 0:8], ALU.mult),
                 reads=["g8"], writes=["w8"])
            P.op("V", lambda p: p.tensor_scalar(
                w8[:], w8[:], smalls[:, 3:4], None, ALU.mult),
                reads=["smalls"], writes=["w8"])
            P.op("S", lambda p, bb=b: p.activation(
                vars_[:, 0:128], PS[bb][:, 0:128], AF.Exp, scale=-1.0),
                reads=[f"P{b}"], writes=["vars"])
            P.op("V", lambda p: p.tensor_scalar(vars_[:, 0:128], vars_[:, 0:128], 1.0, None, ALU.add),
                 reads=["vars"], writes=["vars"])
            P.op("V", lambda p: p.reciprocal(vars_[:, 128:256], vars_[:, 0:128]),
                 reads=["vars"], writes=["vars"])
            P.op("V", lambda p, bb=b, pa=par: p.tensor_tensor(
                a_sb[pa][:], PS[bb][:, 0:128], vars_[:, 128:256], ALU.mult),
                reads=[f"P{b}", "vars"], writes=[f"a_sb{par}"])
            for ex in range(8):
                P.op("V", lambda p, e=ex, pa=par: p.tensor_scalar(
                    ag_sb[pa][:, e * 16:(e + 1) * 16], a_sb[pa][:, e * 16:(e + 1) * 16],
                    w8[:, e:e + 1], None, ALU.mult),
                    reads=[f"a_sb{par}", "w8"], writes=[f"ag_sb{par}"], inc=(ex == 7))

        def transp(t):
            par = t % 2
            tb = 2 + (t % 2)
            P.op("T", lambda p, bb=tb, pa=par: p.transpose(
                PS[bb][:, 0:128], ag_sb[pa][:], ident_s[:]),
                reads=[f"ag_sb{par}", "ident_s"], writes=[f"P{tb}"])
            P.op("S", lambda p, bb=tb, tt=t: p.activation(
                agT[:, tt * 128:(tt + 1) * 128], PS[bb][:, 0:128], AF.Identity),
                reads=[f"P{tb}"], writes=["agT"])

        for t in range(NTC):
            b = t % 2
            for ec in range(NEC):
                mmN(b, 136, y_t[ec][:, t * 128:(t + 1) * 128],
                    ws[slot][:, droff + ec * 136: droff + (ec + 1) * 136],
                    [f"y{ec}", sname], start=(ec == 0), stop=(ec == NEC - 1),
                    inc=(ec == NEC - 1))
            if t > 0:
                transp(t - 1)
            gate_chain(t)
        transp(NTC - 1)
        for fc in range(NEC):
            for tc in range(NQC):
                b = 4 + ((fc * NQC + tc) % 2)
                mm2(b, ws[slot][:, uoff + fc * 128: uoff + (fc + 1) * 128],
                    agT[:, tc * QW:(tc + 1) * QW], [sname, "agT"],
                    start=True, stop=True)
                if last:
                    par = (fc * NQC + tc) % 2
                    P.op("V", lambda p, bb=b, f=fc, t=tc, pa=par: p.tensor_tensor(
                        stage[pa][:], x_t[f][:, t * QW:(t + 1) * QW],
                        PS[bb][:, 0:QW], ALU.add),
                        reads=[f"P{b}", f"x{fc}"], writes=[f"stage{par}"])
                    dma("Q", out[fc * 128:(fc + 1) * 128, tc * QW:(tc + 1) * QW],
                        stage[par][:], [f"stage{par}"], ["outhbm"])
                else:
                    P.op("V", lambda p, bb=b, f=fc, t=tc: p.tensor_tensor(
                        x_t[f][:, t * QW:(t + 1) * QW],
                        x_t[f][:, t * QW:(t + 1) * QW], PS[bb][:, 0:QW], ALU.add),
                        reads=[f"P{b}", f"x{fc}"], writes=[f"x{fc}"])

    gpv = NG // nvp
    ph_wva, ph_wqk = {}, {}
    for pair in range(NH // 2):
        ph_wva[pair] = add_phase([(0, NEC * 256, d_wva[pair, :, :])])
        for hh in range(2):
            h = pair * 2 + hh
            ph_wqk[h] = add_phase([(0, NEC * 128, d_wqa[h, :, :]),
                                   (NEC * 128, NEC * 128, d_wka[h, :, :])])
    ph_woa = [add_phase([(0, NH * 128, d_woa[fc, :, :])]) for fc in range(NEC)]
    PH_FFN1 = add_phase([(0, NEC * 136, d_dr[:, :]), (NEC * 136, E, d_u[:, :])])
    ph_wvb, ph_wkb, ph_wqb = {}, {}, {}
    for vp in range(nvp):
        ph_wvb[vp] = add_phase([(0, NEC * vwb, d_wvb[vp, :, :])])
        for gg in range(gpv):
            g = vp * gpv + gg
            ph_wkb[g] = add_phase([(0, NEC * 128, d_wkb[g, :, :])])
            for r in range(rep):
                h = g * rep + r
                ph_wqb[h] = add_phase([(0, NEC * 128, d_wqb[h, :, :])])
    ph_wob = [add_phase([(0, NH * 128, d_wob[fc, :, :])]) for fc in range(NEC)]
    PH_FFN2 = add_phase([(0, NEC * 136, d_dr[:, :]), (NEC * 136, E, d_u[:, :])])

    layer_norm()
    emit_phase(0)
    for pair in range(NH // 2):
        emit_phase(ph_wva[pair])
        emit_phase(ph_wva[pair] + 1)
        vslot, vsname = slot_of(ph_wva[pair])
        v_proj(vsname, ws[vslot], 256)
        for hh in range(2):
            h = pair * 2 + hh
            phq = ph_wqk[h]
            emit_phase(phq)
            emit_phase(phq + 1)
            qslot, qsname = slot_of(phq)
            proj(q_sb, "q_sb", qsname, ws[qslot], 0, None, None)

            def chains_a(tc):
                rope_tc(k_sb, "k_sb", tc)
                norm_tc(k_sb, "k_sb", tc, 8 + tc)
                rope_tc(q_sb, "q_sb", tc)
                norm_tc(q_sb, "q_sb", tc, 12 + tc)
            proj(k_sb, "k_sb", qsname, ws[qslot], NEC * 128, None, None,
                 after_tc=chains_a)
            P.op("V", lambda p: p.reduce_max(
                smalls[:, 1:2], smalls[:, 8:8 + NQC], axis=AX.X),
                reads=["smalls"], writes=["smalls"])
            exp_bias(scale_a, 0)
            for qc in range(NQC):
                if qc + 1 < NQC:
                    exp_bias(scale_a, qc + 1)
                attn_head_qc(h, slice(hh * 128, (hh + 1) * 128), scale_a, qc)
    wo_phase(ph_woa[0], with_bo=False)

    ffn(PH_FFN1)

    layer_norm()
    dma("G", cos_s[:], d_cosB[:, :], ["cos_s"], ["cos_s"])
    dma("G", sin_s[:], d_sinB[:, :], ["sin_s"], ["sin_s"])
    emit_phase(ph_wvb[0])
    for vp in range(nvp):
        emit_phase(ph_wvb[vp])
        emit_phase(ph_wvb[vp] + 1)
        vslot, vsname = slot_of(ph_wvb[vp])
        dma("G", bv_pad[0:1, :], d_bv[:, vp * vwb:(vp + 1) * vwb], ["bv_pad"], ["bv_pad"])
        v_proj(vsname, ws[vslot], vwb, bvslice=slice(vp * vwb, (vp + 1) * vwb))
        for gg in range(gpv):
            g = vp * gpv + gg
            phk = ph_wkb[g]
            emit_phase(phk)
            emit_phase(phk + 1)
            kslot, ksname = slot_of(phk)
            def kchain_b(tc):
                rms_tc(k_sb, "k_sb", tc, kns_s[:], "kns_s")
                rope_tc(k_sb, "k_sb", tc)
                norm_tc(k_sb, "k_sb", tc, 8 + tc)
            proj(k_sb, "k_sb", ksname, ws[kslot], 0, bk_s[:, g:g + 1], "bk_s",
                 after_tc=kchain_b)
            P.op("V", lambda p: p.reduce_max(
                smalls[:, 1:2], smalls[:, 8:8 + NQC], axis=AX.X),
                reads=["smalls"], writes=["smalls"])
            for r in range(rep):
                h = g * rep + r
                phq = ph_wqb[h]
                emit_phase(phq)
                emit_phase(phq + 1)
                qslot, qsname = slot_of(phq)
                def qchain_b(tc):
                    rms_tc(q_sb, "q_sb", tc, qns_s[:], "qns_s")
                    rope_tc(q_sb, "q_sb", tc)
                    norm_tc(q_sb, "q_sb", tc, 12 + tc)
                proj(q_sb, "q_sb", qsname, ws[qslot], 0, bq_s[:, h:h + 1], "bq_s",
                     after_tc=qchain_b)
                exp_bias(scale_b, 0)
                for qc in range(NQC):
                    if qc + 1 < NQC:
                        exp_bias(scale_b, qc + 1)
                    attn_head_qc(h, slice(gg * 128, (gg + 1) * 128), scale_b, qc)
    wo_phase(ph_wob[0], with_bo=True)

    ffn(PH_FFN2, last=True)

    P.final_wait("Q", "Q")
    P.final_wait("G", "G")
    assert len(emitted) == len(phases), (len(emitted), len(phases))

    import contextlib
    with contextlib.ExitStack() as stack:
        sems = {e: stack.enter_context(nc.semaphore(f"sem{e}")) for e in Prog.ENGS}
        block = stack.enter_context(nc.Block())

        @block.tensor
        def _(t):
            for f in P.ops["T"]:
                f(t, sems)

        @block.scalar
        def _(s):
            for f in P.ops["S"]:
                f(s, sems)

        @block.vector
        def _(v):
            for f in P.ops["V"]:
                f(v, sems)

        @block.sync
        def _(q):
            for f in P.ops["Q"]:
                f(q, sems)

        @block.gpsimd
        def _(g):
            for f in P.ops["G"]:
                f(g, sems)

    return nc


def make_in_maps(inputs):
    x = np.asarray(inputs["L0_attn_a_x"], np.float32)
    B, T, E = x.shape
    NG = np.asarray(inputs["L0_attn_b_wk"]).shape[1] // 128
    consts = host_constants(T, E)
    w = layout_weights(inputs, T, E)
    shared = {
        "cosA": consts["cosA"], "sinA": consts["sinA"],
        "cosB": consts["cosB"], "sinB": consts["sinB"],
        "Smat": consts["Smat"], "ident": consts["ident"], "identb": consts["identb"], "masks": consts["masks"],
        "wqa": w["wqa"], "wka": w["wka"], "wva": w["wva"], "woa": w["woa"],
        "wqb": w["wqb"], "wkb": w["wkb"], "wvb": w["wvb"], "wob": w["wob"],
        "dr": w["dr"], "u": w["u"],
        "bq": w["bq"], "bk": w["bk"], "bo": w["bo"], "bv": w["bv"],
        "qns": w["qns"], "kns": w["kns"],
    }
    in_maps = []
    for b in range(B):
        m = dict(shared)
        m["xin"] = np.ascontiguousarray(x[b].T)
        in_maps.append(m)
    return in_maps, T, E, NG


def kernel(**inputs):
    x = np.asarray(inputs["L0_attn_a_x"], np.float32)
    B, T, E = x.shape
    assert B == N_CORES, (B, N_CORES)
    in_maps, T, E, NG = make_in_maps(inputs)
    nc = build_program(T, E, NG)
    res = run_bass_kernel_spmd(nc, in_maps, core_ids=list(range(N_CORES)))
    outs = [np.ascontiguousarray(res.results[b]["out"].T) for b in range(B)]
    return np.stack(outs).astype(np.float32)


# version 14
# speedup vs baseline: 1.1321x; 1.0757x over previous
import numpy as np
import ml_dtypes

import concourse.bass as bass
import concourse.mybir as mybir
from concourse.bass_utils import run_bass_kernel_spmd

F32 = mybir.dt.float32
BF16 = mybir.dt.bfloat16
AF = mybir.ActivationFunctionType
ALU = mybir.AluOpType
AX = mybir.AxisListType
bfdt = ml_dtypes.bfloat16

N_CORES = 8
QW = 512
KW = 128


class Prog:

    NQ, NG_ = 4, 8
    ENGS = (["T", "S", "V"] + [f"Q{i}" for i in range(NQ)]
            + [f"G{i}" for i in range(NG_)])
    DMA = {e for e in ENGS if e[0] in "QG"}

    def __init__(self):
        self.ops = {"T": [], "S": [], "V": [], "Q": [], "G": []}
        self.seq = {e: 0 for e in self.ENGS}
        self.pend_r = {e: [] for e in self.ENGS}
        self.pend_w = {e: [] for e in self.ENGS}
        self.bufw = {}
        self.bufr = {}
        self.waited = {e: {e2: 0 for e2 in self.ENGS} for e in self.ENGS}
        self.rr = {"Q": 0, "G": 0}
        self.n_waits = 0

    def op(self, eng, fn, reads=(), writes=(), inc=True):
        stream = eng
        if eng == "Q":
            eng = f"Q{self.rr['Q'] % self.NQ}"
            self.rr["Q"] += 1
        elif eng == "G":
            eng = f"G{self.rr['G'] % self.NG_}"
            self.rr["G"] += 1
        waits = {}

        def need(dep_eng, dep_seq, buf):
            if dep_eng == eng == "T" and buf.startswith("P"):
                return
            waits[dep_eng] = max(waits.get(dep_eng, 0), dep_seq)

        if eng in self.DMA and self.seq[eng] > 0:
            waits[eng] = self.seq[eng]
        for b in reads:
            w = self.bufw.get(b)
            if w:
                need(w[0], w[1], b)
        for b in writes:
            w = self.bufw.get(b)
            if w:
                need(w[0], w[1], b)
            for re, rs in self.bufr.get(b, {}).items():
                need(re, rs, b)
        todo = []
        for de, ds in waits.items():
            if ds > self.waited[eng][de]:
                self.waited[eng][de] = ds
                todo.append((de, ds))
        self.n_waits += len(todo)
        self.pend_r[eng].extend(reads)
        self.pend_w[eng].extend(writes)
        if inc:
            self.seq[eng] += 1
            s = self.seq[eng]
            for b in self.pend_w[eng]:
                self.bufw[b] = (eng, s)
                self.bufr.pop(b, None)
            for b in self.pend_r[eng]:
                self.bufr.setdefault(b, {})[eng] = s
            self.pend_r[eng] = []
            self.pend_w[eng] = []
        mult = 16 if eng in self.DMA else 1
        myeng = eng
        dmaset = self.DMA

        def go(p, sems):
            for de, ds in todo:
                p.wait_ge(sems[de], ds * (16 if de in dmaset else 1))
            inst = fn(p)
            if inc and inst is not None:
                inst.then_inc(sems[myeng], mult)

        self.ops[stream].append(go)

    def final_wait(self, eng, dep_stream):
        subs = [e for e in self.ENGS if e.startswith(dep_stream) and e != dep_stream] \
            if dep_stream in ("Q", "G") else [dep_stream]
        vals = [(de, self.seq[de] * (16 if de in self.DMA else 1)) for de in subs]

        def go(p, sems):
            for de, v in vals:
                if v:
                    p.wait_ge(sems[de], v)

        self.ops[eng].append(go)


def rope_tables_fm(T, base):
    i = np.arange(0, 128, 2, dtype=np.float32)
    inv = (np.float32(1.0) / (np.float32(base) ** (i / np.float32(128)))).astype(np.float32)
    t = np.arange(T, dtype=np.float32)
    f = t[None, :] * inv[:, None]
    cos = np.cos(f).astype(np.float32)
    sin = np.sin(f).astype(np.float32)
    return (np.ascontiguousarray(np.concatenate([cos, cos], axis=0)),
            np.ascontiguousarray(np.concatenate([sin, sin], axis=0)))


def host_constants(T, E):
    c = {}
    c["cosA"], c["sinA"] = rope_tables_fm(T, 1000000.0)
    c["cosB"], c["sinB"] = rope_tables_fm(T, 10000.0)
    S = np.zeros((128, 128), np.float32)
    for d in range(64):
        S[d + 64, d] = -1.0
        S[d, d + 64] = 1.0
    c["Smat"] = S
    c["ident"] = np.eye(128, dtype=np.float32)
    n_c = QW // KW
    kk = np.arange(KW)[:, None]
    qq = np.arange(QW)[None, :]
    masks = np.stack([(kk <= qq - KW * cc).astype(np.float32) for cc in range(n_c)])
    c["masks"] = np.ascontiguousarray(masks.transpose(1, 0, 2).reshape(KW, n_c * QW)).astype(bfdt)
    return c


def layout_weights(inp, T, E):
    NH = E // 128
    NEC = E // 128
    NG = np.asarray(inp["L0_attn_b_wk"]).shape[1] // 128
    w = {}

    def heads_img(A, nh, hw):
        return np.ascontiguousarray(
            A.reshape(NEC, 128, nh, hw).transpose(2, 1, 0, 3).reshape(nh, 128, NEC * hw))

    def fchunk_img(A):
        return np.ascontiguousarray(
            A.reshape(NH, 128, NEC, 128).transpose(2, 1, 0, 3).reshape(NEC, 128, NH * 128))

    wq_a = np.asarray(inp["L0_attn_a_wq"], np.float32)
    wk_a = np.asarray(inp["L0_attn_a_wk"], np.float32)
    wv_a = np.asarray(inp["L0_attn_a_wv"], np.float32)
    wo_a = np.asarray(inp["L0_attn_a_wo"], np.float32)
    w["wqa"] = heads_img(wq_a.T, NH, 128)
    w["wka"] = heads_img(wk_a.T, NH, 128)
    w["woa"] = fchunk_img(wo_a.T)
    w["wva"] = heads_img(wv_a.T, NH // 2, 256)

    wq_b = np.asarray(inp["L0_attn_b_wq"], np.float32)
    wk_b = np.asarray(inp["L0_attn_b_wk"], np.float32)
    wv_b = np.asarray(inp["L0_attn_b_wv"], np.float32)
    wo_b = np.asarray(inp["L0_attn_b_wo"], np.float32)
    w["wqb"] = heads_img(wq_b, NH, 128)
    w["wkb"] = heads_img(wk_b, NG, 128)
    nvp = max(1, NG // 2)
    vwb = wv_b.shape[1] // nvp
    w["wvb"] = heads_img(wv_b, nvp, vwb)
    w["wob"] = fchunk_img(wo_b)

    rw = np.asarray(inp["L0_ffn_router_w"], np.float32)
    dw = np.asarray(inp["L0_ffn_down_w"], np.float32)
    uw = np.asarray(inp["L0_ffn_up_w"], np.float32)
    D_all = dw.transpose(2, 0, 1).reshape(E, 128)
    C = np.concatenate([D_all, rw.T], axis=1)
    w["dr"] = np.ascontiguousarray(
        C.reshape(NEC, 128, 136).transpose(1, 0, 2).reshape(128, NEC * 136))
    w["u"] = np.ascontiguousarray(uw.transpose(0, 2, 1).reshape(128, E))

    w["bq"] = np.ascontiguousarray(np.asarray(inp["L0_attn_b_bq"], np.float32).reshape(NH, 128).T)
    w["bk"] = np.ascontiguousarray(np.asarray(inp["L0_attn_b_bk"], np.float32).reshape(NG, 128).T)
    w["bo"] = np.ascontiguousarray(np.asarray(inp["L0_attn_b_bo"], np.float32).reshape(NH, 128).T)
    w["bv"] = np.ascontiguousarray(np.asarray(inp["L0_attn_b_bv"], np.float32).reshape(1, NG * 128))
    w["qns"] = np.ascontiguousarray(np.asarray(inp["L0_attn_b_q_norm_scale"], np.float32).reshape(128, 1))
    w["kns"] = np.ascontiguousarray(np.asarray(inp["L0_attn_b_k_norm_scale"], np.float32).reshape(128, 1))
    return w


def build_program(T, E, NG):
    NH = E // 128
    NEC = E // 128
    NTC = T // 128
    NQC = T // QW
    n_c = QW // KW
    rep = NH // NG
    nvp = max(1, NG // 2)
    vwb = (NG * 128) // nvp
    scale_a = float(np.float32(1.0) / np.sqrt(np.float32(128)))
    scale_b = float((np.float64(128.0) ** -0.5) ** 2)

    nc = bass.Bass("TRN2", target_bir_lowering=False)

    xin = nc.dram_tensor("xin", [E, T], F32, kind="ExternalInput").ap()
    out = nc.dram_tensor("out", [E, T], F32, kind="ExternalOutput").ap()
    d_cosA = nc.dram_tensor("cosA", [128, T], F32, kind="ExternalInput").ap()
    d_sinA = nc.dram_tensor("sinA", [128, T], F32, kind="ExternalInput").ap()
    d_cosB = nc.dram_tensor("cosB", [128, T], F32, kind="ExternalInput").ap()
    d_sinB = nc.dram_tensor("sinB", [128, T], F32, kind="ExternalInput").ap()
    d_S = nc.dram_tensor("Smat", [128, 128], F32, kind="ExternalInput").ap()
    d_ident = nc.dram_tensor("ident", [128, 128], F32, kind="ExternalInput").ap()
    d_masks = nc.dram_tensor("masks", [KW, n_c * QW], BF16, kind="ExternalInput").ap()
    d_wqa = nc.dram_tensor("wqa", [NH, 128, NEC * 128], F32, kind="ExternalInput").ap()
    d_wka = nc.dram_tensor("wka", [NH, 128, NEC * 128], F32, kind="ExternalInput").ap()
    d_wva = nc.dram_tensor("wva", [NH // 2, 128, NEC * 256], F32, kind="ExternalInput").ap()
    d_woa = nc.dram_tensor("woa", [NEC, 128, NH * 128], F32, kind="ExternalInput").ap()
    d_wqb = nc.dram_tensor("wqb", [NH, 128, NEC * 128], F32, kind="ExternalInput").ap()
    d_wkb = nc.dram_tensor("wkb", [NG, 128, NEC * 128], F32, kind="ExternalInput").ap()
    d_wvb = nc.dram_tensor("wvb", [nvp, 128, NEC * vwb], F32, kind="ExternalInput").ap()
    d_wob = nc.dram_tensor("wob", [NEC, 128, NH * 128], F32, kind="ExternalInput").ap()
    d_dr = nc.dram_tensor("dr", [128, NEC * 136], F32, kind="ExternalInput").ap()
    d_u = nc.dram_tensor("u", [128, E], F32, kind="ExternalInput").ap()
    d_bq = nc.dram_tensor("bq", [128, NH], F32, kind="ExternalInput").ap()
    d_bk = nc.dram_tensor("bk", [128, NG], F32, kind="ExternalInput").ap()
    d_bo = nc.dram_tensor("bo", [128, NH], F32, kind="ExternalInput").ap()
    d_bv = nc.dram_tensor("bv", [1, NG * 128], F32, kind="ExternalInput").ap()
    d_qns = nc.dram_tensor("qns", [128, 1], F32, kind="ExternalInput").ap()
    d_kns = nc.dram_tensor("kns", [128, 1], F32, kind="ExternalInput").ap()
    osp = nc.dram_tensor("osp", [NH * 128, T], BF16, kind="Internal").ap()

    A = nc.alloc_sbuf_tensor
    x_t = [A(f"x{i}", [128, T], BF16) for i in range(NEC)]
    y_t = [A(f"y{i}", [128, T], BF16) for i in range(NEC)]
    WSW = max(NEC * 256, NEC * 136 + E, NEC * vwb, NH * 128)
    ws = [A("ws0", [128, WSW], BF16), A("ws1", [128, WSW], BF16)]
    cos_s = A("cos_s", [128, T], BF16)
    sin_s = A("sin_s", [128, T], BF16)
    q_sb = A("q_sb", [128, T], BF16)
    k_sb = A("k_sb", [128, T], BF16)
    tmp1 = A("tmp1", [128, T], BF16)
    v_tm = [A(f"vtm{i}", [128, 256], BF16) for i in range(NTC)]
    expT = [A("expT0", [128, 2 * QW], BF16), A("expT1", [128, 2 * QW], BF16)]
    recipZ = A("recipZ", [128, QW], BF16)
    outn = A("outn", [128, QW], BF16)
    mB = A("mB", [128, T], BF16)
    rstdB = A("rstdB", [128, T], BF16)
    m2s = A("m2s", [128, QW], BF16)
    vars_ = A("vars", [128, QW], F32)
    sqs = [A("sqs0", [128, QW], BF16), A("sqs1", [128, QW], BF16)]
    stage = [A("stage0", [128, QW], F32), A("stage1", [128, QW], F32)]
    agT = A("agT", [128, T], BF16)
    a_sb = [A("a_sb0", [128, 128], BF16), A("a_sb1", [128, 128], BF16)]
    ag_sb = [A("ag_sb0", [128, 128], F32), A("ag_sb1", [128, 128], F32)]
    w8 = A("w8", [128, 8], F32)
    g8 = A("g8", [128, 16], F32)
    smalls = A("smalls", [128, 16], F32)
    ident_s = A("ident_s", [128, 128], F32)
    allones = A("allones", [128, 128], BF16)
    Smat_s = A("Smat_s", [128, 128], BF16)
    masks_s = A("masks_s", [128, n_c * QW], BF16)
    bq_s = A("bq_s", [128, NH], F32)
    bk_s = A("bk_s", [128, NG], F32)
    bo_s = A("bo_s", [128, NH], F32)
    bv_pad = A("bv_pad", [128, vwb], BF16)
    qns_s = A("qns_s", [128, 1], F32)
    kns_s = A("kns_s", [128, 1], F32)
    ebias = A("ebias", [128, 1], F32)
    eps5 = A("eps5", [128, 1], F32)
    eps6 = A("eps6", [128, 1], F32)

    PT = [nc.alloc_psum_tensor(f"PT{i}", [128, 1024], F32) for i in range(4)]

    def bank(b, n=512):
        return PT[b // 2][:, (b % 2) * 512:(b % 2) * 512 + n]

    PS = [bank(i) for i in range(8)]

    P = Prog()

    def dma(eng, dst_ap, src_ap, reads, writes):
        P.op(eng, lambda p, d=dst_ap, s=src_ap: p.dma_start(out=d, in_=s),
             reads=reads, writes=writes)

    def mmN(bi, n, lhsT, rhs, reads, start, stop, inc=True):
        P.op("T", lambda p, l=lhsT, r=rhs, st=start, sp=stop, b=bi, nn=n:
             p.matmul(bank(b, nn), l, r, start=st, stop=sp),
             reads=reads, writes=[f"P{bi}"], inc=inc)

    def mm2(bi, lhsT, rhs, reads, start, stop, inc=True):
        mmN(bi, QW, lhsT, rhs, reads, start, stop, inc)

    P.op("V", lambda p: p.memset(allones[:], 1.0), writes=["allones"])
    P.op("V", lambda p: p.memset(eps5[:], 1e-5), writes=["eps5"])
    P.op("V", lambda p: p.memset(eps6[:], 1e-6), writes=["eps6"])
    P.op("V", lambda p: p.memset(bv_pad[:], 0.0), writes=["bv_pad"])
    dma("Q", ident_s[:], d_ident[:, :], [], ["ident_s"])
    dma("Q", masks_s[:], d_masks[:, :], [], ["masks_s"])
    dma("G", Smat_s[:], d_S[:, :], [], ["Smat_s"])
    dma("G", cos_s[:], d_cosA[:, :], [], ["cos_s"])
    dma("G", sin_s[:], d_sinA[:, :], [], ["sin_s"])
    dma("G", bq_s[:], d_bq[:, :], [], ["bq_s"])
    dma("G", bk_s[:], d_bk[:, :], [], ["bk_s"])
    dma("G", bo_s[:], d_bo[:, :], [], ["bo_s"])
    dma("G", qns_s[:], d_qns[:, :], [], ["qns_s"])
    dma("G", kns_s[:], d_kns[:, :], [], ["kns_s"])
    for i in range(NEC):
        dma("G", x_t[i][:], xin[i * 128:(i + 1) * 128, :], [], [f"x{i}"])

    phases = []

    def add_phase(dmas):
        phases.append(dmas)
        return len(phases) - 1

    emitted = set()

    def emit_phase(idx):
        if idx in emitted or idx >= len(phases):
            return
        emitted.add(idx)
        slot = idx % 2
        name = f"ws{slot}"
        for (off, width, src) in phases[idx]:
            dma("G", ws[slot][:, off:off + width], src, [name], [name])

    def slot_of(idx):
        return idx % 2, f"ws{idx % 2}"

    def layer_norm():
        steps = [(ec, tc) for ec in range(NEC) for tc in range(NQC)]

        def square(i):
            ec, tc = steps[i]
            sq, sqn = sqs[i % 2], f"sqs{i % 2}"
            sl = slice(tc * QW, (tc + 1) * QW)
            P.op("V", lambda p, e=ec, s=sl, q=sq: p.tensor_tensor(
                q[:], x_t[e][:, s], x_t[e][:, s], ALU.mult),
                reads=[f"x{ec}"], writes=[sqn])

        square(0)
        for i, (ec, tc) in enumerate(steps):
            if i + 1 < len(steps):
                square(i + 1)
            sqn = f"sqs{i % 2}"
            sl = slice(tc * QW, (tc + 1) * QW)
            mm2(tc, allones[:], x_t[ec][:, sl], ["allones", f"x{ec}"],
                start=(ec == 0), stop=(ec == NEC - 1))
            mm2(4 + tc, allones[:], sqs[i % 2][:], ["allones", sqn],
                start=(ec == 0), stop=(ec == NEC - 1))
        for tc in range(NQC):
            sl = slice(tc * QW, (tc + 1) * QW)
            P.op("S", lambda p, t=tc, s=sl: p.activation(
                mB[:, s], PS[t][:, 0:QW], AF.Identity, scale=1.0 / E),
                reads=[f"P{tc}"], writes=["mB"])
            P.op("S", lambda p, s=sl: p.activation(m2s[:], mB[:, s], AF.Square),
                 reads=["mB"], writes=["m2s"])
            P.op("V", lambda p, t=tc: p.scalar_tensor_tensor(
                vars_[:], PS[4 + t][:, 0:QW], 1.0 / E, m2s[:], ALU.mult, ALU.subtract),
                reads=[f"P{4 + tc}", "m2s"], writes=["vars"])
            P.op("S", lambda p: p.activation(vars_[:], vars_[:], AF.Ln, bias=eps5[:]),
                 reads=["vars", "eps5"], writes=["vars"])
            P.op("S", lambda p, s=sl: p.activation(rstdB[:, s], vars_[:], AF.Exp, scale=-0.5),
                 reads=["vars"], writes=["rstdB"])
            P.op("V", lambda p, s=sl: p.tensor_tensor(mB[:, s], mB[:, s], rstdB[:, s], ALU.mult),
                 reads=["rstdB"], writes=["mB"])
        for ec in range(NEC):
            P.op("V", lambda p, e=ec: p.tensor_tensor(y_t[e][:], x_t[e][:], rstdB[:], ALU.mult),
                 reads=[f"x{ec}", "rstdB"], writes=[f"y{ec}"])
            P.op("V", lambda p, e=ec: p.tensor_tensor(y_t[e][:], y_t[e][:], mB[:], ALU.subtract),
                 reads=["mB"], writes=[f"y{ec}"])

    def proj(dst, dstname, sname, slot_ap, woff, bias_ap, bias_name, after_tc=None):
        for tc in range(NQC):
            b = 6 + (tc % 2)
            for ec in range(NEC):
                lhs = slot_ap[:, woff + ec * 128: woff + (ec + 1) * 128]
                mm2(b, lhs, y_t[ec][:, tc * QW:(tc + 1) * QW],
                    [sname, f"y{ec}"], start=(ec == 0), stop=(ec == NEC - 1),
                    inc=(ec == NEC - 1))
            if bias_ap is None:
                P.op("S", lambda p, bb=b, t=tc: p.activation(
                    dst[:, t * QW:(t + 1) * QW], PS[bb][:, 0:QW], AF.Identity),
                    reads=[f"P{b}"], writes=[f"{dstname}.{tc}"])
            else:
                P.op("S", lambda p, bb=b, t=tc, ba=bias_ap: p.activation(
                    dst[:, t * QW:(t + 1) * QW], PS[bb][:, 0:QW], AF.Identity, bias=ba),
                    reads=[f"P{b}", bias_name], writes=[f"{dstname}.{tc}"])
            if after_tc is not None and tc > 0:
                after_tc(tc - 1)
        if after_tc is not None:
            after_tc(NQC - 1)

    def rope_tc(buf, bufname, tc, bank0=4):
        if True:
            b = bank0 + (tc % 2)
            sl = slice(tc * QW, (tc + 1) * QW)
            nm = f"{bufname}.{tc}"
            mm2(b, Smat_s[:], buf[:, sl], ["Smat_s", nm], start=True, stop=True)
            P.op("V", lambda p, bb=b, s=sl: p.tensor_tensor(
                tmp1[:, s], PS[bb][:, 0:QW], sin_s[:, s], ALU.mult),
                reads=[f"P{b}", "sin_s"], writes=[f"tmp1.{tc}"])
            P.op("V", lambda p, s=sl: p.tensor_tensor(
                buf[:, s], buf[:, s], cos_s[:, s], ALU.mult),
                reads=["cos_s", nm], writes=[nm])
            P.op("V", lambda p, s=sl: p.tensor_tensor(
                buf[:, s], buf[:, s], tmp1[:, s], ALU.add),
                reads=[f"tmp1.{tc}"], writes=[nm])

    def rope(buf, bufname, bank0=4):
        for tc in range(NQC):
            rope_tc(buf, bufname, tc, bank0)

    def rms_tc(buf, bufname, tc, scale_ap, scale_name):
        sq, sqn = sqs[tc % 2], f"sqs{tc % 2}"
        b = 4 + (tc % 2)
        sl = slice(tc * QW, (tc + 1) * QW)
        nm = f"{bufname}.{tc}"
        P.op("V", lambda p, s=sl, q=sq: p.tensor_tensor(
            q[:], buf[:, s], buf[:, s], ALU.mult),
            reads=[nm], writes=[sqn])
        mm2(b, allones[:], sq[:], ["allones", sqn], start=True, stop=True)
        P.op("S", lambda p, bb=b: p.activation(vars_[:], PS[bb][:, 0:QW], AF.Ln,
                                               scale=1.0 / 128.0, bias=eps6[:]),
             reads=[f"P{b}", "eps6"], writes=["vars"])
        P.op("S", lambda p, s=sl: p.activation(
            rstdB[:, s], vars_[:], AF.Exp, scale=-0.5),
            reads=["vars"], writes=["rstdB"])
        P.op("V", lambda p, s=sl: p.tensor_tensor(buf[:, s], buf[:, s], rstdB[:, s], ALU.mult),
             reads=["rstdB", nm], writes=[nm])
        P.op("V", lambda p, sa=scale_ap, s=sl: p.tensor_scalar(
            buf[:, s], buf[:, s], sa, None, ALU.mult),
            reads=[scale_name, nm], writes=[nm])

    def norm_tc(buf, bufname, tc, scol, bank=4):
        b = bank + (tc % 2)
        sq, sqn = sqs[tc % 2], f"sqs{tc % 2}"
        sl = slice(tc * QW, (tc + 1) * QW)
        P.op("V", lambda p, s=sl, q=sq: p.tensor_tensor(
            q[:], buf[:, s], buf[:, s], ALU.mult),
            reads=[f"{bufname}.{tc}"], writes=[sqn])
        mm2(b, allones[:], sq[:], ["allones", sqn], start=True, stop=True)
        P.op("V", lambda p, bb=b, c=scol: p.reduce_max(
            smalls[:, c:c + 1], PS[bb][:, 0:QW], axis=AX.X),
            reads=[f"P{b}"], writes=["smalls"])

    def combine_norms(outcol, base):
        P.op("V", lambda p, o=outcol, ba=base: p.reduce_max(
            smalls[:, o:o + 1], smalls[:, ba:ba + NQC], axis=AX.X),
            reads=["smalls"], writes=["smalls"])

    def exp_bias(e_scale):
        P.op("V", lambda p: p.tensor_tensor(smalls[:, 2:3], smalls[:, 0:1],
                                            smalls[:, 1:2], ALU.mult),
             reads=["smalls"], writes=["smalls"])
        P.op("S", lambda p: p.activation(smalls[:, 3:4], smalls[:, 2:3], AF.Ln),
             reads=["smalls"], writes=["smalls"])
        P.op("S", lambda p: p.activation(smalls[:, 3:4], smalls[:, 3:4], AF.Exp, scale=0.5),
             reads=["smalls"], writes=["smalls"])
        P.op("V", lambda p, es=e_scale: p.tensor_scalar(
            smalls[:, 4:5], smalls[:, 3:4], -es, 40.0, ALU.mult, ALU.add),
            reads=["smalls"], writes=["smalls"])
        P.op("V", lambda p: p.tensor_scalar(ebias[:], smalls[:, 4:5], 0.0, None, ALU.min),
             reads=["smalls"], writes=["ebias"])

    def attn_head(h, vslice, e_scale):
        for qc in range(NQC):
            qsl = slice(qc * QW, (qc + 1) * QW)
            nkc = (qc + 1) * n_c
            npair = nkc // 2

            def sc_exp(j):
                par = j % 2
                for half in range(2):
                    kc = 2 * j + half
                    mm2(2 * par + half, k_sb[:, kc * KW:(kc + 1) * KW], q_sb[:, qsl],
                        [f"k_sb.{kc // n_c}", f"q_sb.{qc}"], start=True, stop=True)
                P.op("S", lambda p, pa=par: p.activation(
                    expT[pa][:], PT[pa][:, 0:2 * QW], AF.Exp, bias=ebias[:], scale=e_scale),
                    reads=[f"P{2 * par}", f"P{2 * par + 1}", "ebias"], writes=[f"expT{par}"])
                for half in range(2):
                    kc = 2 * j + half
                    cc = kc - qc * n_c
                    if cc >= 0:
                        P.op("V", lambda p, pa=par, c=cc, hf=half: p.tensor_tensor(
                            expT[pa][:, hf * QW:(hf + 1) * QW],
                            expT[pa][:, hf * QW:(hf + 1) * QW],
                            masks_s[:, c * QW:(c + 1) * QW], ALU.mult),
                            reads=["masks_s"], writes=[f"expT{par}"])

            def zav(j):
                par = j % 2
                for half in range(2):
                    kc = 2 * j + half
                    e_ap = expT[par][:, half * QW:(half + 1) * QW]
                    mm2(4, allones[:], e_ap, ["allones", f"expT{par}"],
                        start=(kc == 0), stop=(kc == nkc - 1))
                    mm2(5, v_tm[kc][:, vslice], e_ap,
                        [f"vtm{kc}", f"expT{par}"],
                        start=(kc == 0), stop=(kc == nkc - 1))

            sc_exp(0)
            for j in range(1, npair):
                sc_exp(j)
                zav(j - 1)
            zav(npair - 1)
            P.op("S", lambda p: p.activation(vars_[:], PS[4][:, 0:QW], AF.Ln),
                 reads=["P4"], writes=["vars"])
            P.op("S", lambda p: p.activation(recipZ[:], vars_[:], AF.Exp, scale=-1.0),
                 reads=["vars"], writes=["recipZ"])
            P.op("V", lambda p: p.tensor_tensor(outn[:], PS[5][:, 0:QW], recipZ[:], ALU.mult),
                 reads=["P5", "recipZ"], writes=["outn"])
            dma("Q", osp[h * 128:(h + 1) * 128, qc * QW:(qc + 1) * QW], outn[:],
                ["outn"], [f"osp{h}"])

    def v_proj(sname, slot_ap, vwidth, bvslice=None):
        for tg in range((NTC + 3) // 4):
            tchs = list(range(tg * 4, min((tg + 1) * 4, NTC)))
            for ec in range(NEC):
                for i, t in enumerate(tchs):
                    bank = 4 + i
                    last = (ec == NEC - 1) and bvslice is None
                    mmN(bank, vwidth, y_t[ec][:, t * 128:(t + 1) * 128],
                        slot_ap[:, ec * vwidth:(ec + 1) * vwidth],
                        [f"y{ec}", sname], start=(ec == 0), stop=last,
                        inc=(ec == NEC - 1))
            if bvslice is not None:
                for i, t in enumerate(tchs):
                    mmN(4 + i, vwidth, allones[:], bv_pad[:, 0:vwidth],
                        ["allones", "bv_pad"], start=False, stop=True)
            for i, t in enumerate(tchs):
                bank = 4 + i
                P.op("V", lambda p, b=bank, tt=t, vw=vwidth: p.tensor_copy(
                    v_tm[tt][:, 0:vw], PS[b][:, 0:vw]),
                    reads=[f"P{bank}"], writes=[f"vtm{t}"])

    def wo_phase(phase_base, with_bo):
        for h in range(NH):
            dma("Q", y_t[h][:], osp[h * 128:(h + 1) * 128, :], [f"osp{h}"], [f"y{h}"])
        emit_phase(phase_base)
        for fc in range(NEC):
            if fc + 1 < NEC:
                emit_phase(phase_base + fc + 1)
            slot, sname = slot_of(phase_base + fc)
            for tc in range(NQC):
                b = tc % 2
                for h in range(NH):
                    mm2(b, ws[slot][:, h * 128:(h + 1) * 128],
                        y_t[h][:, tc * QW:(tc + 1) * QW],
                        [sname, f"y{h}"], start=(h == 0), stop=(h == NH - 1),
                        inc=(h == NH - 1))
                if with_bo:
                    P.op("V", lambda p, bb=b, f=fc, t=tc: p.scalar_tensor_tensor(
                        x_t[f][:, t * QW:(t + 1) * QW], PS[bb][:, 0:QW],
                        bo_s[:, f:f + 1], x_t[f][:, t * QW:(t + 1) * QW],
                        ALU.add, ALU.add),
                        reads=[f"P{b}", "bo_s", f"x{fc}"], writes=[f"x{fc}"])
                else:
                    P.op("V", lambda p, bb=b, f=fc, t=tc: p.tensor_tensor(
                        x_t[f][:, t * QW:(t + 1) * QW],
                        x_t[f][:, t * QW:(t + 1) * QW], PS[bb][:, 0:QW], ALU.add),
                        reads=[f"P{b}", f"x{fc}"], writes=[f"x{fc}"])

    def ffn(phase_idx, last=False):
        layer_norm()
        emit_phase(phase_idx)
        slot, sname = slot_of(phase_idx)
        droff = 0
        uoff = NEC * 136
        def gate_chain(t):
            b = t % 2
            par = t % 2
            P.op("V", lambda p, bb=b: p.reduce_max(
                smalls[:, 0:1], PS[bb][:, 128:136], axis=AX.X),
                reads=[f"P{b}"], writes=["smalls"])
            P.op("V", lambda p: p.tensor_scalar(
                smalls[:, 1:2], smalls[:, 0:1], -1.0, None, ALU.mult),
                reads=["smalls"], writes=["smalls"])
            P.op("S", lambda p, bb=b: p.activation(
                w8[:], PS[bb][:, 128:136], AF.Exp, bias=smalls[:, 1:2]),
                reads=[f"P{b}", "smalls"], writes=["w8"])
            P.op("V", lambda p: p.reduce_sum(smalls[:, 2:3], w8[:], axis=AX.X),
                 reads=["w8"], writes=["smalls"])
            P.op("V", lambda p: p.reciprocal(smalls[:, 3:4], smalls[:, 2:3]),
                 reads=["smalls"], writes=["smalls"])
            P.op("V", lambda p: p.tensor_scalar(
                g8[:, 0:8], w8[:], 1.0, -1e9, ALU.is_ge, ALU.mult),
                reads=["w8"], writes=["g8"])
            P.op("V", lambda p: p.tensor_tensor(g8[:, 8:16], w8[:], g8[:, 0:8], ALU.add),
                 reads=["w8"], writes=["g8"])
            P.op("V", lambda p: p.reduce_max(smalls[:, 4:5], g8[:, 8:16], axis=AX.X),
                 reads=["g8"], writes=["smalls"])
            P.op("V", lambda p: p.tensor_scalar(
                g8[:, 0:8], w8[:], smalls[:, 4:5], None, ALU.is_ge),
                reads=["w8", "smalls"], writes=["g8"])
            P.op("V", lambda p: p.tensor_tensor(w8[:], w8[:], g8[:, 0:8], ALU.mult),
                 reads=["g8"], writes=["w8"])
            P.op("V", lambda p: p.tensor_scalar(
                w8[:], w8[:], smalls[:, 3:4], None, ALU.mult),
                reads=["smalls"], writes=["w8"])
            P.op("S", lambda p, bb=b: p.activation(
                vars_[:, 0:128], PS[bb][:, 0:128], AF.Exp, scale=-1.0),
                reads=[f"P{b}"], writes=["vars"])
            P.op("V", lambda p: p.tensor_scalar(vars_[:, 0:128], vars_[:, 0:128], 1.0, None, ALU.add),
                 reads=["vars"], writes=["vars"])
            P.op("V", lambda p: p.reciprocal(vars_[:, 128:256], vars_[:, 0:128]),
                 reads=["vars"], writes=["vars"])
            P.op("V", lambda p, bb=b, pa=par: p.tensor_tensor(
                a_sb[pa][:], PS[bb][:, 0:128], vars_[:, 128:256], ALU.mult),
                reads=[f"P{b}", "vars"], writes=[f"a_sb{par}"])
            for ex in range(8):
                P.op("V", lambda p, e=ex, pa=par: p.tensor_scalar(
                    ag_sb[pa][:, e * 16:(e + 1) * 16], a_sb[pa][:, e * 16:(e + 1) * 16],
                    w8[:, e:e + 1], None, ALU.mult),
                    reads=[f"a_sb{par}", "w8"], writes=[f"ag_sb{par}"], inc=(ex == 7))

        def transp(t):
            par = t % 2
            tb = 2 + (t % 2)
            P.op("T", lambda p, bb=tb, pa=par: p.transpose(
                PS[bb][:, 0:128], ag_sb[pa][:], ident_s[:]),
                reads=[f"ag_sb{par}", "ident_s"], writes=[f"P{tb}"])
            P.op("S", lambda p, bb=tb, tt=t: p.activation(
                agT[:, tt * 128:(tt + 1) * 128], PS[bb][:, 0:128], AF.Identity),
                reads=[f"P{tb}"], writes=["agT"])

        for t in range(NTC):
            b = t % 2
            for ec in range(NEC):
                mmN(b, 136, y_t[ec][:, t * 128:(t + 1) * 128],
                    ws[slot][:, droff + ec * 136: droff + (ec + 1) * 136],
                    [f"y{ec}", sname], start=(ec == 0), stop=(ec == NEC - 1),
                    inc=(ec == NEC - 1))
            if t > 0:
                transp(t - 1)
            gate_chain(t)
        transp(NTC - 1)
        for fc in range(NEC):
            for tc in range(NQC):
                b = 4 + ((fc * NQC + tc) % 2)
                mm2(b, ws[slot][:, uoff + fc * 128: uoff + (fc + 1) * 128],
                    agT[:, tc * QW:(tc + 1) * QW], [sname, "agT"],
                    start=True, stop=True)
                if last:
                    par = (fc * NQC + tc) % 2
                    P.op("V", lambda p, bb=b, f=fc, t=tc, pa=par: p.tensor_tensor(
                        stage[pa][:], x_t[f][:, t * QW:(t + 1) * QW],
                        PS[bb][:, 0:QW], ALU.add),
                        reads=[f"P{b}", f"x{fc}"], writes=[f"stage{par}"])
                    dma("Q", out[fc * 128:(fc + 1) * 128, tc * QW:(tc + 1) * QW],
                        stage[par][:], [f"stage{par}"], ["outhbm"])
                else:
                    P.op("V", lambda p, bb=b, f=fc, t=tc: p.tensor_tensor(
                        x_t[f][:, t * QW:(t + 1) * QW],
                        x_t[f][:, t * QW:(t + 1) * QW], PS[bb][:, 0:QW], ALU.add),
                        reads=[f"P{b}", f"x{fc}"], writes=[f"x{fc}"])

    gpv = NG // nvp
    ph_wva, ph_wqk = {}, {}
    for pair in range(NH // 2):
        ph_wva[pair] = add_phase([(0, NEC * 256, d_wva[pair, :, :])])
        for hh in range(2):
            h = pair * 2 + hh
            ph_wqk[h] = add_phase([(0, NEC * 128, d_wqa[h, :, :]),
                                   (NEC * 128, NEC * 128, d_wka[h, :, :])])
    ph_woa = [add_phase([(0, NH * 128, d_woa[fc, :, :])]) for fc in range(NEC)]
    PH_FFN1 = add_phase([(0, NEC * 136, d_dr[:, :]), (NEC * 136, E, d_u[:, :])])
    ph_wvb, ph_wkb, ph_wqb = {}, {}, {}
    for vp in range(nvp):
        ph_wvb[vp] = add_phase([(0, NEC * vwb, d_wvb[vp, :, :])])
        for gg in range(gpv):
            g = vp * gpv + gg
            ph_wkb[g] = add_phase([(0, NEC * 128, d_wkb[g, :, :])])
            for r in range(rep):
                h = g * rep + r
                ph_wqb[h] = add_phase([(0, NEC * 128, d_wqb[h, :, :])])
    ph_wob = [add_phase([(0, NH * 128, d_wob[fc, :, :])]) for fc in range(NEC)]
    PH_FFN2 = add_phase([(0, NEC * 136, d_dr[:, :]), (NEC * 136, E, d_u[:, :])])

    layer_norm()
    emit_phase(0)
    for pair in range(NH // 2):
        emit_phase(ph_wva[pair])
        emit_phase(ph_wva[pair] + 1)
        vslot, vsname = slot_of(ph_wva[pair])
        v_proj(vsname, ws[vslot], 256)
        for hh in range(2):
            h = pair * 2 + hh
            phq = ph_wqk[h]
            emit_phase(phq)
            emit_phase(phq + 1)
            qslot, qsname = slot_of(phq)
            proj(q_sb, "q_sb", qsname, ws[qslot], 0, None, None)

            def chains_a(tc):
                rope_tc(q_sb, "q_sb", tc)
                norm_tc(q_sb, "q_sb", tc, 12 + tc)
            proj(k_sb, "k_sb", qsname, ws[qslot], NEC * 128, None, None,
                 after_tc=chains_a)
            for tc in range(NQC):
                rope_tc(k_sb, "k_sb", tc)
                norm_tc(k_sb, "k_sb", tc, 8 + tc)
            combine_norms(1, 8)
            combine_norms(0, 12)
            exp_bias(scale_a)
            attn_head(h, slice(hh * 128, (hh + 1) * 128), scale_a)
    wo_phase(ph_woa[0], with_bo=False)

    ffn(PH_FFN1)

    layer_norm()
    dma("G", cos_s[:], d_cosB[:, :], ["cos_s"], ["cos_s"])
    dma("G", sin_s[:], d_sinB[:, :], ["sin_s"], ["sin_s"])
    emit_phase(ph_wvb[0])
    for vp in range(nvp):
        emit_phase(ph_wvb[vp])
        emit_phase(ph_wvb[vp] + 1)
        vslot, vsname = slot_of(ph_wvb[vp])
        dma("G", bv_pad[0:1, :], d_bv[:, vp * vwb:(vp + 1) * vwb], ["bv_pad"], ["bv_pad"])
        v_proj(vsname, ws[vslot], vwb, bvslice=slice(vp * vwb, (vp + 1) * vwb))
        for gg in range(gpv):
            g = vp * gpv + gg
            phk = ph_wkb[g]
            emit_phase(phk)
            emit_phase(phk + 1)
            kslot, ksname = slot_of(phk)
            def kchain_b(tc):
                rms_tc(k_sb, "k_sb", tc, kns_s[:], "kns_s")
                rope_tc(k_sb, "k_sb", tc)
                norm_tc(k_sb, "k_sb", tc, 8 + tc)
            proj(k_sb, "k_sb", ksname, ws[kslot], 0, bk_s[:, g:g + 1], "bk_s",
                 after_tc=kchain_b)
            combine_norms(1, 8)
            for r in range(rep):
                h = g * rep + r
                phq = ph_wqb[h]
                emit_phase(phq)
                emit_phase(phq + 1)
                qslot, qsname = slot_of(phq)
                def qchain_b(tc):
                    rms_tc(q_sb, "q_sb", tc, qns_s[:], "qns_s")
                    rope_tc(q_sb, "q_sb", tc)
                    norm_tc(q_sb, "q_sb", tc, 12 + tc)
                proj(q_sb, "q_sb", qsname, ws[qslot], 0, bq_s[:, h:h + 1], "bq_s",
                     after_tc=qchain_b)
                combine_norms(0, 12)
                exp_bias(scale_b)
                attn_head(h, slice(gg * 128, (gg + 1) * 128), scale_b)
    wo_phase(ph_wob[0], with_bo=True)

    ffn(PH_FFN2, last=True)

    P.final_wait("Q", "Q")
    P.final_wait("G", "G")
    assert len(emitted) == len(phases), (len(emitted), len(phases))

    import contextlib
    with contextlib.ExitStack() as stack:
        sems = {e: stack.enter_context(nc.semaphore(f"sem{e}")) for e in Prog.ENGS}
        block = stack.enter_context(nc.Block())

        @block.tensor
        def _(t):
            for f in P.ops["T"]:
                f(t, sems)

        @block.scalar
        def _(s):
            for f in P.ops["S"]:
                f(s, sems)

        @block.vector
        def _(v):
            for f in P.ops["V"]:
                f(v, sems)

        @block.sync
        def _(q):
            for f in P.ops["Q"]:
                f(q, sems)

        @block.gpsimd
        def _(g):
            for f in P.ops["G"]:
                f(g, sems)

    return nc


def make_in_maps(inputs):
    x = np.asarray(inputs["L0_attn_a_x"], np.float32)
    B, T, E = x.shape
    NG = np.asarray(inputs["L0_attn_b_wk"]).shape[1] // 128
    consts = host_constants(T, E)
    w = layout_weights(inputs, T, E)
    shared = {
        "cosA": consts["cosA"], "sinA": consts["sinA"],
        "cosB": consts["cosB"], "sinB": consts["sinB"],
        "Smat": consts["Smat"], "ident": consts["ident"], "masks": consts["masks"],
        "wqa": w["wqa"], "wka": w["wka"], "wva": w["wva"], "woa": w["woa"],
        "wqb": w["wqb"], "wkb": w["wkb"], "wvb": w["wvb"], "wob": w["wob"],
        "dr": w["dr"], "u": w["u"],
        "bq": w["bq"], "bk": w["bk"], "bo": w["bo"], "bv": w["bv"],
        "qns": w["qns"], "kns": w["kns"],
    }
    in_maps = []
    for b in range(B):
        m = dict(shared)
        m["xin"] = np.ascontiguousarray(x[b].T)
        in_maps.append(m)
    return in_maps, T, E, NG


def kernel(**inputs):
    x = np.asarray(inputs["L0_attn_a_x"], np.float32)
    B, T, E = x.shape
    assert B == N_CORES, (B, N_CORES)
    in_maps, T, E, NG = make_in_maps(inputs)
    nc = build_program(T, E, NG)
    res = run_bass_kernel_spmd(nc, in_maps, core_ids=list(range(N_CORES)))
    outs = [np.ascontiguousarray(res.results[b]["out"].T) for b in range(B)]
    return np.stack(outs).astype(np.float32)


# version 15
# speedup vs baseline: 1.1358x; 1.0033x over previous
import numpy as np
import ml_dtypes

import concourse.bass as bass
import concourse.mybir as mybir
from concourse.bass_utils import run_bass_kernel_spmd

F32 = mybir.dt.float32
BF16 = mybir.dt.bfloat16
AF = mybir.ActivationFunctionType
ALU = mybir.AluOpType
AX = mybir.AxisListType
bfdt = ml_dtypes.bfloat16

N_CORES = 8
QW = 512
KW = 128


class Prog:

    NQ, NG_ = 4, 8
    ENGS = (["T", "S", "V"] + [f"Q{i}" for i in range(NQ)]
            + [f"G{i}" for i in range(NG_)])
    DMA = {e for e in ENGS if e[0] in "QG"}

    def __init__(self):
        self.ops = {"T": [], "S": [], "V": [], "Q": [], "G": []}
        self.seq = {e: 0 for e in self.ENGS}
        self.pend_r = {e: [] for e in self.ENGS}
        self.pend_w = {e: [] for e in self.ENGS}
        self.bufw = {}
        self.bufr = {}
        self.waited = {e: {e2: 0 for e2 in self.ENGS} for e in self.ENGS}
        self.rr = {"Q": 0, "G": 0}
        self.n_waits = 0

    def op(self, eng, fn, reads=(), writes=(), inc=True):
        stream = eng
        if eng == "Q":
            eng = f"Q{self.rr['Q'] % self.NQ}"
            self.rr["Q"] += 1
        elif eng == "G":
            eng = f"G{self.rr['G'] % self.NG_}"
            self.rr["G"] += 1
        waits = {}

        def need(dep_eng, dep_seq, buf):
            if dep_eng == eng == "T" and buf.startswith("P"):
                return
            waits[dep_eng] = max(waits.get(dep_eng, 0), dep_seq)

        if eng in self.DMA and self.seq[eng] > 0:
            waits[eng] = self.seq[eng]
        for b in reads:
            w = self.bufw.get(b)
            if w:
                need(w[0], w[1], b)
        for b in writes:
            w = self.bufw.get(b)
            if w:
                need(w[0], w[1], b)
            for re, rs in self.bufr.get(b, {}).items():
                need(re, rs, b)
        todo = []
        for de, ds in waits.items():
            if ds > self.waited[eng][de]:
                self.waited[eng][de] = ds
                todo.append((de, ds))
        self.n_waits += len(todo)
        self.pend_r[eng].extend(reads)
        self.pend_w[eng].extend(writes)
        if inc:
            self.seq[eng] += 1
            s = self.seq[eng]
            for b in self.pend_w[eng]:
                self.bufw[b] = (eng, s)
                self.bufr.pop(b, None)
            for b in self.pend_r[eng]:
                self.bufr.setdefault(b, {})[eng] = s
            self.pend_r[eng] = []
            self.pend_w[eng] = []
        mult = 16 if eng in self.DMA else 1
        myeng = eng
        dmaset = self.DMA

        def go(p, sems):
            for de, ds in todo:
                p.wait_ge(sems[de], ds * (16 if de in dmaset else 1))
            inst = fn(p)
            if inc and inst is not None:
                inst.then_inc(sems[myeng], mult)

        self.ops[stream].append(go)

    def final_wait(self, eng, dep_stream):
        subs = [e for e in self.ENGS if e.startswith(dep_stream) and e != dep_stream] \
            if dep_stream in ("Q", "G") else [dep_stream]
        vals = [(de, self.seq[de] * (16 if de in self.DMA else 1)) for de in subs]

        def go(p, sems):
            for de, v in vals:
                if v:
                    p.wait_ge(sems[de], v)

        self.ops[eng].append(go)


def rope_tables_fm(T, base):
    i = np.arange(0, 128, 2, dtype=np.float32)
    inv = (np.float32(1.0) / (np.float32(base) ** (i / np.float32(128)))).astype(np.float32)
    t = np.arange(T, dtype=np.float32)
    f = t[None, :] * inv[:, None]
    cos = np.cos(f).astype(np.float32)
    sin = np.sin(f).astype(np.float32)
    return (np.ascontiguousarray(np.concatenate([cos, cos], axis=0)),
            np.ascontiguousarray(np.concatenate([sin, sin], axis=0)))


def host_constants(T, E):
    c = {}
    c["cosA"], c["sinA"] = rope_tables_fm(T, 1000000.0)
    c["cosB"], c["sinB"] = rope_tables_fm(T, 10000.0)
    S = np.zeros((128, 128), np.float32)
    for d in range(64):
        S[d + 64, d] = -1.0
        S[d, d + 64] = 1.0
    c["Smat"] = S
    c["ident"] = np.eye(128, dtype=np.float32)
    n_c = QW // KW
    kk = np.arange(KW)[:, None]
    qq = np.arange(QW)[None, :]
    masks = np.stack([(kk <= qq - KW * cc).astype(np.float32) for cc in range(n_c)])
    c["masks"] = np.ascontiguousarray(masks.transpose(1, 0, 2).reshape(KW, n_c * QW)).astype(bfdt)
    return c


def layout_weights(inp, T, E):
    NH = E // 128
    NEC = E // 128
    NG = np.asarray(inp["L0_attn_b_wk"]).shape[1] // 128
    w = {}

    def heads_img(A, nh, hw):
        return np.ascontiguousarray(
            A.reshape(NEC, 128, nh, hw).transpose(2, 1, 0, 3).reshape(nh, 128, NEC * hw))

    def fchunk_img(A):
        return np.ascontiguousarray(
            A.reshape(NH, 128, NEC, 128).transpose(2, 1, 0, 3).reshape(NEC, 128, NH * 128))

    wq_a = np.asarray(inp["L0_attn_a_wq"], np.float32)
    wk_a = np.asarray(inp["L0_attn_a_wk"], np.float32)
    wv_a = np.asarray(inp["L0_attn_a_wv"], np.float32)
    wo_a = np.asarray(inp["L0_attn_a_wo"], np.float32)
    w["wqa"] = heads_img(wq_a.T, NH, 128)
    w["wka"] = heads_img(wk_a.T, NH, 128)
    w["woa"] = fchunk_img(wo_a.T)
    w["wva"] = heads_img(wv_a.T, NH // 2, 256)

    wq_b = np.asarray(inp["L0_attn_b_wq"], np.float32)
    wk_b = np.asarray(inp["L0_attn_b_wk"], np.float32)
    wv_b = np.asarray(inp["L0_attn_b_wv"], np.float32)
    wo_b = np.asarray(inp["L0_attn_b_wo"], np.float32)
    w["wqb"] = heads_img(wq_b, NH, 128)
    w["wkb"] = heads_img(wk_b, NG, 128)
    nvp = max(1, NG // 2)
    vwb = wv_b.shape[1] // nvp
    w["wvb"] = heads_img(wv_b, nvp, vwb)
    w["wob"] = fchunk_img(wo_b)

    rw = np.asarray(inp["L0_ffn_router_w"], np.float32)
    dw = np.asarray(inp["L0_ffn_down_w"], np.float32)
    uw = np.asarray(inp["L0_ffn_up_w"], np.float32)
    D_all = dw.transpose(2, 0, 1).reshape(E, 128)
    C = np.concatenate([D_all, rw.T], axis=1)
    w["dr"] = np.ascontiguousarray(
        C.reshape(NEC, 128, 136).transpose(1, 0, 2).reshape(128, NEC * 136))
    w["u"] = np.ascontiguousarray(uw.transpose(0, 2, 1).reshape(128, E))

    w["bq"] = np.ascontiguousarray(np.asarray(inp["L0_attn_b_bq"], np.float32).reshape(NH, 128).T)
    w["bk"] = np.ascontiguousarray(np.asarray(inp["L0_attn_b_bk"], np.float32).reshape(NG, 128).T)
    w["bo"] = np.ascontiguousarray(np.asarray(inp["L0_attn_b_bo"], np.float32).reshape(NH, 128).T)
    w["bv"] = np.ascontiguousarray(np.asarray(inp["L0_attn_b_bv"], np.float32).reshape(1, NG * 128))
    w["qns"] = np.ascontiguousarray(np.asarray(inp["L0_attn_b_q_norm_scale"], np.float32).reshape(128, 1))
    w["kns"] = np.ascontiguousarray(np.asarray(inp["L0_attn_b_k_norm_scale"], np.float32).reshape(128, 1))
    return w


def build_program(T, E, NG):
    NH = E // 128
    NEC = E // 128
    NTC = T // 128
    NQC = T // QW
    n_c = QW // KW
    rep = NH // NG
    nvp = max(1, NG // 2)
    vwb = (NG * 128) // nvp
    scale_a = float(np.float32(1.0) / np.sqrt(np.float32(128)))
    scale_b = float((np.float64(128.0) ** -0.5) ** 2)

    nc = bass.Bass("TRN2", target_bir_lowering=False)

    xin = nc.dram_tensor("xin", [E, T], F32, kind="ExternalInput").ap()
    out = nc.dram_tensor("out", [E, T], F32, kind="ExternalOutput").ap()
    d_cosA = nc.dram_tensor("cosA", [128, T], F32, kind="ExternalInput").ap()
    d_sinA = nc.dram_tensor("sinA", [128, T], F32, kind="ExternalInput").ap()
    d_cosB = nc.dram_tensor("cosB", [128, T], F32, kind="ExternalInput").ap()
    d_sinB = nc.dram_tensor("sinB", [128, T], F32, kind="ExternalInput").ap()
    d_S = nc.dram_tensor("Smat", [128, 128], F32, kind="ExternalInput").ap()
    d_ident = nc.dram_tensor("ident", [128, 128], F32, kind="ExternalInput").ap()
    d_masks = nc.dram_tensor("masks", [KW, n_c * QW], BF16, kind="ExternalInput").ap()
    d_wqa = nc.dram_tensor("wqa", [NH, 128, NEC * 128], F32, kind="ExternalInput").ap()
    d_wka = nc.dram_tensor("wka", [NH, 128, NEC * 128], F32, kind="ExternalInput").ap()
    d_wva = nc.dram_tensor("wva", [NH // 2, 128, NEC * 256], F32, kind="ExternalInput").ap()
    d_woa = nc.dram_tensor("woa", [NEC, 128, NH * 128], F32, kind="ExternalInput").ap()
    d_wqb = nc.dram_tensor("wqb", [NH, 128, NEC * 128], F32, kind="ExternalInput").ap()
    d_wkb = nc.dram_tensor("wkb", [NG, 128, NEC * 128], F32, kind="ExternalInput").ap()
    d_wvb = nc.dram_tensor("wvb", [nvp, 128, NEC * vwb], F32, kind="ExternalInput").ap()
    d_wob = nc.dram_tensor("wob", [NEC, 128, NH * 128], F32, kind="ExternalInput").ap()
    d_dr = nc.dram_tensor("dr", [128, NEC * 136], F32, kind="ExternalInput").ap()
    d_u = nc.dram_tensor("u", [128, E], F32, kind="ExternalInput").ap()
    d_bq = nc.dram_tensor("bq", [128, NH], F32, kind="ExternalInput").ap()
    d_bk = nc.dram_tensor("bk", [128, NG], F32, kind="ExternalInput").ap()
    d_bo = nc.dram_tensor("bo", [128, NH], F32, kind="ExternalInput").ap()
    d_bv = nc.dram_tensor("bv", [1, NG * 128], F32, kind="ExternalInput").ap()
    d_qns = nc.dram_tensor("qns", [128, 1], F32, kind="ExternalInput").ap()
    d_kns = nc.dram_tensor("kns", [128, 1], F32, kind="ExternalInput").ap()
    osp = nc.dram_tensor("osp", [NH * 128, T], BF16, kind="Internal").ap()

    A = nc.alloc_sbuf_tensor
    x_t = [A(f"x{i}", [128, T], BF16) for i in range(NEC)]
    y_t = [A(f"y{i}", [128, T], BF16) for i in range(NEC)]
    WSW = max(NEC * 256, NEC * 136 + E, NEC * vwb, NH * 128)
    ws = [A("ws0", [128, WSW], BF16), A("ws1", [128, WSW], BF16)]
    cos_s = A("cos_s", [128, T], BF16)
    sin_s = A("sin_s", [128, T], BF16)
    q_sb = A("q_sb", [128, T], BF16)
    k_sb = A("k_sb", [128, T], BF16)
    tmp1 = A("tmp1", [128, T], BF16)
    v_tm = [A(f"vtm{i}", [128, 256], BF16) for i in range(NTC)]
    expT = [A("expT0", [128, 2 * QW], BF16), A("expT1", [128, 2 * QW], BF16)]
    recipZ = A("recipZ", [128, QW], BF16)
    outn = A("outn", [128, QW], BF16)
    mB = A("mB", [128, T], BF16)
    rstdB = A("rstdB", [128, T], BF16)
    m2s = A("m2s", [128, QW], BF16)
    vars_ = A("vars", [128, QW], F32)
    sqs = [A("sqs0", [128, QW], BF16), A("sqs1", [128, QW], BF16)]
    stage = [A("stage0", [128, QW], F32), A("stage1", [128, QW], F32)]
    agT = A("agT", [128, T], BF16)
    a_sb = [A("a_sb0", [128, 128], BF16), A("a_sb1", [128, 128], BF16)]
    ag_sb = [A("ag_sb0", [128, 128], F32), A("ag_sb1", [128, 128], F32)]
    w8 = A("w8", [128, 8], F32)
    g8 = A("g8", [128, 16], F32)
    smalls = A("smalls", [128, 16], F32)
    ident_s = A("ident_s", [128, 128], F32)
    allones = A("allones", [128, 128], BF16)
    Smat_s = A("Smat_s", [128, 128], BF16)
    masks_s = A("masks_s", [128, n_c * QW], BF16)
    bq_s = A("bq_s", [128, NH], F32)
    bk_s = A("bk_s", [128, NG], F32)
    bo_s = A("bo_s", [128, NH], F32)
    bv_pad = A("bv_pad", [128, vwb], BF16)
    qns_s = A("qns_s", [128, 1], F32)
    kns_s = A("kns_s", [128, 1], F32)
    ebias = A("ebias", [128, 1], F32)
    eps5 = A("eps5", [128, 1], F32)
    eps6 = A("eps6", [128, 1], F32)

    PT = [nc.alloc_psum_tensor(f"PT{i}", [128, 1024], F32) for i in range(4)]

    def bank(b, n=512):
        return PT[b // 2][:, (b % 2) * 512:(b % 2) * 512 + n]

    PS = [bank(i) for i in range(8)]

    P = Prog()

    def dma(eng, dst_ap, src_ap, reads, writes):
        P.op(eng, lambda p, d=dst_ap, s=src_ap: p.dma_start(out=d, in_=s),
             reads=reads, writes=writes)

    def mmN(bi, n, lhsT, rhs, reads, start, stop, inc=True):
        P.op("T", lambda p, l=lhsT, r=rhs, st=start, sp=stop, b=bi, nn=n:
             p.matmul(bank(b, nn), l, r, start=st, stop=sp),
             reads=reads, writes=[f"P{bi}"], inc=inc)

    def mm2(bi, lhsT, rhs, reads, start, stop, inc=True):
        mmN(bi, QW, lhsT, rhs, reads, start, stop, inc)

    P.op("V", lambda p: p.memset(allones[:], 1.0), writes=["allones"])
    P.op("V", lambda p: p.memset(eps5[:], 1e-5), writes=["eps5"])
    P.op("V", lambda p: p.memset(eps6[:], 1e-6), writes=["eps6"])
    P.op("V", lambda p: p.memset(bv_pad[:], 0.0), writes=["bv_pad"])
    dma("Q", ident_s[:], d_ident[:, :], [], ["ident_s"])
    dma("Q", masks_s[:], d_masks[:, :], [], ["masks_s"])
    dma("G", Smat_s[:], d_S[:, :], [], ["Smat_s"])
    dma("G", cos_s[:], d_cosA[:, :], [], ["cos_s"])
    dma("G", sin_s[:], d_sinA[:, :], [], ["sin_s"])
    dma("G", bq_s[:], d_bq[:, :], [], ["bq_s"])
    dma("G", bk_s[:], d_bk[:, :], [], ["bk_s"])
    dma("G", bo_s[:], d_bo[:, :], [], ["bo_s"])
    dma("G", qns_s[:], d_qns[:, :], [], ["qns_s"])
    dma("G", kns_s[:], d_kns[:, :], [], ["kns_s"])
    for i in range(NEC):
        dma("G", x_t[i][:], xin[i * 128:(i + 1) * 128, :], [], [f"x{i}"])

    phases = []

    def add_phase(dmas):
        phases.append(dmas)
        return len(phases) - 1

    emitted = set()

    def emit_phase(idx):
        if idx in emitted or idx >= len(phases):
            return
        emitted.add(idx)
        slot = idx % 2
        name = f"ws{slot}"
        for (off, width, src) in phases[idx]:
            dma("G", ws[slot][:, off:off + width], src, [name], [name])

    def slot_of(idx):
        return idx % 2, f"ws{idx % 2}"

    def layer_norm():
        steps = [(ec, tc) for ec in range(NEC) for tc in range(NQC)]

        def square(i):
            ec, tc = steps[i]
            sq, sqn = sqs[i % 2], f"sqs{i % 2}"
            sl = slice(tc * QW, (tc + 1) * QW)
            P.op("V", lambda p, e=ec, s=sl, q=sq: p.tensor_tensor(
                q[:], x_t[e][:, s], x_t[e][:, s], ALU.mult),
                reads=[f"x{ec}"], writes=[sqn])

        square(0)
        for i, (ec, tc) in enumerate(steps):
            if i + 1 < len(steps):
                square(i + 1)
            sqn = f"sqs{i % 2}"
            sl = slice(tc * QW, (tc + 1) * QW)
            mm2(tc, allones[:], x_t[ec][:, sl], ["allones", f"x{ec}"],
                start=(ec == 0), stop=(ec == NEC - 1))
            mm2(4 + tc, allones[:], sqs[i % 2][:], ["allones", sqn],
                start=(ec == 0), stop=(ec == NEC - 1))
        for tc in range(NQC):
            sl = slice(tc * QW, (tc + 1) * QW)
            P.op("S", lambda p, t=tc, s=sl: p.activation(
                mB[:, s], PS[t][:, 0:QW], AF.Identity, scale=1.0 / E),
                reads=[f"P{tc}"], writes=["mB"])
            P.op("S", lambda p, s=sl: p.activation(m2s[:], mB[:, s], AF.Square),
                 reads=["mB"], writes=["m2s"])
            P.op("V", lambda p, t=tc: p.scalar_tensor_tensor(
                vars_[:], PS[4 + t][:, 0:QW], 1.0 / E, m2s[:], ALU.mult, ALU.subtract),
                reads=[f"P{4 + tc}", "m2s"], writes=["vars"])
            P.op("S", lambda p: p.activation(vars_[:], vars_[:], AF.Ln, bias=eps5[:]),
                 reads=["vars", "eps5"], writes=["vars"])
            P.op("S", lambda p, s=sl: p.activation(rstdB[:, s], vars_[:], AF.Exp, scale=-0.5),
                 reads=["vars"], writes=["rstdB"])
            P.op("V", lambda p, s=sl: p.tensor_tensor(mB[:, s], mB[:, s], rstdB[:, s], ALU.mult),
                 reads=["rstdB"], writes=["mB"])
        for ec in range(NEC):
            P.op("V", lambda p, e=ec: p.tensor_tensor(y_t[e][:], x_t[e][:], rstdB[:], ALU.mult),
                 reads=[f"x{ec}", "rstdB"], writes=[f"y{ec}"])
            P.op("V", lambda p, e=ec: p.tensor_tensor(y_t[e][:], y_t[e][:], mB[:], ALU.subtract),
                 reads=["mB"], writes=[f"y{ec}"])

    def proj(dst, dstname, sname, slot_ap, woff, bias_ap, bias_name, after_tc=None):
        for tc in range(NQC):
            b = 6 + (tc % 2)
            for ec in range(NEC):
                lhs = slot_ap[:, woff + ec * 128: woff + (ec + 1) * 128]
                mm2(b, lhs, y_t[ec][:, tc * QW:(tc + 1) * QW],
                    [sname, f"y{ec}"], start=(ec == 0), stop=(ec == NEC - 1),
                    inc=(ec == NEC - 1))
            if bias_ap is None:
                P.op("S", lambda p, bb=b, t=tc: p.activation(
                    dst[:, t * QW:(t + 1) * QW], PS[bb][:, 0:QW], AF.Identity),
                    reads=[f"P{b}"], writes=[f"{dstname}.{tc}"])
            else:
                P.op("S", lambda p, bb=b, t=tc, ba=bias_ap: p.activation(
                    dst[:, t * QW:(t + 1) * QW], PS[bb][:, 0:QW], AF.Identity, bias=ba),
                    reads=[f"P{b}", bias_name], writes=[f"{dstname}.{tc}"])
            if after_tc is not None and tc > 1:
                after_tc(tc - 2)
        if after_tc is not None:
            for _t in range(max(0, NQC - 2), NQC):
                after_tc(_t)

    def rope_tc(buf, bufname, tc, bank0=4):
        if True:
            b = bank0 + (tc % 2)
            sl = slice(tc * QW, (tc + 1) * QW)
            nm = f"{bufname}.{tc}"
            mm2(b, Smat_s[:], buf[:, sl], ["Smat_s", nm], start=True, stop=True)
            P.op("V", lambda p, bb=b, s=sl: p.tensor_tensor(
                tmp1[:, s], PS[bb][:, 0:QW], sin_s[:, s], ALU.mult),
                reads=[f"P{b}", "sin_s"], writes=[f"tmp1.{tc}"])
            P.op("V", lambda p, s=sl: p.tensor_tensor(
                buf[:, s], buf[:, s], cos_s[:, s], ALU.mult),
                reads=["cos_s", nm], writes=[nm])
            P.op("V", lambda p, s=sl: p.tensor_tensor(
                buf[:, s], buf[:, s], tmp1[:, s], ALU.add),
                reads=[f"tmp1.{tc}"], writes=[nm])

    def rope(buf, bufname, bank0=4):
        for tc in range(NQC):
            rope_tc(buf, bufname, tc, bank0)

    def rms_tc(buf, bufname, tc, scale_ap, scale_name):
        sq, sqn = sqs[tc % 2], f"sqs{tc % 2}"
        b = 4 + (tc % 2)
        sl = slice(tc * QW, (tc + 1) * QW)
        nm = f"{bufname}.{tc}"
        P.op("V", lambda p, s=sl, q=sq: p.tensor_tensor(
            q[:], buf[:, s], buf[:, s], ALU.mult),
            reads=[nm], writes=[sqn])
        mm2(b, allones[:], sq[:], ["allones", sqn], start=True, stop=True)
        P.op("S", lambda p, bb=b: p.activation(vars_[:], PS[bb][:, 0:QW], AF.Ln,
                                               scale=1.0 / 128.0, bias=eps6[:]),
             reads=[f"P{b}", "eps6"], writes=["vars"])
        P.op("S", lambda p, s=sl: p.activation(
            rstdB[:, s], vars_[:], AF.Exp, scale=-0.5),
            reads=["vars"], writes=["rstdB"])
        P.op("V", lambda p, s=sl: p.tensor_tensor(buf[:, s], buf[:, s], rstdB[:, s], ALU.mult),
             reads=["rstdB", nm], writes=[nm])
        P.op("V", lambda p, sa=scale_ap, s=sl: p.tensor_scalar(
            buf[:, s], buf[:, s], sa, None, ALU.mult),
            reads=[scale_name, nm], writes=[nm])

    def norm_tc(buf, bufname, tc, scol, bank=4):
        b = bank + (tc % 2)
        sq, sqn = sqs[tc % 2], f"sqs{tc % 2}"
        sl = slice(tc * QW, (tc + 1) * QW)
        P.op("V", lambda p, s=sl, q=sq: p.tensor_tensor(
            q[:], buf[:, s], buf[:, s], ALU.mult),
            reads=[f"{bufname}.{tc}"], writes=[sqn])
        mm2(b, allones[:], sq[:], ["allones", sqn], start=True, stop=True)
        P.op("V", lambda p, bb=b, c=scol: p.reduce_max(
            smalls[:, c:c + 1], PS[bb][:, 0:QW], axis=AX.X),
            reads=[f"P{b}"], writes=["smalls"])

    def combine_norms(outcol, base):
        P.op("V", lambda p, o=outcol, ba=base: p.reduce_max(
            smalls[:, o:o + 1], smalls[:, ba:ba + NQC], axis=AX.X),
            reads=["smalls"], writes=["smalls"])

    def exp_bias(e_scale):
        P.op("V", lambda p: p.tensor_tensor(smalls[:, 2:3], smalls[:, 0:1],
                                            smalls[:, 1:2], ALU.mult),
             reads=["smalls"], writes=["smalls"])
        P.op("S", lambda p: p.activation(smalls[:, 3:4], smalls[:, 2:3], AF.Ln),
             reads=["smalls"], writes=["smalls"])
        P.op("S", lambda p: p.activation(smalls[:, 3:4], smalls[:, 3:4], AF.Exp, scale=0.5),
             reads=["smalls"], writes=["smalls"])
        P.op("V", lambda p, es=e_scale: p.tensor_scalar(
            smalls[:, 4:5], smalls[:, 3:4], -es, 40.0, ALU.mult, ALU.add),
            reads=["smalls"], writes=["smalls"])
        P.op("V", lambda p: p.tensor_scalar(ebias[:], smalls[:, 4:5], 0.0, None, ALU.min),
             reads=["smalls"], writes=["ebias"])

    def attn_head(h, vslice, e_scale):
        for qc in range(NQC):
            qsl = slice(qc * QW, (qc + 1) * QW)
            nkc = (qc + 1) * n_c
            npair = nkc // 2

            def sc_exp(j):
                par = j % 2
                for half in range(2):
                    kc = 2 * j + half
                    mm2(2 * par + half, k_sb[:, kc * KW:(kc + 1) * KW], q_sb[:, qsl],
                        [f"k_sb.{kc // n_c}", f"q_sb.{qc}"], start=True, stop=True)
                P.op("S", lambda p, pa=par: p.activation(
                    expT[pa][:], PT[pa][:, 0:2 * QW], AF.Exp, bias=ebias[:], scale=e_scale),
                    reads=[f"P{2 * par}", f"P{2 * par + 1}", "ebias"], writes=[f"expT{par}"])
                for half in range(2):
                    kc = 2 * j + half
                    cc = kc - qc * n_c
                    if cc >= 0:
                        P.op("V", lambda p, pa=par, c=cc, hf=half: p.tensor_tensor(
                            expT[pa][:, hf * QW:(hf + 1) * QW],
                            expT[pa][:, hf * QW:(hf + 1) * QW],
                            masks_s[:, c * QW:(c + 1) * QW], ALU.mult),
                            reads=["masks_s"], writes=[f"expT{par}"])

            def zav(j):
                par = j % 2
                for half in range(2):
                    kc = 2 * j + half
                    e_ap = expT[par][:, half * QW:(half + 1) * QW]
                    mm2(4, allones[:], e_ap, ["allones", f"expT{par}"],
                        start=(kc == 0), stop=(kc == nkc - 1))
                    mm2(5, v_tm[kc][:, vslice], e_ap,
                        [f"vtm{kc}", f"expT{par}"],
                        start=(kc == 0), stop=(kc == nkc - 1))

            sc_exp(0)
            for j in range(1, npair):
                sc_exp(j)
                zav(j - 1)
            zav(npair - 1)
            P.op("S", lambda p: p.activation(vars_[:], PS[4][:, 0:QW], AF.Ln),
                 reads=["P4"], writes=["vars"])
            P.op("S", lambda p: p.activation(recipZ[:], vars_[:], AF.Exp, scale=-1.0),
                 reads=["vars"], writes=["recipZ"])
            P.op("V", lambda p: p.tensor_tensor(outn[:], PS[5][:, 0:QW], recipZ[:], ALU.mult),
                 reads=["P5", "recipZ"], writes=["outn"])
            dma("Q", osp[h * 128:(h + 1) * 128, qc * QW:(qc + 1) * QW], outn[:],
                ["outn"], [f"osp{h}"])

    def v_proj(sname, slot_ap, vwidth, bvslice=None):
        for tg in range((NTC + 3) // 4):
            tchs = list(range(tg * 4, min((tg + 1) * 4, NTC)))
            for ec in range(NEC):
                for i, t in enumerate(tchs):
                    bank = 4 + i
                    last = (ec == NEC - 1) and bvslice is None
                    mmN(bank, vwidth, y_t[ec][:, t * 128:(t + 1) * 128],
                        slot_ap[:, ec * vwidth:(ec + 1) * vwidth],
                        [f"y{ec}", sname], start=(ec == 0), stop=last,
                        inc=(ec == NEC - 1))
            if bvslice is not None:
                for i, t in enumerate(tchs):
                    mmN(4 + i, vwidth, allones[:], bv_pad[:, 0:vwidth],
                        ["allones", "bv_pad"], start=False, stop=True)
            for i, t in enumerate(tchs):
                bank = 4 + i
                P.op("V", lambda p, b=bank, tt=t, vw=vwidth: p.tensor_copy(
                    v_tm[tt][:, 0:vw], PS[b][:, 0:vw]),
                    reads=[f"P{bank}"], writes=[f"vtm{t}"])

    def wo_phase(phase_base, with_bo):
        for h in range(NH):
            dma("Q", y_t[h][:], osp[h * 128:(h + 1) * 128, :], [f"osp{h}"], [f"y{h}"])
        emit_phase(phase_base)
        for fc in range(NEC):
            if fc + 1 < NEC:
                emit_phase(phase_base + fc + 1)
            slot, sname = slot_of(phase_base + fc)
            for tc in range(NQC):
                b = tc % 2
                for h in range(NH):
                    mm2(b, ws[slot][:, h * 128:(h + 1) * 128],
                        y_t[h][:, tc * QW:(tc + 1) * QW],
                        [sname, f"y{h}"], start=(h == 0), stop=(h == NH - 1),
                        inc=(h == NH - 1))
                if with_bo:
                    P.op("V", lambda p, bb=b, f=fc, t=tc: p.scalar_tensor_tensor(
                        x_t[f][:, t * QW:(t + 1) * QW], PS[bb][:, 0:QW],
                        bo_s[:, f:f + 1], x_t[f][:, t * QW:(t + 1) * QW],
                        ALU.add, ALU.add),
                        reads=[f"P{b}", "bo_s", f"x{fc}"], writes=[f"x{fc}"])
                else:
                    P.op("V", lambda p, bb=b, f=fc, t=tc: p.tensor_tensor(
                        x_t[f][:, t * QW:(t + 1) * QW],
                        x_t[f][:, t * QW:(t + 1) * QW], PS[bb][:, 0:QW], ALU.add),
                        reads=[f"P{b}", f"x{fc}"], writes=[f"x{fc}"])

    def ffn(phase_idx, last=False):
        layer_norm()
        emit_phase(phase_idx)
        slot, sname = slot_of(phase_idx)
        droff = 0
        uoff = NEC * 136
        def gate_chain(t):
            b = t % 2
            par = t % 2
            P.op("V", lambda p, bb=b: p.reduce_max(
                smalls[:, 0:1], PS[bb][:, 128:136], axis=AX.X),
                reads=[f"P{b}"], writes=["smalls"])
            P.op("V", lambda p: p.tensor_scalar(
                smalls[:, 1:2], smalls[:, 0:1], -1.0, None, ALU.mult),
                reads=["smalls"], writes=["smalls"])
            P.op("S", lambda p, bb=b: p.activation(
                w8[:], PS[bb][:, 128:136], AF.Exp, bias=smalls[:, 1:2]),
                reads=[f"P{b}", "smalls"], writes=["w8"])
            P.op("V", lambda p: p.reduce_sum(smalls[:, 2:3], w8[:], axis=AX.X),
                 reads=["w8"], writes=["smalls"])
            P.op("V", lambda p: p.reciprocal(smalls[:, 3:4], smalls[:, 2:3]),
                 reads=["smalls"], writes=["smalls"])
            P.op("V", lambda p: p.tensor_scalar(
                g8[:, 0:8], w8[:], 1.0, -1e9, ALU.is_ge, ALU.mult),
                reads=["w8"], writes=["g8"])
            P.op("V", lambda p: p.tensor_tensor(g8[:, 8:16], w8[:], g8[:, 0:8], ALU.add),
                 reads=["w8"], writes=["g8"])
            P.op("V", lambda p: p.reduce_max(smalls[:, 4:5], g8[:, 8:16], axis=AX.X),
                 reads=["g8"], writes=["smalls"])
            P.op("V", lambda p: p.tensor_scalar(
                g8[:, 0:8], w8[:], smalls[:, 4:5], None, ALU.is_ge),
                reads=["w8", "smalls"], writes=["g8"])
            P.op("V", lambda p: p.tensor_tensor(w8[:], w8[:], g8[:, 0:8], ALU.mult),
                 reads=["g8"], writes=["w8"])
            P.op("V", lambda p: p.tensor_scalar(
                w8[:], w8[:], smalls[:, 3:4], None, ALU.mult),
                reads=["smalls"], writes=["w8"])
            P.op("S", lambda p, bb=b: p.activation(
                vars_[:, 0:128], PS[bb][:, 0:128], AF.Exp, scale=-1.0),
                reads=[f"P{b}"], writes=["vars"])
            P.op("V", lambda p: p.tensor_scalar(vars_[:, 0:128], vars_[:, 0:128], 1.0, None, ALU.add),
                 reads=["vars"], writes=["vars"])
            P.op("V", lambda p: p.reciprocal(vars_[:, 128:256], vars_[:, 0:128]),
                 reads=["vars"], writes=["vars"])
            P.op("V", lambda p, bb=b, pa=par: p.tensor_tensor(
                a_sb[pa][:], PS[bb][:, 0:128], vars_[:, 128:256], ALU.mult),
                reads=[f"P{b}", "vars"], writes=[f"a_sb{par}"])
            for ex in range(8):
                P.op("V", lambda p, e=ex, pa=par: p.tensor_scalar(
                    ag_sb[pa][:, e * 16:(e + 1) * 16], a_sb[pa][:, e * 16:(e + 1) * 16],
                    w8[:, e:e + 1], None, ALU.mult),
                    reads=[f"a_sb{par}", "w8"], writes=[f"ag_sb{par}"], inc=(ex == 7))

        def transp(t):
            par = t % 2
            tb = 2 + (t % 2)
            P.op("T", lambda p, bb=tb, pa=par: p.transpose(
                PS[bb][:, 0:128], ag_sb[pa][:], ident_s[:]),
                reads=[f"ag_sb{par}", "ident_s"], writes=[f"P{tb}"])
            P.op("S", lambda p, bb=tb, tt=t: p.activation(
                agT[:, tt * 128:(tt + 1) * 128], PS[bb][:, 0:128], AF.Identity),
                reads=[f"P{tb}"], writes=["agT"])

        for t in range(NTC):
            b = t % 2
            for ec in range(NEC):
                mmN(b, 136, y_t[ec][:, t * 128:(t + 1) * 128],
                    ws[slot][:, droff + ec * 136: droff + (ec + 1) * 136],
                    [f"y{ec}", sname], start=(ec == 0), stop=(ec == NEC - 1),
                    inc=(ec == NEC - 1))
            if t > 0:
                transp(t - 1)
            gate_chain(t)
        transp(NTC - 1)
        for fc in range(NEC):
            for tc in range(NQC):
                b = 4 + ((fc * NQC + tc) % 2)
                mm2(b, ws[slot][:, uoff + fc * 128: uoff + (fc + 1) * 128],
                    agT[:, tc * QW:(tc + 1) * QW], [sname, "agT"],
                    start=True, stop=True)
                if last:
                    par = (fc * NQC + tc) % 2
                    P.op("V", lambda p, bb=b, f=fc, t=tc, pa=par: p.tensor_tensor(
                        stage[pa][:], x_t[f][:, t * QW:(t + 1) * QW],
                        PS[bb][:, 0:QW], ALU.add),
                        reads=[f"P{b}", f"x{fc}"], writes=[f"stage{par}"])
                    dma("Q", out[fc * 128:(fc + 1) * 128, tc * QW:(tc + 1) * QW],
                        stage[par][:], [f"stage{par}"], ["outhbm"])
                else:
                    P.op("V", lambda p, bb=b, f=fc, t=tc: p.tensor_tensor(
                        x_t[f][:, t * QW:(t + 1) * QW],
                        x_t[f][:, t * QW:(t + 1) * QW], PS[bb][:, 0:QW], ALU.add),
                        reads=[f"P{b}", f"x{fc}"], writes=[f"x{fc}"])

    gpv = NG // nvp
    ph_wva, ph_wqk = {}, {}
    for pair in range(NH // 2):
        ph_wva[pair] = add_phase([(0, NEC * 256, d_wva[pair, :, :])])
        for hh in range(2):
            h = pair * 2 + hh
            ph_wqk[h] = add_phase([(0, NEC * 128, d_wqa[h, :, :]),
                                   (NEC * 128, NEC * 128, d_wka[h, :, :])])
    ph_woa = [add_phase([(0, NH * 128, d_woa[fc, :, :])]) for fc in range(NEC)]
    PH_FFN1 = add_phase([(0, NEC * 136, d_dr[:, :]), (NEC * 136, E, d_u[:, :])])
    ph_wvb, ph_wkb, ph_wqb = {}, {}, {}
    for vp in range(nvp):
        ph_wvb[vp] = add_phase([(0, NEC * vwb, d_wvb[vp, :, :])])
        for gg in range(gpv):
            g = vp * gpv + gg
            ph_wkb[g] = add_phase([(0, NEC * 128, d_wkb[g, :, :])])
            for r in range(rep):
                h = g * rep + r
                ph_wqb[h] = add_phase([(0, NEC * 128, d_wqb[h, :, :])])
    ph_wob = [add_phase([(0, NH * 128, d_wob[fc, :, :])]) for fc in range(NEC)]
    PH_FFN2 = add_phase([(0, NEC * 136, d_dr[:, :]), (NEC * 136, E, d_u[:, :])])

    layer_norm()
    emit_phase(0)
    for pair in range(NH // 2):
        emit_phase(ph_wva[pair])
        emit_phase(ph_wva[pair] + 1)
        vslot, vsname = slot_of(ph_wva[pair])
        v_proj(vsname, ws[vslot], 256)
        for hh in range(2):
            h = pair * 2 + hh
            phq = ph_wqk[h]
            emit_phase(phq)
            emit_phase(phq + 1)
            qslot, qsname = slot_of(phq)
            proj(q_sb, "q_sb", qsname, ws[qslot], 0, None, None)

            def chains_a(tc):
                rope_tc(q_sb, "q_sb", tc)
                norm_tc(q_sb, "q_sb", tc, 12 + tc)
            proj(k_sb, "k_sb", qsname, ws[qslot], NEC * 128, None, None,
                 after_tc=chains_a)
            for tc in range(NQC):
                rope_tc(k_sb, "k_sb", tc)
                norm_tc(k_sb, "k_sb", tc, 8 + tc)
            combine_norms(1, 8)
            combine_norms(0, 12)
            exp_bias(scale_a)
            attn_head(h, slice(hh * 128, (hh + 1) * 128), scale_a)
    wo_phase(ph_woa[0], with_bo=False)

    ffn(PH_FFN1)

    layer_norm()
    dma("G", cos_s[:], d_cosB[:, :], ["cos_s"], ["cos_s"])
    dma("G", sin_s[:], d_sinB[:, :], ["sin_s"], ["sin_s"])
    emit_phase(ph_wvb[0])
    for vp in range(nvp):
        emit_phase(ph_wvb[vp])
        emit_phase(ph_wvb[vp] + 1)
        vslot, vsname = slot_of(ph_wvb[vp])
        dma("G", bv_pad[0:1, :], d_bv[:, vp * vwb:(vp + 1) * vwb], ["bv_pad"], ["bv_pad"])
        v_proj(vsname, ws[vslot], vwb, bvslice=slice(vp * vwb, (vp + 1) * vwb))
        for gg in range(gpv):
            g = vp * gpv + gg
            phk = ph_wkb[g]
            emit_phase(phk)
            emit_phase(phk + 1)
            kslot, ksname = slot_of(phk)
            def kchain_b(tc):
                rms_tc(k_sb, "k_sb", tc, kns_s[:], "kns_s")
                rope_tc(k_sb, "k_sb", tc)
                norm_tc(k_sb, "k_sb", tc, 8 + tc)
            proj(k_sb, "k_sb", ksname, ws[kslot], 0, bk_s[:, g:g + 1], "bk_s",
                 after_tc=kchain_b)
            combine_norms(1, 8)
            for r in range(rep):
                h = g * rep + r
                phq = ph_wqb[h]
                emit_phase(phq)
                emit_phase(phq + 1)
                qslot, qsname = slot_of(phq)
                def qchain_b(tc):
                    rms_tc(q_sb, "q_sb", tc, qns_s[:], "qns_s")
                    rope_tc(q_sb, "q_sb", tc)
                    norm_tc(q_sb, "q_sb", tc, 12 + tc)
                proj(q_sb, "q_sb", qsname, ws[qslot], 0, bq_s[:, h:h + 1], "bq_s",
                     after_tc=qchain_b)
                combine_norms(0, 12)
                exp_bias(scale_b)
                attn_head(h, slice(gg * 128, (gg + 1) * 128), scale_b)
    wo_phase(ph_wob[0], with_bo=True)

    ffn(PH_FFN2, last=True)

    P.final_wait("Q", "Q")
    P.final_wait("G", "G")
    assert len(emitted) == len(phases), (len(emitted), len(phases))

    import contextlib
    with contextlib.ExitStack() as stack:
        sems = {e: stack.enter_context(nc.semaphore(f"sem{e}")) for e in Prog.ENGS}
        block = stack.enter_context(nc.Block())

        @block.tensor
        def _(t):
            for f in P.ops["T"]:
                f(t, sems)

        @block.scalar
        def _(s):
            for f in P.ops["S"]:
                f(s, sems)

        @block.vector
        def _(v):
            for f in P.ops["V"]:
                f(v, sems)

        @block.sync
        def _(q):
            for f in P.ops["Q"]:
                f(q, sems)

        @block.gpsimd
        def _(g):
            for f in P.ops["G"]:
                f(g, sems)

    return nc


def make_in_maps(inputs):
    x = np.asarray(inputs["L0_attn_a_x"], np.float32)
    B, T, E = x.shape
    NG = np.asarray(inputs["L0_attn_b_wk"]).shape[1] // 128
    consts = host_constants(T, E)
    w = layout_weights(inputs, T, E)
    shared = {
        "cosA": consts["cosA"], "sinA": consts["sinA"],
        "cosB": consts["cosB"], "sinB": consts["sinB"],
        "Smat": consts["Smat"], "ident": consts["ident"], "masks": consts["masks"],
        "wqa": w["wqa"], "wka": w["wka"], "wva": w["wva"], "woa": w["woa"],
        "wqb": w["wqb"], "wkb": w["wkb"], "wvb": w["wvb"], "wob": w["wob"],
        "dr": w["dr"], "u": w["u"],
        "bq": w["bq"], "bk": w["bk"], "bo": w["bo"], "bv": w["bv"],
        "qns": w["qns"], "kns": w["kns"],
    }
    in_maps = []
    for b in range(B):
        m = dict(shared)
        m["xin"] = np.ascontiguousarray(x[b].T)
        in_maps.append(m)
    return in_maps, T, E, NG


def kernel(**inputs):
    x = np.asarray(inputs["L0_attn_a_x"], np.float32)
    B, T, E = x.shape
    assert B == N_CORES, (B, N_CORES)
    in_maps, T, E, NG = make_in_maps(inputs)
    nc = build_program(T, E, NG)
    res = run_bass_kernel_spmd(nc, in_maps, core_ids=list(range(N_CORES)))
    outs = [np.ascontiguousarray(res.results[b]["out"].T) for b in range(B)]
    return np.stack(outs).astype(np.float32)


# version 16
# speedup vs baseline: 1.2176x; 1.0720x over previous
import numpy as np
import ml_dtypes

import concourse.bass as bass
import concourse.mybir as mybir
from concourse.bass_utils import run_bass_kernel_spmd

F32 = mybir.dt.float32
BF16 = mybir.dt.bfloat16
AF = mybir.ActivationFunctionType
ALU = mybir.AluOpType
AX = mybir.AxisListType
bfdt = ml_dtypes.bfloat16

N_CORES = 8
QW = 512
KW = 128


class Prog:

    NQ, NG_ = 4, 8
    ENGS = (["T", "S", "V"] + [f"Q{i}" for i in range(NQ)]
            + [f"G{i}" for i in range(NG_)])
    DMA = {e for e in ENGS if e[0] in "QG"}

    def __init__(self):
        self.ops = {"T": [], "S": [], "V": [], "Q": [], "G": []}
        self.seq = {e: 0 for e in self.ENGS}
        self.pend_r = {e: [] for e in self.ENGS}
        self.pend_w = {e: [] for e in self.ENGS}
        self.bufw = {}
        self.bufr = {}
        self.waited = {e: {e2: 0 for e2 in self.ENGS} for e in self.ENGS}
        self.rr = {"Q": 0, "G": 0}
        self.n_waits = 0

    def op(self, eng, fn, reads=(), writes=(), inc=True):
        stream = eng
        if eng == "Q":
            eng = f"Q{self.rr['Q'] % self.NQ}"
            self.rr["Q"] += 1
        elif eng == "G":
            eng = f"G{self.rr['G'] % self.NG_}"
            self.rr["G"] += 1
        waits = {}

        def need(dep_eng, dep_seq, buf):
            if dep_eng == eng == "T" and buf.startswith("P"):
                return
            waits[dep_eng] = max(waits.get(dep_eng, 0), dep_seq)

        if eng in self.DMA and self.seq[eng] > 0:
            waits[eng] = self.seq[eng]
        for b in reads:
            w = self.bufw.get(b)
            if w:
                need(w[0], w[1], b)
        for b in writes:
            w = self.bufw.get(b)
            if w:
                need(w[0], w[1], b)
            for re, rs in self.bufr.get(b, {}).items():
                need(re, rs, b)
        todo = []
        for de, ds in waits.items():
            if ds > self.waited[eng][de]:
                self.waited[eng][de] = ds
                todo.append((de, ds))
        self.n_waits += len(todo)
        self.pend_r[eng].extend(reads)
        self.pend_w[eng].extend(writes)
        if inc:
            self.seq[eng] += 1
            s = self.seq[eng]
            for b in self.pend_w[eng]:
                self.bufw[b] = (eng, s)
                self.bufr.pop(b, None)
            for b in self.pend_r[eng]:
                self.bufr.setdefault(b, {})[eng] = s
            self.pend_r[eng] = []
            self.pend_w[eng] = []
        mult = 16 if eng in self.DMA else 1
        myeng = eng
        dmaset = self.DMA

        def go(p, sems):
            for de, ds in todo:
                p.wait_ge(sems[de], ds * (16 if de in dmaset else 1))
            inst = fn(p)
            if inc and inst is not None:
                inst.then_inc(sems[myeng], mult)

        self.ops[stream].append(go)

    def final_wait(self, eng, dep_stream):
        subs = [e for e in self.ENGS if e.startswith(dep_stream) and e != dep_stream] \
            if dep_stream in ("Q", "G") else [dep_stream]
        vals = [(de, self.seq[de] * (16 if de in self.DMA else 1)) for de in subs]

        def go(p, sems):
            for de, v in vals:
                if v:
                    p.wait_ge(sems[de], v)

        self.ops[eng].append(go)


def rope_tables_fm(T, base):
    i = np.arange(0, 128, 2, dtype=np.float32)
    inv = (np.float32(1.0) / (np.float32(base) ** (i / np.float32(128)))).astype(np.float32)
    t = np.arange(T, dtype=np.float32)
    f = t[None, :] * inv[:, None]
    cos = np.cos(f).astype(np.float32)
    sin = np.sin(f).astype(np.float32)
    return (np.ascontiguousarray(np.concatenate([cos, cos], axis=0)),
            np.ascontiguousarray(np.concatenate([sin, sin], axis=0)))


def host_constants(T, E):
    c = {}
    c["cosA"], c["sinA"] = rope_tables_fm(T, 1000000.0)
    c["cosB"], c["sinB"] = rope_tables_fm(T, 10000.0)
    S = np.zeros((128, 128), np.float32)
    for d in range(64):
        S[d + 64, d] = -1.0
        S[d, d + 64] = 1.0
    c["Smat"] = S
    c["ident"] = np.eye(128, dtype=np.float32)
    c["identb"] = np.eye(128, dtype=np.float32).astype(bfdt)
    n_c = QW // KW
    kk = np.arange(KW)[:, None]
    qq = np.arange(QW)[None, :]
    masks = np.stack([np.where(kk <= qq - KW * cc, 0.0, -1e9).astype(np.float32)
                      for cc in range(n_c)])
    c["masks"] = np.ascontiguousarray(masks.transpose(1, 0, 2).reshape(KW, n_c * QW)).astype(bfdt)
    return c


def layout_weights(inp, T, E):
    NH = E // 128
    NEC = E // 128
    NG = np.asarray(inp["L0_attn_b_wk"]).shape[1] // 128
    w = {}

    def heads_img(A, nh, hw):
        return np.ascontiguousarray(
            A.reshape(NEC, 128, nh, hw).transpose(2, 1, 0, 3).reshape(nh, 128, NEC * hw))

    def fchunk_img(A):
        return np.ascontiguousarray(
            A.reshape(NH, 128, NEC, 128).transpose(2, 1, 0, 3).reshape(NEC, 128, NH * 128))

    wq_a = np.asarray(inp["L0_attn_a_wq"], np.float32)
    wk_a = np.asarray(inp["L0_attn_a_wk"], np.float32)
    wv_a = np.asarray(inp["L0_attn_a_wv"], np.float32)
    wo_a = np.asarray(inp["L0_attn_a_wo"], np.float32)
    w["wqa"] = heads_img(wq_a.T, NH, 128)
    w["wka"] = heads_img(wk_a.T, NH, 128)
    w["woa"] = fchunk_img(wo_a.T)
    w["wva"] = heads_img(wv_a.T, NH // 2, 256)

    wq_b = np.asarray(inp["L0_attn_b_wq"], np.float32)
    wk_b = np.asarray(inp["L0_attn_b_wk"], np.float32)
    wv_b = np.asarray(inp["L0_attn_b_wv"], np.float32)
    wo_b = np.asarray(inp["L0_attn_b_wo"], np.float32)
    w["wqb"] = heads_img(wq_b, NH, 128)
    w["wkb"] = heads_img(wk_b, NG, 128)
    nvp = max(1, NG // 2)
    vwb = wv_b.shape[1] // nvp
    w["wvb"] = heads_img(wv_b, nvp, vwb)
    w["wob"] = fchunk_img(wo_b)

    rw = np.asarray(inp["L0_ffn_router_w"], np.float32)
    dw = np.asarray(inp["L0_ffn_down_w"], np.float32)
    uw = np.asarray(inp["L0_ffn_up_w"], np.float32)
    D_all = dw.transpose(2, 0, 1).reshape(E, 128)
    C = np.concatenate([D_all, rw.T], axis=1)
    w["dr"] = np.ascontiguousarray(
        C.reshape(NEC, 128, 136).transpose(1, 0, 2).reshape(128, NEC * 136))
    w["u"] = np.ascontiguousarray(uw.transpose(0, 2, 1).reshape(128, E))

    w["bq"] = np.ascontiguousarray(np.asarray(inp["L0_attn_b_bq"], np.float32).reshape(NH, 128).T)
    w["bk"] = np.ascontiguousarray(np.asarray(inp["L0_attn_b_bk"], np.float32).reshape(NG, 128).T)
    w["bo"] = np.ascontiguousarray(np.asarray(inp["L0_attn_b_bo"], np.float32).reshape(NH, 128).T)
    w["bv"] = np.ascontiguousarray(np.asarray(inp["L0_attn_b_bv"], np.float32).reshape(1, NG * 128))
    w["qns"] = np.ascontiguousarray(np.asarray(inp["L0_attn_b_q_norm_scale"], np.float32).reshape(128, 1))
    w["kns"] = np.ascontiguousarray(np.asarray(inp["L0_attn_b_k_norm_scale"], np.float32).reshape(128, 1))
    return w


def build_program(T, E, NG):
    NH = E // 128
    NEC = E // 128
    NTC = T // 128
    NQC = T // QW
    n_c = QW // KW
    rep = NH // NG
    nvp = max(1, NG // 2)
    vwb = (NG * 128) // nvp
    scale_a = float(np.float32(1.0) / np.sqrt(np.float32(128)))
    scale_b = float((np.float64(128.0) ** -0.5) ** 2)

    nc = bass.Bass("TRN2", target_bir_lowering=False)

    xin = nc.dram_tensor("xin", [E, T], F32, kind="ExternalInput").ap()
    out = nc.dram_tensor("out", [E, T], F32, kind="ExternalOutput").ap()
    d_cosA = nc.dram_tensor("cosA", [128, T], F32, kind="ExternalInput").ap()
    d_sinA = nc.dram_tensor("sinA", [128, T], F32, kind="ExternalInput").ap()
    d_cosB = nc.dram_tensor("cosB", [128, T], F32, kind="ExternalInput").ap()
    d_sinB = nc.dram_tensor("sinB", [128, T], F32, kind="ExternalInput").ap()
    d_S = nc.dram_tensor("Smat", [128, 128], F32, kind="ExternalInput").ap()
    d_ident = nc.dram_tensor("ident", [128, 128], F32, kind="ExternalInput").ap()
    d_identb = nc.dram_tensor("identb", [128, 128], BF16, kind="ExternalInput").ap()
    d_masks = nc.dram_tensor("masks", [KW, n_c * QW], BF16, kind="ExternalInput").ap()
    d_wqa = nc.dram_tensor("wqa", [NH, 128, NEC * 128], F32, kind="ExternalInput").ap()
    d_wka = nc.dram_tensor("wka", [NH, 128, NEC * 128], F32, kind="ExternalInput").ap()
    d_wva = nc.dram_tensor("wva", [NH // 2, 128, NEC * 256], F32, kind="ExternalInput").ap()
    d_woa = nc.dram_tensor("woa", [NEC, 128, NH * 128], F32, kind="ExternalInput").ap()
    d_wqb = nc.dram_tensor("wqb", [NH, 128, NEC * 128], F32, kind="ExternalInput").ap()
    d_wkb = nc.dram_tensor("wkb", [NG, 128, NEC * 128], F32, kind="ExternalInput").ap()
    d_wvb = nc.dram_tensor("wvb", [nvp, 128, NEC * vwb], F32, kind="ExternalInput").ap()
    d_wob = nc.dram_tensor("wob", [NEC, 128, NH * 128], F32, kind="ExternalInput").ap()
    d_dr = nc.dram_tensor("dr", [128, NEC * 136], F32, kind="ExternalInput").ap()
    d_u = nc.dram_tensor("u", [128, E], F32, kind="ExternalInput").ap()
    d_bq = nc.dram_tensor("bq", [128, NH], F32, kind="ExternalInput").ap()
    d_bk = nc.dram_tensor("bk", [128, NG], F32, kind="ExternalInput").ap()
    d_bo = nc.dram_tensor("bo", [128, NH], F32, kind="ExternalInput").ap()
    d_bv = nc.dram_tensor("bv", [1, NG * 128], F32, kind="ExternalInput").ap()
    d_qns = nc.dram_tensor("qns", [128, 1], F32, kind="ExternalInput").ap()
    d_kns = nc.dram_tensor("kns", [128, 1], F32, kind="ExternalInput").ap()
    osp = nc.dram_tensor("osp", [NH * 128, T], BF16, kind="Internal").ap()

    A = nc.alloc_sbuf_tensor
    x_t = [A(f"x{i}", [128, T], BF16) for i in range(NEC)]
    y_t = [A(f"y{i}", [128, T], BF16) for i in range(NEC)]
    WSW = max(NEC * 256, NEC * 136 + E, NEC * vwb, NH * 128)
    ws = [A("ws0", [128, WSW], BF16), A("ws1", [128, WSW], BF16)]
    cos_s = A("cos_s", [128, T], BF16)
    sin_s = A("sin_s", [128, T], BF16)
    q_sb = A("q_sb", [128, T], BF16)
    k_sb = A("k_sb", [128, T], BF16)
    tmp1 = A("tmp1", [128, T], BF16)
    v_tm = [A(f"vtm{i}", [128, 256], BF16) for i in range(NTC)]
    expT = [A("expT0", [128, 2 * QW], BF16), A("expT1", [128, 2 * QW], BF16)]
    recipZ = A("recipZ", [128, QW], BF16)
    outn = A("outn", [128, QW], BF16)
    mB = A("mB", [128, T], BF16)
    rstdB = A("rstdB", [128, T], BF16)
    m2s = A("m2s", [128, QW], BF16)
    vars_ = A("vars", [128, QW], F32)
    sqs = [A("sqs0", [128, QW], BF16), A("sqs1", [128, QW], BF16)]
    stage = [A("stage0", [128, QW], F32), A("stage1", [128, QW], F32)]
    agT = A("agT", [128, T], BF16)
    a_sb = [A("a_sb0", [128, 128], BF16), A("a_sb1", [128, 128], BF16)]
    ag_sb = [A("ag_sb0", [128, 128], F32), A("ag_sb1", [128, 128], F32)]
    w8 = A("w8", [128, 8], F32)
    g8 = A("g8", [128, 16], F32)
    smalls = A("smalls", [128, 16], F32)
    ident_s = A("ident_s", [128, 128], F32)
    identb_s = A("identb_s", [128, 128], BF16)
    allones = A("allones", [128, 128], BF16)
    Smat_s = A("Smat_s", [128, 128], BF16)
    masks_s = A("masks_s", [128, n_c * QW], BF16)
    bq_s = A("bq_s", [128, NH], F32)
    bk_s = A("bk_s", [128, NG], F32)
    bo_s = A("bo_s", [128, NH], F32)
    bv_pad = A("bv_pad", [128, vwb], BF16)
    qns_s = A("qns_s", [128, 1], F32)
    kns_s = A("kns_s", [128, 1], F32)
    ebias = A("ebias", [128, 1], F32)
    eps5 = A("eps5", [128, 1], F32)
    eps6 = A("eps6", [128, 1], F32)

    PT = [nc.alloc_psum_tensor(f"PT{i}", [128, 1024], F32) for i in range(4)]

    def bank(b, n=512):
        return PT[b // 2][:, (b % 2) * 512:(b % 2) * 512 + n]

    PS = [bank(i) for i in range(8)]

    P = Prog()

    def dma(eng, dst_ap, src_ap, reads, writes):
        P.op(eng, lambda p, d=dst_ap, s=src_ap: p.dma_start(out=d, in_=s),
             reads=reads, writes=writes)

    def mmN(bi, n, lhsT, rhs, reads, start, stop, inc=True):
        P.op("T", lambda p, l=lhsT, r=rhs, st=start, sp=stop, b=bi, nn=n:
             p.matmul(bank(b, nn), l, r, start=st, stop=sp),
             reads=reads, writes=[f"P{bi}"], inc=inc)

    def mm2(bi, lhsT, rhs, reads, start, stop, inc=True):
        mmN(bi, QW, lhsT, rhs, reads, start, stop, inc)

    P.op("V", lambda p: p.memset(allones[:], 1.0), writes=["allones"])
    P.op("V", lambda p: p.memset(eps5[:], 1e-5), writes=["eps5"])
    P.op("V", lambda p: p.memset(eps6[:], 1e-6), writes=["eps6"])
    P.op("V", lambda p: p.memset(bv_pad[:], 0.0), writes=["bv_pad"])
    dma("Q", ident_s[:], d_ident[:, :], [], ["ident_s"])
    dma("Q", identb_s[:], d_identb[:, :], [], ["identb_s"])
    dma("Q", masks_s[:], d_masks[:, :], [], ["masks_s"])
    dma("G", Smat_s[:], d_S[:, :], [], ["Smat_s"])
    dma("G", cos_s[:], d_cosA[:, :], [], ["cos_s"])
    dma("G", sin_s[:], d_sinA[:, :], [], ["sin_s"])
    dma("G", bq_s[:], d_bq[:, :], [], ["bq_s"])
    dma("G", bk_s[:], d_bk[:, :], [], ["bk_s"])
    dma("G", bo_s[:], d_bo[:, :], [], ["bo_s"])
    dma("G", qns_s[:], d_qns[:, :], [], ["qns_s"])
    dma("G", kns_s[:], d_kns[:, :], [], ["kns_s"])
    for i in range(NEC):
        dma("G", x_t[i][:], xin[i * 128:(i + 1) * 128, :], [], [f"x{i}"])

    phases = []

    def add_phase(dmas):
        phases.append(dmas)
        return len(phases) - 1

    emitted = set()

    def emit_phase(idx):
        if idx in emitted or idx >= len(phases):
            return
        emitted.add(idx)
        slot = idx % 2
        name = f"ws{slot}"
        for (off, width, src) in phases[idx]:
            dma("G", ws[slot][:, off:off + width], src, [name], [name])

    def slot_of(idx):
        return idx % 2, f"ws{idx % 2}"

    def layer_norm():
        steps = [(ec, tc) for ec in range(NEC) for tc in range(NQC)]

        def square(i):
            ec, tc = steps[i]
            sq, sqn = sqs[i % 2], f"sqs{i % 2}"
            sl = slice(tc * QW, (tc + 1) * QW)
            P.op("V", lambda p, e=ec, s=sl, q=sq: p.tensor_tensor(
                q[:], x_t[e][:, s], x_t[e][:, s], ALU.mult),
                reads=[f"x{ec}"], writes=[sqn])

        square(0)
        for i, (ec, tc) in enumerate(steps):
            if i + 1 < len(steps):
                square(i + 1)
            sqn = f"sqs{i % 2}"
            sl = slice(tc * QW, (tc + 1) * QW)
            mm2(tc, allones[:], x_t[ec][:, sl], ["allones", f"x{ec}"],
                start=(ec == 0), stop=(ec == NEC - 1))
            mm2(4 + tc, allones[:], sqs[i % 2][:], ["allones", sqn],
                start=(ec == 0), stop=(ec == NEC - 1))
        for tc in range(NQC):
            sl = slice(tc * QW, (tc + 1) * QW)
            P.op("S", lambda p, t=tc, s=sl: p.activation(
                mB[:, s], PS[t][:, 0:QW], AF.Identity, scale=1.0 / E),
                reads=[f"P{tc}"], writes=["mB"])
            P.op("S", lambda p, s=sl: p.activation(m2s[:], mB[:, s], AF.Square),
                 reads=["mB"], writes=["m2s"])
            P.op("V", lambda p, t=tc: p.scalar_tensor_tensor(
                vars_[:], PS[4 + t][:, 0:QW], 1.0 / E, m2s[:], ALU.mult, ALU.subtract),
                reads=[f"P{4 + tc}", "m2s"], writes=["vars"])
            P.op("S", lambda p: p.activation(vars_[:], vars_[:], AF.Ln, bias=eps5[:]),
                 reads=["vars", "eps5"], writes=["vars"])
            P.op("S", lambda p, s=sl: p.activation(rstdB[:, s], vars_[:], AF.Exp, scale=-0.5),
                 reads=["vars"], writes=["rstdB"])
            P.op("V", lambda p, s=sl: p.tensor_tensor(mB[:, s], mB[:, s], rstdB[:, s], ALU.mult),
                 reads=["rstdB"], writes=["mB"])
        for ec in range(NEC):
            P.op("V", lambda p, e=ec: p.tensor_tensor(y_t[e][:], x_t[e][:], rstdB[:], ALU.mult),
                 reads=[f"x{ec}", "rstdB"], writes=[f"y{ec}"])
            P.op("V", lambda p, e=ec: p.tensor_tensor(y_t[e][:], y_t[e][:], mB[:], ALU.subtract),
                 reads=["mB"], writes=[f"y{ec}"])

    def proj(dst, dstname, sname, slot_ap, woff, bias_ap, bias_name):
        for tc in range(NQC):
            b = 6 + (tc % 2)
            for ec in range(NEC):
                lhs = slot_ap[:, woff + ec * 128: woff + (ec + 1) * 128]
                mm2(b, lhs, y_t[ec][:, tc * QW:(tc + 1) * QW],
                    [sname, f"y{ec}"], start=(ec == 0), stop=(ec == NEC - 1),
                    inc=(ec == NEC - 1))
            if bias_ap is None:
                P.op("S", lambda p, bb=b, t=tc: p.activation(
                    dst[:, t * QW:(t + 1) * QW], PS[bb][:, 0:QW], AF.Identity),
                    reads=[f"P{b}"], writes=[f"{dstname}.{tc}"])
            else:
                P.op("S", lambda p, bb=b, t=tc, ba=bias_ap: p.activation(
                    dst[:, t * QW:(t + 1) * QW], PS[bb][:, 0:QW], AF.Identity, bias=ba),
                    reads=[f"P{b}", bias_name], writes=[f"{dstname}.{tc}"])

    def rope(buf, bufname, bank0):
        for tc in range(NQC):
            b = bank0 + (tc % 2)
            sl = slice(tc * QW, (tc + 1) * QW)
            nm = f"{bufname}.{tc}"
            mm2(b, Smat_s[:], buf[:, sl], ["Smat_s", nm], start=True, stop=True)
            P.op("V", lambda p, bb=b, s=sl: p.tensor_tensor(
                tmp1[:, s], PS[bb][:, 0:QW], sin_s[:, s], ALU.mult),
                reads=[f"P{b}", "sin_s"], writes=[f"tmp1.{tc}"])
            P.op("V", lambda p, s=sl: p.tensor_tensor(
                buf[:, s], buf[:, s], cos_s[:, s], ALU.mult),
                reads=["cos_s", nm], writes=[nm])
            P.op("V", lambda p, s=sl: p.tensor_tensor(
                buf[:, s], buf[:, s], tmp1[:, s], ALU.add),
                reads=[f"tmp1.{tc}"], writes=[nm])

    def rmsnorm(buf, bufname, scale_ap, scale_name, eps_ap, eps_name):
        allc = [f"{bufname}.{t}" for t in range(NQC)]
        for tc in range(NQC):
            sq, sqn = sqs[tc % 2], f"sqs{tc % 2}"
            b = 6 + (tc % 2)
            sl = slice(tc * QW, (tc + 1) * QW)
            P.op("V", lambda p, s=sl, q=sq: p.tensor_tensor(
                q[:], buf[:, s], buf[:, s], ALU.mult),
                reads=[f"{bufname}.{tc}"], writes=[sqn])
            mm2(b, allones[:], sq[:], ["allones", sqn], start=True, stop=True)
            P.op("S", lambda p, bb=b, ea=eps_ap: p.activation(vars_[:], PS[bb][:, 0:QW], AF.Ln,
                                                   scale=1.0 / 128.0, bias=ea),
                 reads=[f"P{b}", eps_name], writes=["vars"])
            P.op("S", lambda p, s=sl: p.activation(
                rstdB[:, s], vars_[:], AF.Exp, scale=-0.5),
                reads=["vars"], writes=["rstdB"])
        P.op("V", lambda p: p.tensor_tensor(buf[:], buf[:], rstdB[:], ALU.mult),
             reads=["rstdB"] + allc, writes=allc)
        P.op("V", lambda p, sa=scale_ap: p.tensor_scalar(
            buf[:], buf[:], sa, None, ALU.mult),
            reads=[scale_name] + allc, writes=allc)

    def fill_maxnorm2(buf, bufname, outcol, bank):
        for tc in range(NQC):
            sq, sqn = sqs[tc % 2], f"sqs{tc % 2}"
            sl = slice(tc * QW, (tc + 1) * QW)
            P.op("V", lambda p, s=sl, q=sq: p.tensor_tensor(
                q[:], buf[:, s], buf[:, s], ALU.mult),
                reads=[f"{bufname}.{tc}"], writes=[sqn])
            mm2(bank, allones[:], sq[:], ["allones", sqn], start=True, stop=True)
            P.op("V", lambda p, t=tc: p.reduce_max(
                smalls[:, 8 + t:9 + t], PS[bank][:, 0:QW], axis=AX.X),
                reads=[f"P{bank}"], writes=["smalls"])
        P.op("V", lambda p, o=outcol: p.reduce_max(
            smalls[:, o:o + 1], smalls[:, 8:8 + NQC], axis=AX.X),
            reads=["smalls"], writes=["smalls"])

    def exp_bias(e_scale):
        P.op("V", lambda p: p.tensor_tensor(smalls[:, 2:3], smalls[:, 0:1],
                                            smalls[:, 1:2], ALU.mult),
             reads=["smalls"], writes=["smalls"])
        P.op("S", lambda p: p.activation(smalls[:, 3:4], smalls[:, 2:3], AF.Ln),
             reads=["smalls"], writes=["smalls"])
        P.op("S", lambda p: p.activation(smalls[:, 3:4], smalls[:, 3:4], AF.Exp, scale=0.5),
             reads=["smalls"], writes=["smalls"])
        P.op("V", lambda p, es=e_scale: p.tensor_scalar(
            smalls[:, 4:5], smalls[:, 3:4], -es, 40.0, ALU.mult, ALU.add),
            reads=["smalls"], writes=["smalls"])
        P.op("V", lambda p: p.tensor_scalar(ebias[:], smalls[:, 4:5], 0.0, None, ALU.min),
             reads=["smalls"], writes=["ebias"])

    def attn_head(h, vslice, e_scale):
        for qc in range(NQC):
            qsl = slice(qc * QW, (qc + 1) * QW)
            nkc = (qc + 1) * n_c
            npair = nkc // 2

            def sc_exp(j):
                par = j % 2
                for half in range(2):
                    kc = 2 * j + half
                    cc = kc - qc * n_c
                    diag = cc >= 0
                    mm2(2 * par + half, k_sb[:, kc * KW:(kc + 1) * KW], q_sb[:, qsl],
                        [f"k_sb.{kc // n_c}", f"q_sb.{qc}"], start=True, stop=not diag,
                        inc=not diag)
                    if diag:
                        mm2(2 * par + half, identb_s[:], masks_s[:, cc * QW:(cc + 1) * QW],
                            ["identb_s", "masks_s"], start=False, stop=True)
                P.op("S", lambda p, pa=par: p.activation(
                    expT[pa][:], PT[pa][:, 0:2 * QW], AF.Exp, bias=ebias[:], scale=e_scale),
                    reads=[f"P{2 * par}", f"P{2 * par + 1}", "ebias"], writes=[f"expT{par}"])

            def zav(j):
                par = j % 2
                for half in range(2):
                    kc = 2 * j + half
                    e_ap = expT[par][:, half * QW:(half + 1) * QW]
                    mm2(4, allones[:], e_ap, ["allones", f"expT{par}"],
                        start=(kc == 0), stop=(kc == nkc - 1))
                    mm2(5, v_tm[kc][:, vslice], e_ap,
                        [f"vtm{kc}", f"expT{par}"],
                        start=(kc == 0), stop=(kc == nkc - 1))

            sc_exp(0)
            for j in range(1, npair):
                sc_exp(j)
                zav(j - 1)
            zav(npair - 1)
            P.op("S", lambda p: p.activation(vars_[:], PS[4][:, 0:QW], AF.Ln),
                 reads=["P4"], writes=["vars"])
            P.op("S", lambda p: p.activation(recipZ[:], vars_[:], AF.Exp, scale=-1.0),
                 reads=["vars"], writes=["recipZ"])
            P.op("V", lambda p: p.tensor_tensor(outn[:], PS[5][:, 0:QW], recipZ[:], ALU.mult),
                 reads=["P5", "recipZ"], writes=["outn"])
            dma("Q", osp[h * 128:(h + 1) * 128, qc * QW:(qc + 1) * QW], outn[:],
                ["outn"], [f"osp{h}"])

    def v_proj(sname, slot_ap, vwidth, bvslice=None):
        for tg in range((NTC + 3) // 4):
            tchs = list(range(tg * 4, min((tg + 1) * 4, NTC)))
            for ec in range(NEC):
                for i, t in enumerate(tchs):
                    bank = 4 + i
                    last = (ec == NEC - 1) and bvslice is None
                    mmN(bank, vwidth, y_t[ec][:, t * 128:(t + 1) * 128],
                        slot_ap[:, ec * vwidth:(ec + 1) * vwidth],
                        [f"y{ec}", sname], start=(ec == 0), stop=last,
                        inc=(ec == NEC - 1))
            if bvslice is not None:
                for i, t in enumerate(tchs):
                    mmN(4 + i, vwidth, allones[:], bv_pad[:, 0:vwidth],
                        ["allones", "bv_pad"], start=False, stop=True)
            for i, t in enumerate(tchs):
                bank = 4 + i
                P.op("V", lambda p, b=bank, tt=t, vw=vwidth: p.tensor_copy(
                    v_tm[tt][:, 0:vw], PS[b][:, 0:vw]),
                    reads=[f"P{bank}"], writes=[f"vtm{t}"])

    def wo_phase(phase_base, with_bo):
        for h in range(NH):
            dma("Q", y_t[h][:], osp[h * 128:(h + 1) * 128, :], [f"osp{h}"], [f"y{h}"])
        emit_phase(phase_base)
        for fc in range(NEC):
            if fc + 1 < NEC:
                emit_phase(phase_base + fc + 1)
            slot, sname = slot_of(phase_base + fc)
            for tc in range(NQC):
                b = tc % 2
                for h in range(NH):
                    mm2(b, ws[slot][:, h * 128:(h + 1) * 128],
                        y_t[h][:, tc * QW:(tc + 1) * QW],
                        [sname, f"y{h}"], start=(h == 0), stop=(h == NH - 1),
                        inc=(h == NH - 1))
                if with_bo:
                    P.op("V", lambda p, bb=b, f=fc, t=tc: p.scalar_tensor_tensor(
                        x_t[f][:, t * QW:(t + 1) * QW], PS[bb][:, 0:QW],
                        bo_s[:, f:f + 1], x_t[f][:, t * QW:(t + 1) * QW],
                        ALU.add, ALU.add),
                        reads=[f"P{b}", "bo_s", f"x{fc}"], writes=[f"x{fc}"])
                else:
                    P.op("V", lambda p, bb=b, f=fc, t=tc: p.tensor_tensor(
                        x_t[f][:, t * QW:(t + 1) * QW],
                        x_t[f][:, t * QW:(t + 1) * QW], PS[bb][:, 0:QW], ALU.add),
                        reads=[f"P{b}", f"x{fc}"], writes=[f"x{fc}"])

    def ffn(phase_idx, last=False):
        layer_norm()
        emit_phase(phase_idx)
        slot, sname = slot_of(phase_idx)
        droff = 0
        uoff = NEC * 136
        def gate_chain(t):
            b = t % 2
            par = t % 2
            P.op("V", lambda p, bb=b: p.reduce_max(
                smalls[:, 0:1], PS[bb][:, 128:136], axis=AX.X),
                reads=[f"P{b}"], writes=["smalls"])
            P.op("V", lambda p: p.tensor_scalar(
                smalls[:, 1:2], smalls[:, 0:1], -1.0, None, ALU.mult),
                reads=["smalls"], writes=["smalls"])
            P.op("S", lambda p, bb=b: p.activation(
                w8[:], PS[bb][:, 128:136], AF.Exp, bias=smalls[:, 1:2]),
                reads=[f"P{b}", "smalls"], writes=["w8"])
            P.op("V", lambda p: p.reduce_sum(smalls[:, 2:3], w8[:], axis=AX.X),
                 reads=["w8"], writes=["smalls"])
            P.op("V", lambda p: p.reciprocal(smalls[:, 3:4], smalls[:, 2:3]),
                 reads=["smalls"], writes=["smalls"])
            P.op("V", lambda p: p.tensor_scalar(
                g8[:, 0:8], w8[:], 1.0, -1e9, ALU.is_ge, ALU.mult),
                reads=["w8"], writes=["g8"])
            P.op("V", lambda p: p.tensor_tensor(g8[:, 8:16], w8[:], g8[:, 0:8], ALU.add),
                 reads=["w8"], writes=["g8"])
            P.op("V", lambda p: p.reduce_max(smalls[:, 4:5], g8[:, 8:16], axis=AX.X),
                 reads=["g8"], writes=["smalls"])
            P.op("V", lambda p: p.tensor_scalar(
                g8[:, 0:8], w8[:], smalls[:, 4:5], None, ALU.is_ge),
                reads=["w8", "smalls"], writes=["g8"])
            P.op("V", lambda p: p.tensor_tensor(w8[:], w8[:], g8[:, 0:8], ALU.mult),
                 reads=["g8"], writes=["w8"])
            P.op("V", lambda p: p.tensor_scalar(
                w8[:], w8[:], smalls[:, 3:4], None, ALU.mult),
                reads=["smalls"], writes=["w8"])
            P.op("S", lambda p, bb=b: p.activation(
                vars_[:, 0:128], PS[bb][:, 0:128], AF.Exp, scale=-1.0),
                reads=[f"P{b}"], writes=["vars"])
            P.op("V", lambda p: p.tensor_scalar(vars_[:, 0:128], vars_[:, 0:128], 1.0, None, ALU.add),
                 reads=["vars"], writes=["vars"])
            P.op("V", lambda p: p.reciprocal(vars_[:, 128:256], vars_[:, 0:128]),
                 reads=["vars"], writes=["vars"])
            P.op("V", lambda p, bb=b, pa=par: p.tensor_tensor(
                a_sb[pa][:], PS[bb][:, 0:128], vars_[:, 128:256], ALU.mult),
                reads=[f"P{b}", "vars"], writes=[f"a_sb{par}"])
            for ex in range(8):
                P.op("V", lambda p, e=ex, pa=par: p.tensor_scalar(
                    ag_sb[pa][:, e * 16:(e + 1) * 16], a_sb[pa][:, e * 16:(e + 1) * 16],
                    w8[:, e:e + 1], None, ALU.mult),
                    reads=[f"a_sb{par}", "w8"], writes=[f"ag_sb{par}"], inc=(ex == 7))

        def transp(t):
            par = t % 2
            tb = 2 + (t % 2)
            P.op("T", lambda p, bb=tb, pa=par: p.transpose(
                PS[bb][:, 0:128], ag_sb[pa][:], ident_s[:]),
                reads=[f"ag_sb{par}", "ident_s"], writes=[f"P{tb}"])
            P.op("S", lambda p, bb=tb, tt=t: p.activation(
                agT[:, tt * 128:(tt + 1) * 128], PS[bb][:, 0:128], AF.Identity),
                reads=[f"P{tb}"], writes=["agT"])

        for t in range(NTC):
            b = t % 2
            for ec in range(NEC):
                mmN(b, 136, y_t[ec][:, t * 128:(t + 1) * 128],
                    ws[slot][:, droff + ec * 136: droff + (ec + 1) * 136],
                    [f"y{ec}", sname], start=(ec == 0), stop=(ec == NEC - 1),
                    inc=(ec == NEC - 1))
            if t > 0:
                transp(t - 1)
            gate_chain(t)
        transp(NTC - 1)
        for fc in range(NEC):
            for tc in range(NQC):
                b = 4 + ((fc * NQC + tc) % 2)
                mm2(b, ws[slot][:, uoff + fc * 128: uoff + (fc + 1) * 128],
                    agT[:, tc * QW:(tc + 1) * QW], [sname, "agT"],
                    start=True, stop=True)
                if last:
                    par = (fc * NQC + tc) % 2
                    P.op("V", lambda p, bb=b, f=fc, t=tc, pa=par: p.tensor_tensor(
                        stage[pa][:], x_t[f][:, t * QW:(t + 1) * QW],
                        PS[bb][:, 0:QW], ALU.add),
                        reads=[f"P{b}", f"x{fc}"], writes=[f"stage{par}"])
                    dma("Q", out[fc * 128:(fc + 1) * 128, tc * QW:(tc + 1) * QW],
                        stage[par][:], [f"stage{par}"], ["outhbm"])
                else:
                    P.op("V", lambda p, bb=b, f=fc, t=tc: p.tensor_tensor(
                        x_t[f][:, t * QW:(t + 1) * QW],
                        x_t[f][:, t * QW:(t + 1) * QW], PS[bb][:, 0:QW], ALU.add),
                        reads=[f"P{b}", f"x{fc}"], writes=[f"x{fc}"])

    gpv = NG // nvp
    ph_wva, ph_wqk = {}, {}
    for pair in range(NH // 2):
        ph_wva[pair] = add_phase([(0, NEC * 256, d_wva[pair, :, :])])
        for hh in range(2):
            h = pair * 2 + hh
            ph_wqk[h] = add_phase([(0, NEC * 128, d_wqa[h, :, :]),
                                   (NEC * 128, NEC * 128, d_wka[h, :, :])])
    ph_woa = [add_phase([(0, NH * 128, d_woa[fc, :, :])]) for fc in range(NEC)]
    PH_FFN1 = add_phase([(0, NEC * 136, d_dr[:, :]), (NEC * 136, E, d_u[:, :])])
    ph_wvb, ph_wkb, ph_wqb = {}, {}, {}
    for vp in range(nvp):
        ph_wvb[vp] = add_phase([(0, NEC * vwb, d_wvb[vp, :, :])])
        for gg in range(gpv):
            g = vp * gpv + gg
            ph_wkb[g] = add_phase([(0, NEC * 128, d_wkb[g, :, :])])
            for r in range(rep):
                h = g * rep + r
                ph_wqb[h] = add_phase([(0, NEC * 128, d_wqb[h, :, :])])
    ph_wob = [add_phase([(0, NH * 128, d_wob[fc, :, :])]) for fc in range(NEC)]
    PH_FFN2 = add_phase([(0, NEC * 136, d_dr[:, :]), (NEC * 136, E, d_u[:, :])])

    layer_norm()
    emit_phase(0)
    for pair in range(NH // 2):
        emit_phase(ph_wva[pair])
        emit_phase(ph_wva[pair] + 1)
        vslot, vsname = slot_of(ph_wva[pair])
        v_proj(vsname, ws[vslot], 256)
        for hh in range(2):
            h = pair * 2 + hh
            phq = ph_wqk[h]
            emit_phase(phq)
            emit_phase(phq + 1)
            qslot, qsname = slot_of(phq)
            proj(q_sb, "q_sb", qsname, ws[qslot], 0, None, None)
            proj(k_sb, "k_sb", qsname, ws[qslot], NEC * 128, None, None)
            rope(q_sb, "q_sb", 6)
            rope(k_sb, "k_sb", 6)
            fill_maxnorm2(q_sb, "q_sb", 0, 6)
            fill_maxnorm2(k_sb, "k_sb", 1, 6)
            exp_bias(scale_a)
            attn_head(h, slice(hh * 128, (hh + 1) * 128), scale_a)
    wo_phase(ph_woa[0], with_bo=False)

    ffn(PH_FFN1)

    layer_norm()
    dma("G", cos_s[:], d_cosB[:, :], ["cos_s"], ["cos_s"])
    dma("G", sin_s[:], d_sinB[:, :], ["sin_s"], ["sin_s"])
    emit_phase(ph_wvb[0])
    for vp in range(nvp):
        emit_phase(ph_wvb[vp])
        emit_phase(ph_wvb[vp] + 1)
        vslot, vsname = slot_of(ph_wvb[vp])
        dma("G", bv_pad[0:1, :], d_bv[:, vp * vwb:(vp + 1) * vwb], ["bv_pad"], ["bv_pad"])
        v_proj(vsname, ws[vslot], vwb, bvslice=slice(vp * vwb, (vp + 1) * vwb))
        for gg in range(gpv):
            g = vp * gpv + gg
            phk = ph_wkb[g]
            emit_phase(phk)
            emit_phase(phk + 1)
            kslot, ksname = slot_of(phk)
            proj(k_sb, "k_sb", ksname, ws[kslot], 0, bk_s[:, g:g + 1], "bk_s")
            rmsnorm(k_sb, "k_sb", kns_s[:], "kns_s", eps6[:], "eps6")
            rope(k_sb, "k_sb", 6)
            fill_maxnorm2(k_sb, "k_sb", 1, 6)
            for r in range(rep):
                h = g * rep + r
                phq = ph_wqb[h]
                emit_phase(phq)
                emit_phase(phq + 1)
                qslot, qsname = slot_of(phq)
                proj(q_sb, "q_sb", qsname, ws[qslot], 0, bq_s[:, h:h + 1], "bq_s")
                rmsnorm(q_sb, "q_sb", qns_s[:], "qns_s", eps6[:], "eps6")
                rope(q_sb, "q_sb", 6)
                fill_maxnorm2(q_sb, "q_sb", 0, 6)
                exp_bias(scale_b)
                attn_head(h, slice(gg * 128, (gg + 1) * 128), scale_b)
    wo_phase(ph_wob[0], with_bo=True)

    ffn(PH_FFN2, last=True)

    P.final_wait("Q", "Q")
    P.final_wait("G", "G")
    assert len(emitted) == len(phases), (len(emitted), len(phases))

    import contextlib
    with contextlib.ExitStack() as stack:
        sems = {e: stack.enter_context(nc.semaphore(f"sem{e}")) for e in Prog.ENGS}
        block = stack.enter_context(nc.Block())

        @block.tensor
        def _(t):
            for f in P.ops["T"]:
                f(t, sems)

        @block.scalar
        def _(s):
            for f in P.ops["S"]:
                f(s, sems)

        @block.vector
        def _(v):
            for f in P.ops["V"]:
                f(v, sems)

        @block.sync
        def _(q):
            for f in P.ops["Q"]:
                f(q, sems)

        @block.gpsimd
        def _(g):
            for f in P.ops["G"]:
                f(g, sems)

    return nc


def make_in_maps(inputs):
    x = np.asarray(inputs["L0_attn_a_x"], np.float32)
    B, T, E = x.shape
    NG = np.asarray(inputs["L0_attn_b_wk"]).shape[1] // 128
    consts = host_constants(T, E)
    w = layout_weights(inputs, T, E)
    shared = {
        "cosA": consts["cosA"], "sinA": consts["sinA"],
        "cosB": consts["cosB"], "sinB": consts["sinB"],
        "Smat": consts["Smat"], "ident": consts["ident"], "identb": consts["identb"], "masks": consts["masks"],
        "wqa": w["wqa"], "wka": w["wka"], "wva": w["wva"], "woa": w["woa"],
        "wqb": w["wqb"], "wkb": w["wkb"], "wvb": w["wvb"], "wob": w["wob"],
        "dr": w["dr"], "u": w["u"],
        "bq": w["bq"], "bk": w["bk"], "bo": w["bo"], "bv": w["bv"],
        "qns": w["qns"], "kns": w["kns"],
    }
    in_maps = []
    for b in range(B):
        m = dict(shared)
        m["xin"] = np.ascontiguousarray(x[b].T)
        in_maps.append(m)
    return in_maps, T, E, NG


def kernel(**inputs):
    x = np.asarray(inputs["L0_attn_a_x"], np.float32)
    B, T, E = x.shape
    assert B == N_CORES, (B, N_CORES)
    in_maps, T, E, NG = make_in_maps(inputs)
    nc = build_program(T, E, NG)
    res = run_bass_kernel_spmd(nc, in_maps, core_ids=list(range(N_CORES)))
    outs = [np.ascontiguousarray(res.results[b]["out"].T) for b in range(B)]
    return np.stack(outs).astype(np.float32)


# version 17 (final)
# speedup vs baseline: 1.2177x; 1.0000x over previous
import numpy as np
import ml_dtypes

import concourse.bass as bass
import concourse.mybir as mybir
from concourse.bass_utils import run_bass_kernel_spmd

F32 = mybir.dt.float32
BF16 = mybir.dt.bfloat16
AF = mybir.ActivationFunctionType
ALU = mybir.AluOpType
AX = mybir.AxisListType
bfdt = ml_dtypes.bfloat16

N_CORES = 8
QW = 512
KW = 128


class Prog:

    NQ, NG_ = 4, 8
    ENGS = (["T", "S", "V"] + [f"Q{i}" for i in range(NQ)]
            + [f"G{i}" for i in range(NG_)])
    DMA = {e for e in ENGS if e[0] in "QG"}

    def __init__(self):
        self.ops = {"T": [], "S": [], "V": [], "Q": [], "G": []}
        self.seq = {e: 0 for e in self.ENGS}
        self.pend_r = {e: [] for e in self.ENGS}
        self.pend_w = {e: [] for e in self.ENGS}
        self.bufw = {}
        self.bufr = {}
        self.waited = {e: {e2: 0 for e2 in self.ENGS} for e in self.ENGS}
        self.rr = {"Q": 0, "G": 0}
        self.n_waits = 0

    def op(self, eng, fn, reads=(), writes=(), inc=True):
        stream = eng
        if eng == "Q":
            eng = f"Q{self.rr['Q'] % self.NQ}"
            self.rr["Q"] += 1
        elif eng == "G":
            eng = f"G{self.rr['G'] % self.NG_}"
            self.rr["G"] += 1
        waits = {}

        def need(dep_eng, dep_seq, buf):
            if dep_eng == eng == "T" and buf.startswith("P"):
                return
            waits[dep_eng] = max(waits.get(dep_eng, 0), dep_seq)

        if eng in self.DMA and self.seq[eng] > 0:
            waits[eng] = self.seq[eng]
        for b in reads:
            w = self.bufw.get(b)
            if w:
                need(w[0], w[1], b)
        for b in writes:
            w = self.bufw.get(b)
            if w:
                need(w[0], w[1], b)
            for re, rs in self.bufr.get(b, {}).items():
                need(re, rs, b)
        todo = []
        for de, ds in waits.items():
            if ds > self.waited[eng][de]:
                self.waited[eng][de] = ds
                todo.append((de, ds))
        self.n_waits += len(todo)
        self.pend_r[eng].extend(reads)
        self.pend_w[eng].extend(writes)
        if inc:
            self.seq[eng] += 1
            s = self.seq[eng]
            for b in self.pend_w[eng]:
                self.bufw[b] = (eng, s)
                self.bufr.pop(b, None)
            for b in self.pend_r[eng]:
                self.bufr.setdefault(b, {})[eng] = s
            self.pend_r[eng] = []
            self.pend_w[eng] = []
        mult = 16 if eng in self.DMA else 1
        myeng = eng
        dmaset = self.DMA

        def go(p, sems):
            for de, ds in todo:
                p.wait_ge(sems[de], ds * (16 if de in dmaset else 1))
            inst = fn(p)
            if inc and inst is not None:
                inst.then_inc(sems[myeng], mult)

        self.ops[stream].append(go)

    def final_wait(self, eng, dep_stream):
        subs = [e for e in self.ENGS if e.startswith(dep_stream) and e != dep_stream] \
            if dep_stream in ("Q", "G") else [dep_stream]
        vals = [(de, self.seq[de] * (16 if de in self.DMA else 1)) for de in subs]

        def go(p, sems):
            for de, v in vals:
                if v:
                    p.wait_ge(sems[de], v)

        self.ops[eng].append(go)


def rope_tables_fm(T, base):
    i = np.arange(0, 128, 2, dtype=np.float32)
    inv = (np.float32(1.0) / (np.float32(base) ** (i / np.float32(128)))).astype(np.float32)
    t = np.arange(T, dtype=np.float32)
    f = t[None, :] * inv[:, None]
    cos = np.cos(f).astype(np.float32)
    sin = np.sin(f).astype(np.float32)
    return (np.ascontiguousarray(np.concatenate([cos, cos], axis=0)),
            np.ascontiguousarray(np.concatenate([sin, sin], axis=0)))


def host_constants(T, E):
    c = {}
    c["cosA"], c["sinA"] = rope_tables_fm(T, 1000000.0)
    c["cosB"], c["sinB"] = rope_tables_fm(T, 10000.0)
    S = np.zeros((128, 128), np.float32)
    for d in range(64):
        S[d + 64, d] = -1.0
        S[d, d + 64] = 1.0
    c["Smat"] = S
    c["ident"] = np.eye(128, dtype=np.float32)
    c["identb"] = np.eye(128, dtype=np.float32).astype(bfdt)
    n_c = QW // KW
    kk = np.arange(KW)[:, None]
    qq = np.arange(QW)[None, :]
    masks = np.stack([np.where(kk <= qq - KW * cc, 0.0, -1e9).astype(np.float32)
                      for cc in range(n_c)])
    c["masks"] = np.ascontiguousarray(masks.transpose(1, 0, 2).reshape(KW, n_c * QW)).astype(bfdt)
    return c


def layout_weights(inp, T, E):
    NH = E // 128
    NEC = E // 128
    NG = np.asarray(inp["L0_attn_b_wk"]).shape[1] // 128
    w = {}

    def heads_img(A, nh, hw):
        return np.ascontiguousarray(
            A.reshape(NEC, 128, nh, hw).transpose(2, 1, 0, 3).reshape(nh, 128, NEC * hw))

    def fchunk_img(A):
        return np.ascontiguousarray(
            A.reshape(NH, 128, NEC, 128).transpose(2, 1, 0, 3).reshape(NEC, 128, NH * 128))

    wq_a = np.asarray(inp["L0_attn_a_wq"], np.float32)
    wk_a = np.asarray(inp["L0_attn_a_wk"], np.float32)
    wv_a = np.asarray(inp["L0_attn_a_wv"], np.float32)
    wo_a = np.asarray(inp["L0_attn_a_wo"], np.float32)
    w["wqa"] = heads_img(wq_a.T, NH, 128)
    w["wka"] = heads_img(wk_a.T, NH, 128)
    w["woa"] = fchunk_img(wo_a.T)
    w["wva"] = heads_img(wv_a.T, NH // 2, 256)

    wq_b = np.asarray(inp["L0_attn_b_wq"], np.float32)
    wk_b = np.asarray(inp["L0_attn_b_wk"], np.float32)
    wv_b = np.asarray(inp["L0_attn_b_wv"], np.float32)
    wo_b = np.asarray(inp["L0_attn_b_wo"], np.float32)
    w["wqb"] = heads_img(wq_b, NH, 128)
    w["wkb"] = heads_img(wk_b, NG, 128)
    nvp = max(1, NG // 2)
    vwb = wv_b.shape[1] // nvp
    w["wvb"] = heads_img(wv_b, nvp, vwb)
    w["wob"] = fchunk_img(wo_b)

    rw = np.asarray(inp["L0_ffn_router_w"], np.float32)
    dw = np.asarray(inp["L0_ffn_down_w"], np.float32)
    uw = np.asarray(inp["L0_ffn_up_w"], np.float32)
    D_all = dw.transpose(2, 0, 1).reshape(E, 128)
    C = np.concatenate([D_all, rw.T], axis=1)
    w["dr"] = np.ascontiguousarray(
        C.reshape(NEC, 128, 136).transpose(1, 0, 2).reshape(128, NEC * 136))
    w["u"] = np.ascontiguousarray(uw.transpose(0, 2, 1).reshape(128, E))

    w["bq"] = np.ascontiguousarray(np.asarray(inp["L0_attn_b_bq"], np.float32).reshape(NH, 128).T)
    w["bk"] = np.ascontiguousarray(np.asarray(inp["L0_attn_b_bk"], np.float32).reshape(NG, 128).T)
    w["bo"] = np.ascontiguousarray(np.asarray(inp["L0_attn_b_bo"], np.float32).reshape(NH, 128).T)
    w["bv"] = np.ascontiguousarray(np.asarray(inp["L0_attn_b_bv"], np.float32).reshape(1, NG * 128))
    w["qns"] = np.ascontiguousarray(np.asarray(inp["L0_attn_b_q_norm_scale"], np.float32).reshape(128, 1))
    w["kns"] = np.ascontiguousarray(np.asarray(inp["L0_attn_b_k_norm_scale"], np.float32).reshape(128, 1))
    return w


def build_program(T, E, NG):
    NH = E // 128
    NEC = E // 128
    NTC = T // 128
    NQC = T // QW
    n_c = QW // KW
    rep = NH // NG
    nvp = max(1, NG // 2)
    vwb = (NG * 128) // nvp
    scale_a = float(np.float32(1.0) / np.sqrt(np.float32(128)))
    scale_b = float((np.float64(128.0) ** -0.5) ** 2)

    nc = bass.Bass("TRN2", target_bir_lowering=False)

    xin = nc.dram_tensor("xin", [E, T], F32, kind="ExternalInput").ap()
    out = nc.dram_tensor("out", [E, T], F32, kind="ExternalOutput").ap()
    d_cosA = nc.dram_tensor("cosA", [128, T], F32, kind="ExternalInput").ap()
    d_sinA = nc.dram_tensor("sinA", [128, T], F32, kind="ExternalInput").ap()
    d_cosB = nc.dram_tensor("cosB", [128, T], F32, kind="ExternalInput").ap()
    d_sinB = nc.dram_tensor("sinB", [128, T], F32, kind="ExternalInput").ap()
    d_S = nc.dram_tensor("Smat", [128, 128], F32, kind="ExternalInput").ap()
    d_ident = nc.dram_tensor("ident", [128, 128], F32, kind="ExternalInput").ap()
    d_identb = nc.dram_tensor("identb", [128, 128], BF16, kind="ExternalInput").ap()
    d_masks = nc.dram_tensor("masks", [KW, n_c * QW], BF16, kind="ExternalInput").ap()
    d_wqa = nc.dram_tensor("wqa", [NH, 128, NEC * 128], F32, kind="ExternalInput").ap()
    d_wka = nc.dram_tensor("wka", [NH, 128, NEC * 128], F32, kind="ExternalInput").ap()
    d_wva = nc.dram_tensor("wva", [NH // 2, 128, NEC * 256], F32, kind="ExternalInput").ap()
    d_woa = nc.dram_tensor("woa", [NEC, 128, NH * 128], F32, kind="ExternalInput").ap()
    d_wqb = nc.dram_tensor("wqb", [NH, 128, NEC * 128], F32, kind="ExternalInput").ap()
    d_wkb = nc.dram_tensor("wkb", [NG, 128, NEC * 128], F32, kind="ExternalInput").ap()
    d_wvb = nc.dram_tensor("wvb", [nvp, 128, NEC * vwb], F32, kind="ExternalInput").ap()
    d_wob = nc.dram_tensor("wob", [NEC, 128, NH * 128], F32, kind="ExternalInput").ap()
    d_dr = nc.dram_tensor("dr", [128, NEC * 136], F32, kind="ExternalInput").ap()
    d_u = nc.dram_tensor("u", [128, E], F32, kind="ExternalInput").ap()
    d_bq = nc.dram_tensor("bq", [128, NH], F32, kind="ExternalInput").ap()
    d_bk = nc.dram_tensor("bk", [128, NG], F32, kind="ExternalInput").ap()
    d_bo = nc.dram_tensor("bo", [128, NH], F32, kind="ExternalInput").ap()
    d_bv = nc.dram_tensor("bv", [1, NG * 128], F32, kind="ExternalInput").ap()
    d_qns = nc.dram_tensor("qns", [128, 1], F32, kind="ExternalInput").ap()
    d_kns = nc.dram_tensor("kns", [128, 1], F32, kind="ExternalInput").ap()
    osp = nc.dram_tensor("osp", [NH * 128, T], BF16, kind="Internal").ap()

    A = nc.alloc_sbuf_tensor
    x_t = [A(f"x{i}", [128, T], BF16) for i in range(NEC)]
    y_t = [A(f"y{i}", [128, T], BF16) for i in range(NEC)]
    WSW = max(NEC * 256, NEC * 136 + E, NEC * vwb, NH * 128)
    ws = [A("ws0", [128, WSW], BF16), A("ws1", [128, WSW], BF16)]
    cos_s = A("cos_s", [128, T], BF16)
    sin_s = A("sin_s", [128, T], BF16)
    q_sb = A("q_sb", [128, T], BF16)
    k_sb = A("k_sb", [128, T], BF16)
    tmp1 = A("tmp1", [128, T], BF16)
    v_tm = [A(f"vtm{i}", [128, 256], BF16) for i in range(NTC)]
    expT = [A("expT0", [128, 2 * QW], BF16), A("expT1", [128, 2 * QW], BF16)]
    recipZ = A("recipZ", [128, QW], BF16)
    outn = A("outn", [128, QW], BF16)
    mB = A("mB", [128, T], BF16)
    rstdB = A("rstdB", [128, T], BF16)
    m2s = A("m2s", [128, QW], BF16)
    vars_ = A("vars", [128, QW], F32)
    sqs = [A("sqs0", [128, QW], BF16), A("sqs1", [128, QW], BF16)]
    stage = [A("stage0", [128, QW], F32), A("stage1", [128, QW], F32)]
    agT = A("agT", [128, T], BF16)
    a_sb = [A("a_sb0", [128, 128], BF16), A("a_sb1", [128, 128], BF16)]
    ag_sb = [A("ag_sb0", [128, 128], F32), A("ag_sb1", [128, 128], F32)]
    w8 = A("w8", [128, 8], F32)
    g8 = A("g8", [128, 16], F32)
    smalls = A("smalls", [128, 16], F32)
    ident_s = A("ident_s", [128, 128], F32)
    identb_s = A("identb_s", [128, 128], BF16)
    allones = A("allones", [128, 128], BF16)
    Smat_s = A("Smat_s", [128, 128], BF16)
    masks_s = A("masks_s", [128, n_c * QW], BF16)
    bq_s = A("bq_s", [128, NH], F32)
    bk_s = A("bk_s", [128, NG], F32)
    bo_s = A("bo_s", [128, NH], F32)
    bv_pad = A("bv_pad", [128, vwb], BF16)
    qns_s = A("qns_s", [128, 1], F32)
    kns_s = A("kns_s", [128, 1], F32)
    ebias = A("ebias", [128, 1], F32)
    eps5 = A("eps5", [128, 1], F32)
    eps6 = A("eps6", [128, 1], F32)

    PT = [nc.alloc_psum_tensor(f"PT{i}", [128, 1024], F32) for i in range(4)]

    def bank(b, n=512):
        return PT[b // 2][:, (b % 2) * 512:(b % 2) * 512 + n]

    PS = [bank(i) for i in range(8)]

    P = Prog()

    def dma(eng, dst_ap, src_ap, reads, writes):
        P.op(eng, lambda p, d=dst_ap, s=src_ap: p.dma_start(out=d, in_=s),
             reads=reads, writes=writes)

    def mmN(bi, n, lhsT, rhs, reads, start, stop, inc=True):
        P.op("T", lambda p, l=lhsT, r=rhs, st=start, sp=stop, b=bi, nn=n:
             p.matmul(bank(b, nn), l, r, start=st, stop=sp),
             reads=reads, writes=[f"P{bi}"], inc=inc)

    def mm2(bi, lhsT, rhs, reads, start, stop, inc=True):
        mmN(bi, QW, lhsT, rhs, reads, start, stop, inc)

    P.op("V", lambda p: p.memset(allones[:], 1.0), writes=["allones"])
    P.op("V", lambda p: p.memset(eps5[:], 1e-5), writes=["eps5"])
    P.op("V", lambda p: p.memset(eps6[:], 1e-6), writes=["eps6"])
    P.op("V", lambda p: p.memset(bv_pad[:], 0.0), writes=["bv_pad"])
    dma("Q", ident_s[:], d_ident[:, :], [], ["ident_s"])
    dma("Q", identb_s[:], d_identb[:, :], [], ["identb_s"])
    dma("Q", masks_s[:], d_masks[:, :], [], ["masks_s"])
    dma("G", Smat_s[:], d_S[:, :], [], ["Smat_s"])
    dma("G", cos_s[:], d_cosA[:, :], [], ["cos_s"])
    dma("G", sin_s[:], d_sinA[:, :], [], ["sin_s"])
    dma("G", bq_s[:], d_bq[:, :], [], ["bq_s"])
    dma("G", bk_s[:], d_bk[:, :], [], ["bk_s"])
    dma("G", bo_s[:], d_bo[:, :], [], ["bo_s"])
    dma("G", qns_s[:], d_qns[:, :], [], ["qns_s"])
    dma("G", kns_s[:], d_kns[:, :], [], ["kns_s"])
    for i in range(NEC):
        dma("G", x_t[i][:], xin[i * 128:(i + 1) * 128, :], [], [f"x{i}"])

    phases = []

    def add_phase(dmas):
        phases.append(dmas)
        return len(phases) - 1

    emitted = set()

    def emit_phase(idx):
        if idx in emitted or idx >= len(phases):
            return
        emitted.add(idx)
        slot = idx % 2
        name = f"ws{slot}"
        for (off, width, src) in phases[idx]:
            dma("G", ws[slot][:, off:off + width], src, [name], [name])

    def slot_of(idx):
        return idx % 2, f"ws{idx % 2}"

    def layer_norm():
        steps = [(ec, tc) for ec in range(NEC) for tc in range(NQC)]

        def square(i):
            ec, tc = steps[i]
            sq, sqn = sqs[i % 2], f"sqs{i % 2}"
            sl = slice(tc * QW, (tc + 1) * QW)
            P.op("V", lambda p, e=ec, s=sl, q=sq: p.tensor_tensor(
                q[:], x_t[e][:, s], x_t[e][:, s], ALU.mult),
                reads=[f"x{ec}"], writes=[sqn])

        square(0)
        for i, (ec, tc) in enumerate(steps):
            if i + 1 < len(steps):
                square(i + 1)
            sqn = f"sqs{i % 2}"
            sl = slice(tc * QW, (tc + 1) * QW)
            mm2(tc, allones[:], x_t[ec][:, sl], ["allones", f"x{ec}"],
                start=(ec == 0), stop=(ec == NEC - 1))
            mm2(4 + tc, allones[:], sqs[i % 2][:], ["allones", sqn],
                start=(ec == 0), stop=(ec == NEC - 1))
        for tc in range(NQC):
            sl = slice(tc * QW, (tc + 1) * QW)
            P.op("S", lambda p, t=tc, s=sl: p.activation(
                mB[:, s], PS[t][:, 0:QW], AF.Identity, scale=1.0 / E),
                reads=[f"P{tc}"], writes=["mB"])
            P.op("S", lambda p, s=sl: p.activation(m2s[:], mB[:, s], AF.Square),
                 reads=["mB"], writes=["m2s"])
            P.op("V", lambda p, t=tc: p.scalar_tensor_tensor(
                vars_[:], PS[4 + t][:, 0:QW], 1.0 / E, m2s[:], ALU.mult, ALU.subtract),
                reads=[f"P{4 + tc}", "m2s"], writes=["vars"])
            P.op("S", lambda p: p.activation(vars_[:], vars_[:], AF.Ln, bias=eps5[:]),
                 reads=["vars", "eps5"], writes=["vars"])
            P.op("S", lambda p, s=sl: p.activation(rstdB[:, s], vars_[:], AF.Exp, scale=-0.5),
                 reads=["vars"], writes=["rstdB"])
            P.op("V", lambda p, s=sl: p.tensor_tensor(mB[:, s], mB[:, s], rstdB[:, s], ALU.mult),
                 reads=["rstdB"], writes=["mB"])
        for ec in range(NEC):
            P.op("V", lambda p, e=ec: p.tensor_tensor(y_t[e][:], x_t[e][:], rstdB[:], ALU.mult),
                 reads=[f"x{ec}", "rstdB"], writes=[f"y{ec}"])
            P.op("V", lambda p, e=ec: p.tensor_tensor(y_t[e][:], y_t[e][:], mB[:], ALU.subtract),
                 reads=["mB"], writes=[f"y{ec}"])

    def proj(dst, dstname, sname, slot_ap, woff, bias_ap, bias_name):
        for tc in range(NQC):
            b = 6 + (tc % 2)
            for ec in range(NEC):
                lhs = slot_ap[:, woff + ec * 128: woff + (ec + 1) * 128]
                mm2(b, lhs, y_t[ec][:, tc * QW:(tc + 1) * QW],
                    [sname, f"y{ec}"], start=(ec == 0), stop=(ec == NEC - 1),
                    inc=(ec == NEC - 1))
            if bias_ap is None:
                P.op("S", lambda p, bb=b, t=tc: p.activation(
                    dst[:, t * QW:(t + 1) * QW], PS[bb][:, 0:QW], AF.Identity),
                    reads=[f"P{b}"], writes=[f"{dstname}.{tc}"])
            else:
                P.op("S", lambda p, bb=b, t=tc, ba=bias_ap: p.activation(
                    dst[:, t * QW:(t + 1) * QW], PS[bb][:, 0:QW], AF.Identity, bias=ba),
                    reads=[f"P{b}", bias_name], writes=[f"{dstname}.{tc}"])

    def rope(buf, bufname, bank0):
        for tc in range(NQC):
            b = bank0 + (tc % 2)
            sl = slice(tc * QW, (tc + 1) * QW)
            nm = f"{bufname}.{tc}"
            mm2(b, Smat_s[:], buf[:, sl], ["Smat_s", nm], start=True, stop=True)
            P.op("V", lambda p, bb=b, s=sl: p.tensor_tensor(
                tmp1[:, s], PS[bb][:, 0:QW], sin_s[:, s], ALU.mult),
                reads=[f"P{b}", "sin_s"], writes=[f"tmp1.{tc}"])
            P.op("V", lambda p, s=sl: p.tensor_tensor(
                buf[:, s], buf[:, s], cos_s[:, s], ALU.mult),
                reads=["cos_s", nm], writes=[nm])
            P.op("V", lambda p, s=sl: p.tensor_tensor(
                buf[:, s], buf[:, s], tmp1[:, s], ALU.add),
                reads=[f"tmp1.{tc}"], writes=[nm])

    def rmsnorm(buf, bufname, scale_ap, scale_name, eps_ap, eps_name):
        allc = [f"{bufname}.{t}" for t in range(NQC)]
        for tc in range(NQC):
            sq, sqn = sqs[tc % 2], f"sqs{tc % 2}"
            b = 6 + (tc % 2)
            sl = slice(tc * QW, (tc + 1) * QW)
            P.op("V", lambda p, s=sl, q=sq: p.tensor_tensor(
                q[:], buf[:, s], buf[:, s], ALU.mult),
                reads=[f"{bufname}.{tc}"], writes=[sqn])
            mm2(b, allones[:], sq[:], ["allones", sqn], start=True, stop=True)
            P.op("S", lambda p, bb=b, ea=eps_ap: p.activation(vars_[:], PS[bb][:, 0:QW], AF.Ln,
                                                   scale=1.0 / 128.0, bias=ea),
                 reads=[f"P{b}", eps_name], writes=["vars"])
            P.op("S", lambda p, s=sl: p.activation(
                rstdB[:, s], vars_[:], AF.Exp, scale=-0.5),
                reads=["vars"], writes=["rstdB"])
        P.op("V", lambda p: p.tensor_tensor(buf[:], buf[:], rstdB[:], ALU.mult),
             reads=["rstdB"] + allc, writes=allc)
        P.op("V", lambda p, sa=scale_ap: p.tensor_scalar(
            buf[:], buf[:], sa, None, ALU.mult),
            reads=[scale_name] + allc, writes=allc)

    def fill_maxnorm2(buf, bufname, outcol, bank):
        for tc in range(NQC):
            sq, sqn = sqs[tc % 2], f"sqs{tc % 2}"
            sl = slice(tc * QW, (tc + 1) * QW)
            P.op("V", lambda p, s=sl, q=sq: p.tensor_tensor(
                q[:], buf[:, s], buf[:, s], ALU.mult),
                reads=[f"{bufname}.{tc}"], writes=[sqn])
            mm2(bank, allones[:], sq[:], ["allones", sqn], start=True, stop=True)
            P.op("V", lambda p, t=tc: p.reduce_max(
                smalls[:, 8 + t:9 + t], PS[bank][:, 0:QW], axis=AX.X),
                reads=[f"P{bank}"], writes=["smalls"])
        P.op("V", lambda p, o=outcol: p.reduce_max(
            smalls[:, o:o + 1], smalls[:, 8:8 + NQC], axis=AX.X),
            reads=["smalls"], writes=["smalls"])

    def exp_bias(e_scale):
        P.op("V", lambda p: p.tensor_tensor(smalls[:, 2:3], smalls[:, 0:1],
                                            smalls[:, 1:2], ALU.mult),
             reads=["smalls"], writes=["smalls"])
        P.op("S", lambda p: p.activation(smalls[:, 3:4], smalls[:, 2:3], AF.Ln),
             reads=["smalls"], writes=["smalls"])
        P.op("S", lambda p: p.activation(smalls[:, 3:4], smalls[:, 3:4], AF.Exp, scale=0.5),
             reads=["smalls"], writes=["smalls"])
        P.op("V", lambda p, es=e_scale: p.tensor_scalar(
            smalls[:, 4:5], smalls[:, 3:4], -es, 40.0, ALU.mult, ALU.add),
            reads=["smalls"], writes=["smalls"])
        P.op("V", lambda p: p.tensor_scalar(ebias[:], smalls[:, 4:5], 0.0, None, ALU.min),
             reads=["smalls"], writes=["ebias"])

    def attn_head(h, vslice, e_scale):
        for qc in range(NQC):
            qsl = slice(qc * QW, (qc + 1) * QW)
            nkc = (qc + 1) * n_c
            npair = nkc // 2

            def sc_exp(j):
                par = j % 2
                for half in range(2):
                    kc = 2 * j + half
                    cc = kc - qc * n_c
                    diag = cc >= 0
                    mm2(2 * par + half, k_sb[:, kc * KW:(kc + 1) * KW], q_sb[:, qsl],
                        [f"k_sb.{kc // n_c}", f"q_sb.{qc}"], start=True, stop=not diag,
                        inc=not diag)
                    if diag:
                        mm2(2 * par + half, identb_s[:], masks_s[:, cc * QW:(cc + 1) * QW],
                            ["identb_s", "masks_s"], start=False, stop=True)
                P.op("S", lambda p, pa=par: p.activation(
                    expT[pa][:], PT[pa][:, 0:2 * QW], AF.Exp, bias=ebias[:], scale=e_scale),
                    reads=[f"P{2 * par}", f"P{2 * par + 1}", "ebias"], writes=[f"expT{par}"])

            def zav(j):
                par = j % 2
                for half in range(2):
                    kc = 2 * j + half
                    e_ap = expT[par][:, half * QW:(half + 1) * QW]
                    mm2(4, allones[:], e_ap, ["allones", f"expT{par}"],
                        start=(kc == 0), stop=(kc == nkc - 1))
                    mm2(5, v_tm[kc][:, vslice], e_ap,
                        [f"vtm{kc}", f"expT{par}"],
                        start=(kc == 0), stop=(kc == nkc - 1))

            sc_exp(0)
            for j in range(1, npair):
                sc_exp(j)
                zav(j - 1)
            zav(npair - 1)
            P.op("S", lambda p: p.activation(vars_[:], PS[4][:, 0:QW], AF.Ln),
                 reads=["P4"], writes=["vars"])
            P.op("S", lambda p: p.activation(recipZ[:], vars_[:], AF.Exp, scale=-1.0),
                 reads=["vars"], writes=["recipZ"])
            P.op("V", lambda p: p.tensor_tensor(outn[:], PS[5][:, 0:QW], recipZ[:], ALU.mult),
                 reads=["P5", "recipZ"], writes=["outn"])
            dma("Q", osp[h * 128:(h + 1) * 128, qc * QW:(qc + 1) * QW], outn[:],
                ["outn"], [f"osp{h}"])

    def v_proj(sname, slot_ap, vwidth, bvslice=None):
        for tg in range((NTC + 3) // 4):
            tchs = list(range(tg * 4, min((tg + 1) * 4, NTC)))
            for ec in range(NEC):
                for i, t in enumerate(tchs):
                    bank = 4 + i
                    last = (ec == NEC - 1) and bvslice is None
                    mmN(bank, vwidth, y_t[ec][:, t * 128:(t + 1) * 128],
                        slot_ap[:, ec * vwidth:(ec + 1) * vwidth],
                        [f"y{ec}", sname], start=(ec == 0), stop=last,
                        inc=(ec == NEC - 1))
            if bvslice is not None:
                for i, t in enumerate(tchs):
                    mmN(4 + i, vwidth, allones[:], bv_pad[:, 0:vwidth],
                        ["allones", "bv_pad"], start=False, stop=True)
            for i, t in enumerate(tchs):
                bank = 4 + i
                P.op("V", lambda p, b=bank, tt=t, vw=vwidth: p.tensor_copy(
                    v_tm[tt][:, 0:vw], PS[b][:, 0:vw]),
                    reads=[f"P{bank}"], writes=[f"vtm{t}"])

    def wo_phase(phase_base, with_bo):
        for h in range(NH):
            dma("Q", y_t[h][:], osp[h * 128:(h + 1) * 128, :], [f"osp{h}"], [f"y{h}"])
        emit_phase(phase_base)
        for fc in range(NEC):
            if fc + 1 < NEC:
                emit_phase(phase_base + fc + 1)
            slot, sname = slot_of(phase_base + fc)
            for tc in range(NQC):
                b = tc % 2
                for h in range(NH):
                    mm2(b, ws[slot][:, h * 128:(h + 1) * 128],
                        y_t[h][:, tc * QW:(tc + 1) * QW],
                        [sname, f"y{h}"], start=(h == 0), stop=(h == NH - 1),
                        inc=(h == NH - 1))
                if with_bo:
                    P.op("V", lambda p, bb=b, f=fc, t=tc: p.scalar_tensor_tensor(
                        x_t[f][:, t * QW:(t + 1) * QW], PS[bb][:, 0:QW],
                        bo_s[:, f:f + 1], x_t[f][:, t * QW:(t + 1) * QW],
                        ALU.add, ALU.add),
                        reads=[f"P{b}", "bo_s", f"x{fc}"], writes=[f"x{fc}"])
                else:
                    P.op("V", lambda p, bb=b, f=fc, t=tc: p.tensor_tensor(
                        x_t[f][:, t * QW:(t + 1) * QW],
                        x_t[f][:, t * QW:(t + 1) * QW], PS[bb][:, 0:QW], ALU.add),
                        reads=[f"P{b}", f"x{fc}"], writes=[f"x{fc}"])

    def ffn(phase_idx, last=False):
        layer_norm()
        emit_phase(phase_idx)
        slot, sname = slot_of(phase_idx)
        droff = 0
        uoff = NEC * 136
        def gate_chain(t):
            b = t % 2
            par = t % 2
            P.op("V", lambda p, bb=b: p.reduce_max(
                smalls[:, 0:1], PS[bb][:, 128:136], axis=AX.X),
                reads=[f"P{b}"], writes=["smalls"])
            P.op("V", lambda p: p.tensor_scalar(
                smalls[:, 1:2], smalls[:, 0:1], -1.0, None, ALU.mult),
                reads=["smalls"], writes=["smalls"])
            P.op("S", lambda p, bb=b: p.activation(
                w8[:], PS[bb][:, 128:136], AF.Exp, bias=smalls[:, 1:2]),
                reads=[f"P{b}", "smalls"], writes=["w8"])
            P.op("V", lambda p: p.reduce_sum(smalls[:, 2:3], w8[:], axis=AX.X),
                 reads=["w8"], writes=["smalls"])
            P.op("V", lambda p: p.reciprocal(smalls[:, 3:4], smalls[:, 2:3]),
                 reads=["smalls"], writes=["smalls"])
            P.op("V", lambda p: p.tensor_scalar(
                g8[:, 0:8], w8[:], 1.0, -1e9, ALU.is_ge, ALU.mult),
                reads=["w8"], writes=["g8"])
            P.op("V", lambda p: p.tensor_tensor(g8[:, 8:16], w8[:], g8[:, 0:8], ALU.add),
                 reads=["w8"], writes=["g8"])
            P.op("V", lambda p: p.reduce_max(smalls[:, 4:5], g8[:, 8:16], axis=AX.X),
                 reads=["g8"], writes=["smalls"])
            P.op("V", lambda p: p.tensor_scalar(
                g8[:, 0:8], w8[:], smalls[:, 4:5], None, ALU.is_ge),
                reads=["w8", "smalls"], writes=["g8"])
            P.op("V", lambda p: p.tensor_tensor(w8[:], w8[:], g8[:, 0:8], ALU.mult),
                 reads=["g8"], writes=["w8"])
            P.op("V", lambda p: p.tensor_scalar(
                w8[:], w8[:], smalls[:, 3:4], None, ALU.mult),
                reads=["smalls"], writes=["w8"])
            P.op("S", lambda p, bb=b: p.activation(
                vars_[:, 0:128], PS[bb][:, 0:128], AF.Exp, scale=-1.0),
                reads=[f"P{b}"], writes=["vars"])
            P.op("V", lambda p: p.tensor_scalar(vars_[:, 0:128], vars_[:, 0:128], 1.0, None, ALU.add),
                 reads=["vars"], writes=["vars"])
            P.op("V", lambda p: p.reciprocal(vars_[:, 128:256], vars_[:, 0:128]),
                 reads=["vars"], writes=["vars"])
            P.op("V", lambda p, bb=b, pa=par: p.tensor_tensor(
                a_sb[pa][:], PS[bb][:, 0:128], vars_[:, 128:256], ALU.mult),
                reads=[f"P{b}", "vars"], writes=[f"a_sb{par}"])
            for ex in range(8):
                P.op("V", lambda p, e=ex, pa=par: p.tensor_scalar(
                    ag_sb[pa][:, e * 16:(e + 1) * 16], a_sb[pa][:, e * 16:(e + 1) * 16],
                    w8[:, e:e + 1], None, ALU.mult),
                    reads=[f"a_sb{par}", "w8"], writes=[f"ag_sb{par}"], inc=(ex == 7))

        def transp(t):
            par = t % 2
            tb = 2 + (t % 2)
            P.op("T", lambda p, bb=tb, pa=par: p.transpose(
                PS[bb][:, 0:128], ag_sb[pa][:], ident_s[:]),
                reads=[f"ag_sb{par}", "ident_s"], writes=[f"P{tb}"])
            P.op("S", lambda p, bb=tb, tt=t: p.activation(
                agT[:, tt * 128:(tt + 1) * 128], PS[bb][:, 0:128], AF.Identity),
                reads=[f"P{tb}"], writes=["agT"])

        for t in range(NTC):
            b = t % 2
            for ec in range(NEC):
                mmN(b, 136, y_t[ec][:, t * 128:(t + 1) * 128],
                    ws[slot][:, droff + ec * 136: droff + (ec + 1) * 136],
                    [f"y{ec}", sname], start=(ec == 0), stop=(ec == NEC - 1),
                    inc=(ec == NEC - 1))
            if t > 0:
                transp(t - 1)
            gate_chain(t)
        transp(NTC - 1)
        stage3 = [stage[0][:], stage[1][:], vars_[:, 0:QW]]
        for fc in range(NEC):
            for tc in range(NQC):
                idx = fc * NQC + tc
                b = 4 + (idx % 4)
                mm2(b, ws[slot][:, uoff + fc * 128: uoff + (fc + 1) * 128],
                    agT[:, tc * QW:(tc + 1) * QW], [sname, "agT"],
                    start=True, stop=True)
                if last:
                    par = idx % 3
                    sname3 = ["stage0", "stage1", "vars"][par]
                    sap = stage3[par]
                    P.op("V", lambda p, bb=b, f=fc, t=tc, sa=sap: p.tensor_tensor(
                        sa, x_t[f][:, t * QW:(t + 1) * QW],
                        PS[bb][:, 0:QW], ALU.add),
                        reads=[f"P{b}", f"x{fc}"], writes=[sname3])
                    dma("Q", out[fc * 128:(fc + 1) * 128, tc * QW:(tc + 1) * QW],
                        sap, [sname3], ["outhbm"])
                else:
                    P.op("V", lambda p, bb=b, f=fc, t=tc: p.tensor_tensor(
                        x_t[f][:, t * QW:(t + 1) * QW],
                        x_t[f][:, t * QW:(t + 1) * QW], PS[bb][:, 0:QW], ALU.add),
                        reads=[f"P{b}", f"x{fc}"], writes=[f"x{fc}"])

    gpv = NG // nvp
    ph_wva, ph_wqk = {}, {}
    for pair in range(NH // 2):
        ph_wva[pair] = add_phase([(0, NEC * 256, d_wva[pair, :, :])])
        for hh in range(2):
            h = pair * 2 + hh
            ph_wqk[h] = add_phase([(0, NEC * 128, d_wqa[h, :, :]),
                                   (NEC * 128, NEC * 128, d_wka[h, :, :])])
    ph_woa = [add_phase([(0, NH * 128, d_woa[fc, :, :])]) for fc in range(NEC)]
    PH_FFN1 = add_phase([(0, NEC * 136, d_dr[:, :]), (NEC * 136, E, d_u[:, :])])
    ph_wvb, ph_wkb, ph_wqb = {}, {}, {}
    for vp in range(nvp):
        ph_wvb[vp] = add_phase([(0, NEC * vwb, d_wvb[vp, :, :])])
        for gg in range(gpv):
            g = vp * gpv + gg
            ph_wkb[g] = add_phase([(0, NEC * 128, d_wkb[g, :, :])])
            for r in range(rep):
                h = g * rep + r
                ph_wqb[h] = add_phase([(0, NEC * 128, d_wqb[h, :, :])])
    ph_wob = [add_phase([(0, NH * 128, d_wob[fc, :, :])]) for fc in range(NEC)]
    PH_FFN2 = add_phase([(0, NEC * 136, d_dr[:, :]), (NEC * 136, E, d_u[:, :])])

    layer_norm()
    emit_phase(0)
    for pair in range(NH // 2):
        emit_phase(ph_wva[pair])
        emit_phase(ph_wva[pair] + 1)
        vslot, vsname = slot_of(ph_wva[pair])
        v_proj(vsname, ws[vslot], 256)
        for hh in range(2):
            h = pair * 2 + hh
            phq = ph_wqk[h]
            emit_phase(phq)
            emit_phase(phq + 1)
            qslot, qsname = slot_of(phq)
            proj(q_sb, "q_sb", qsname, ws[qslot], 0, None, None)
            proj(k_sb, "k_sb", qsname, ws[qslot], NEC * 128, None, None)
            rope(q_sb, "q_sb", 6)
            rope(k_sb, "k_sb", 6)
            fill_maxnorm2(q_sb, "q_sb", 0, 6)
            fill_maxnorm2(k_sb, "k_sb", 1, 6)
            exp_bias(scale_a)
            attn_head(h, slice(hh * 128, (hh + 1) * 128), scale_a)
    wo_phase(ph_woa[0], with_bo=False)

    ffn(PH_FFN1)

    layer_norm()
    dma("G", cos_s[:], d_cosB[:, :], ["cos_s"], ["cos_s"])
    dma("G", sin_s[:], d_sinB[:, :], ["sin_s"], ["sin_s"])
    emit_phase(ph_wvb[0])
    for vp in range(nvp):
        emit_phase(ph_wvb[vp])
        emit_phase(ph_wvb[vp] + 1)
        vslot, vsname = slot_of(ph_wvb[vp])
        dma("G", bv_pad[0:1, :], d_bv[:, vp * vwb:(vp + 1) * vwb], ["bv_pad"], ["bv_pad"])
        v_proj(vsname, ws[vslot], vwb, bvslice=slice(vp * vwb, (vp + 1) * vwb))
        for gg in range(gpv):
            g = vp * gpv + gg
            phk = ph_wkb[g]
            emit_phase(phk)
            emit_phase(phk + 1)
            kslot, ksname = slot_of(phk)
            proj(k_sb, "k_sb", ksname, ws[kslot], 0, bk_s[:, g:g + 1], "bk_s")
            rmsnorm(k_sb, "k_sb", kns_s[:], "kns_s", eps6[:], "eps6")
            rope(k_sb, "k_sb", 6)
            fill_maxnorm2(k_sb, "k_sb", 1, 6)
            for r in range(rep):
                h = g * rep + r
                phq = ph_wqb[h]
                emit_phase(phq)
                emit_phase(phq + 1)
                qslot, qsname = slot_of(phq)
                proj(q_sb, "q_sb", qsname, ws[qslot], 0, bq_s[:, h:h + 1], "bq_s")
                rmsnorm(q_sb, "q_sb", qns_s[:], "qns_s", eps6[:], "eps6")
                rope(q_sb, "q_sb", 6)
                fill_maxnorm2(q_sb, "q_sb", 0, 6)
                exp_bias(scale_b)
                attn_head(h, slice(gg * 128, (gg + 1) * 128), scale_b)
    wo_phase(ph_wob[0], with_bo=True)

    ffn(PH_FFN2, last=True)

    P.final_wait("Q", "Q")
    P.final_wait("G", "G")
    assert len(emitted) == len(phases), (len(emitted), len(phases))

    import contextlib
    with contextlib.ExitStack() as stack:
        sems = {e: stack.enter_context(nc.semaphore(f"sem{e}")) for e in Prog.ENGS}
        block = stack.enter_context(nc.Block())

        @block.tensor
        def _(t):
            for f in P.ops["T"]:
                f(t, sems)

        @block.scalar
        def _(s):
            for f in P.ops["S"]:
                f(s, sems)

        @block.vector
        def _(v):
            for f in P.ops["V"]:
                f(v, sems)

        @block.sync
        def _(q):
            for f in P.ops["Q"]:
                f(q, sems)

        @block.gpsimd
        def _(g):
            for f in P.ops["G"]:
                f(g, sems)

    return nc


def make_in_maps(inputs):
    x = np.asarray(inputs["L0_attn_a_x"], np.float32)
    B, T, E = x.shape
    NG = np.asarray(inputs["L0_attn_b_wk"]).shape[1] // 128
    consts = host_constants(T, E)
    w = layout_weights(inputs, T, E)
    shared = {
        "cosA": consts["cosA"], "sinA": consts["sinA"],
        "cosB": consts["cosB"], "sinB": consts["sinB"],
        "Smat": consts["Smat"], "ident": consts["ident"], "identb": consts["identb"], "masks": consts["masks"],
        "wqa": w["wqa"], "wka": w["wka"], "wva": w["wva"], "woa": w["woa"],
        "wqb": w["wqb"], "wkb": w["wkb"], "wvb": w["wvb"], "wob": w["wob"],
        "dr": w["dr"], "u": w["u"],
        "bq": w["bq"], "bk": w["bk"], "bo": w["bo"], "bv": w["bv"],
        "qns": w["qns"], "kns": w["kns"],
    }
    in_maps = []
    for b in range(B):
        m = dict(shared)
        m["xin"] = np.ascontiguousarray(x[b].T)
        in_maps.append(m)
    return in_maps, T, E, NG


def kernel(**inputs):
    x = np.asarray(inputs["L0_attn_a_x"], np.float32)
    B, T, E = x.shape
    assert B == N_CORES, (B, N_CORES)
    in_maps, T, E, NG = make_in_maps(inputs)
    nc = build_program(T, E, NG)
    res = run_bass_kernel_spmd(nc, in_maps, core_ids=list(range(N_CORES)))
    outs = [np.ascontiguousarray(res.results[b]["out"].T) for b in range(B)]
    return np.stack(outs).astype(np.float32)


# version 18
# speedup vs baseline: 1.2198x; 1.0018x over previous
import numpy as np
import ml_dtypes

import concourse.bass as bass
import concourse.mybir as mybir
from concourse.bass_utils import run_bass_kernel_spmd

F32 = mybir.dt.float32
BF16 = mybir.dt.bfloat16
AF = mybir.ActivationFunctionType
ALU = mybir.AluOpType
AX = mybir.AxisListType
bfdt = ml_dtypes.bfloat16

N_CORES = 8
QW = 512
KW = 128


class Prog:

    NQ, NG_ = 4, 8
    ENGS = (["T", "S", "V"] + [f"Q{i}" for i in range(NQ)]
            + [f"G{i}" for i in range(NG_)])
    DMA = {e for e in ENGS if e[0] in "QG"}

    def __init__(self):
        self.ops = {"T": [], "S": [], "V": [], "Q": [], "G": []}
        self.seq = {e: 0 for e in self.ENGS}
        self.pend_r = {e: [] for e in self.ENGS}
        self.pend_w = {e: [] for e in self.ENGS}
        self.bufw = {}
        self.bufr = {}
        self.waited = {e: {e2: 0 for e2 in self.ENGS} for e in self.ENGS}
        self.rr = {"Q": 0, "G": 0}
        self.n_waits = 0

    def op(self, eng, fn, reads=(), writes=(), inc=True):
        stream = eng
        if eng == "Q":
            eng = f"Q{self.rr['Q'] % self.NQ}"
            self.rr["Q"] += 1
        elif eng == "G":
            eng = f"G{self.rr['G'] % self.NG_}"
            self.rr["G"] += 1
        waits = {}

        def need(dep_eng, dep_seq, buf):
            if dep_eng == eng == "T" and buf.startswith("P"):
                return
            waits[dep_eng] = max(waits.get(dep_eng, 0), dep_seq)

        if eng in self.DMA and self.seq[eng] > 0:
            waits[eng] = self.seq[eng]
        for b in reads:
            w = self.bufw.get(b)
            if w:
                need(w[0], w[1], b)
        for b in writes:
            w = self.bufw.get(b)
            if w:
                need(w[0], w[1], b)
            for re, rs in self.bufr.get(b, {}).items():
                need(re, rs, b)
        todo = []
        for de, ds in waits.items():
            if ds > self.waited[eng][de]:
                self.waited[eng][de] = ds
                todo.append((de, ds))
        self.n_waits += len(todo)
        self.pend_r[eng].extend(reads)
        self.pend_w[eng].extend(writes)
        if inc:
            self.seq[eng] += 1
            s = self.seq[eng]
            for b in self.pend_w[eng]:
                self.bufw[b] = (eng, s)
                self.bufr.pop(b, None)
            for b in self.pend_r[eng]:
                self.bufr.setdefault(b, {})[eng] = s
            self.pend_r[eng] = []
            self.pend_w[eng] = []
        mult = 16 if eng in self.DMA else 1
        myeng = eng
        dmaset = self.DMA

        def go(p, sems):
            for de, ds in todo:
                p.wait_ge(sems[de], ds * (16 if de in dmaset else 1))
            inst = fn(p)
            if inc and inst is not None:
                inst.then_inc(sems[myeng], mult)

        self.ops[stream].append(go)

    def final_wait(self, eng, dep_stream):
        subs = [e for e in self.ENGS if e.startswith(dep_stream) and e != dep_stream] \
            if dep_stream in ("Q", "G") else [dep_stream]
        vals = [(de, self.seq[de] * (16 if de in self.DMA else 1)) for de in subs]

        def go(p, sems):
            for de, v in vals:
                if v:
                    p.wait_ge(sems[de], v)

        self.ops[eng].append(go)


def rope_tables_fm(T, base):
    i = np.arange(0, 128, 2, dtype=np.float32)
    inv = (np.float32(1.0) / (np.float32(base) ** (i / np.float32(128)))).astype(np.float32)
    t = np.arange(T, dtype=np.float32)
    f = t[None, :] * inv[:, None]
    cos = np.cos(f).astype(np.float32)
    sin = np.sin(f).astype(np.float32)
    return (np.ascontiguousarray(np.concatenate([cos, cos], axis=0)),
            np.ascontiguousarray(np.concatenate([sin, sin], axis=0)))


def host_constants(T, E):
    c = {}
    c["cosA"], c["sinA"] = rope_tables_fm(T, 1000000.0)
    c["cosB"], c["sinB"] = rope_tables_fm(T, 10000.0)
    S = np.zeros((128, 128), np.float32)
    for d in range(64):
        S[d + 64, d] = -1.0
        S[d, d + 64] = 1.0
    c["Smat"] = S
    c["ident"] = np.eye(128, dtype=np.float32)
    c["identb"] = np.eye(128, dtype=np.float32).astype(bfdt)
    n_c = QW // KW
    kk = np.arange(KW)[:, None]
    qq = np.arange(QW)[None, :]
    masks = np.stack([np.where(kk <= qq - KW * cc, 0.0, -1e9).astype(np.float32)
                      for cc in range(n_c)])
    c["masks"] = np.ascontiguousarray(masks.transpose(1, 0, 2).reshape(KW, n_c * QW)).astype(bfdt)
    return c


def layout_weights(inp, T, E):
    NH = E // 128
    NEC = E // 128
    NG = np.asarray(inp["L0_attn_b_wk"]).shape[1] // 128
    w = {}

    def heads_img(A, nh, hw):
        return np.ascontiguousarray(
            A.reshape(NEC, 128, nh, hw).transpose(2, 1, 0, 3).reshape(nh, 128, NEC * hw))

    def fchunk_img(A):
        return np.ascontiguousarray(
            A.reshape(NH, 128, NEC, 128).transpose(2, 1, 0, 3).reshape(NEC, 128, NH * 128))

    wq_a = np.asarray(inp["L0_attn_a_wq"], np.float32)
    wk_a = np.asarray(inp["L0_attn_a_wk"], np.float32)
    wv_a = np.asarray(inp["L0_attn_a_wv"], np.float32)
    wo_a = np.asarray(inp["L0_attn_a_wo"], np.float32)
    w["wqa"] = heads_img(wq_a.T, NH, 128)
    w["wka"] = heads_img(wk_a.T, NH, 128)
    w["woa"] = fchunk_img(wo_a.T)
    w["wva"] = heads_img(wv_a.T, NH // 2, 256)

    wq_b = np.asarray(inp["L0_attn_b_wq"], np.float32)
    wk_b = np.asarray(inp["L0_attn_b_wk"], np.float32)
    wv_b = np.asarray(inp["L0_attn_b_wv"], np.float32)
    wo_b = np.asarray(inp["L0_attn_b_wo"], np.float32)
    w["wqb"] = heads_img(wq_b, NH, 128)
    w["wkb"] = heads_img(wk_b, NG, 128)
    nvp = max(1, NG // 2)
    vwb = wv_b.shape[1] // nvp
    w["wvb"] = heads_img(wv_b, nvp, vwb)
    w["wob"] = fchunk_img(wo_b)

    rw = np.asarray(inp["L0_ffn_router_w"], np.float32)
    dw = np.asarray(inp["L0_ffn_down_w"], np.float32)
    uw = np.asarray(inp["L0_ffn_up_w"], np.float32)
    D_all = dw.transpose(2, 0, 1).reshape(E, 128)
    C = np.concatenate([D_all, rw.T], axis=1)
    w["dr"] = np.ascontiguousarray(
        C.reshape(NEC, 128, 136).transpose(1, 0, 2).reshape(128, NEC * 136))
    w["u"] = np.ascontiguousarray(uw.transpose(0, 2, 1).reshape(128, E))

    w["bq"] = np.ascontiguousarray(np.asarray(inp["L0_attn_b_bq"], np.float32).reshape(NH, 128).T)
    w["bk"] = np.ascontiguousarray(np.asarray(inp["L0_attn_b_bk"], np.float32).reshape(NG, 128).T)
    w["bo"] = np.ascontiguousarray(np.asarray(inp["L0_attn_b_bo"], np.float32).reshape(NH, 128).T)
    w["bv"] = np.ascontiguousarray(np.asarray(inp["L0_attn_b_bv"], np.float32).reshape(1, NG * 128))
    w["qns"] = np.ascontiguousarray(np.asarray(inp["L0_attn_b_q_norm_scale"], np.float32).reshape(128, 1))
    w["kns"] = np.ascontiguousarray(np.asarray(inp["L0_attn_b_k_norm_scale"], np.float32).reshape(128, 1))
    return w


def build_program(T, E, NG):
    NH = E // 128
    NEC = E // 128
    NTC = T // 128
    NQC = T // QW
    n_c = QW // KW
    rep = NH // NG
    nvp = max(1, NG // 2)
    vwb = (NG * 128) // nvp
    scale_a = float(np.float32(1.0) / np.sqrt(np.float32(128)))
    scale_b = float((np.float64(128.0) ** -0.5) ** 2)

    nc = bass.Bass("TRN2", target_bir_lowering=False)

    xin = nc.dram_tensor("xin", [E, T], F32, kind="ExternalInput").ap()
    out = nc.dram_tensor("out", [E, T], F32, kind="ExternalOutput").ap()
    d_cosA = nc.dram_tensor("cosA", [128, T], F32, kind="ExternalInput").ap()
    d_sinA = nc.dram_tensor("sinA", [128, T], F32, kind="ExternalInput").ap()
    d_cosB = nc.dram_tensor("cosB", [128, T], F32, kind="ExternalInput").ap()
    d_sinB = nc.dram_tensor("sinB", [128, T], F32, kind="ExternalInput").ap()
    d_S = nc.dram_tensor("Smat", [128, 128], F32, kind="ExternalInput").ap()
    d_ident = nc.dram_tensor("ident", [128, 128], F32, kind="ExternalInput").ap()
    d_identb = nc.dram_tensor("identb", [128, 128], BF16, kind="ExternalInput").ap()
    d_masks = nc.dram_tensor("masks", [KW, n_c * QW], BF16, kind="ExternalInput").ap()
    d_wqa = nc.dram_tensor("wqa", [NH, 128, NEC * 128], F32, kind="ExternalInput").ap()
    d_wka = nc.dram_tensor("wka", [NH, 128, NEC * 128], F32, kind="ExternalInput").ap()
    d_wva = nc.dram_tensor("wva", [NH // 2, 128, NEC * 256], F32, kind="ExternalInput").ap()
    d_woa = nc.dram_tensor("woa", [NEC, 128, NH * 128], F32, kind="ExternalInput").ap()
    d_wqb = nc.dram_tensor("wqb", [NH, 128, NEC * 128], F32, kind="ExternalInput").ap()
    d_wkb = nc.dram_tensor("wkb", [NG, 128, NEC * 128], F32, kind="ExternalInput").ap()
    d_wvb = nc.dram_tensor("wvb", [nvp, 128, NEC * vwb], F32, kind="ExternalInput").ap()
    d_wob = nc.dram_tensor("wob", [NEC, 128, NH * 128], F32, kind="ExternalInput").ap()
    d_dr = nc.dram_tensor("dr", [128, NEC * 136], F32, kind="ExternalInput").ap()
    d_u = nc.dram_tensor("u", [128, E], F32, kind="ExternalInput").ap()
    d_bq = nc.dram_tensor("bq", [128, NH], F32, kind="ExternalInput").ap()
    d_bk = nc.dram_tensor("bk", [128, NG], F32, kind="ExternalInput").ap()
    d_bo = nc.dram_tensor("bo", [128, NH], F32, kind="ExternalInput").ap()
    d_bv = nc.dram_tensor("bv", [1, NG * 128], F32, kind="ExternalInput").ap()
    d_qns = nc.dram_tensor("qns", [128, 1], F32, kind="ExternalInput").ap()
    d_kns = nc.dram_tensor("kns", [128, 1], F32, kind="ExternalInput").ap()
    osp = nc.dram_tensor("osp", [NH * 128, T], BF16, kind="Internal").ap()

    A = nc.alloc_sbuf_tensor
    x_t = [A(f"x{i}", [128, T], BF16) for i in range(NEC)]
    y_t = [A(f"y{i}", [128, T], BF16) for i in range(NEC)]
    WSW = max(NEC * 256, NEC * 136 + E, NEC * vwb, NH * 128)
    ws = [A("ws0", [128, WSW], BF16), A("ws1", [128, WSW], BF16)]
    cos_s = A("cos_s", [128, T], BF16)
    sin_s = A("sin_s", [128, T], BF16)
    q_sb = A("q_sb", [128, T], BF16)
    k_sb = A("k_sb", [128, T], BF16)
    tmp1 = A("tmp1", [128, T], BF16)
    v_tm = [A(f"vtm{i}", [128, 256], BF16) for i in range(NTC)]
    expT = [A("expT0", [128, 2 * QW], BF16), A("expT1", [128, 2 * QW], BF16)]
    recipZ = A("recipZ", [128, QW], BF16)
    outn = A("outn", [128, QW], BF16)
    mB = A("mB", [128, T], BF16)
    rstdB = A("rstdB", [128, T], BF16)
    m2s = A("m2s", [128, QW], BF16)
    vars_ = A("vars", [128, QW], F32)
    sqs = [A("sqs0", [128, QW], BF16), A("sqs1", [128, QW], BF16)]
    stage = [A("stage0", [128, QW], F32), A("stage1", [128, QW], F32)]
    agT = A("agT", [128, T], BF16)
    a_sb = [A("a_sb0", [128, 128], BF16), A("a_sb1", [128, 128], BF16)]
    ag_sb = [A("ag_sb0", [128, 128], F32), A("ag_sb1", [128, 128], F32)]
    w8 = A("w8", [128, 8], F32)
    g8 = A("g8", [128, 16], F32)
    smalls = A("smalls", [128, 16], F32)
    ident_s = A("ident_s", [128, 128], F32)
    identb_s = A("identb_s", [128, 128], BF16)
    allones = A("allones", [128, 128], BF16)
    Smat_s = A("Smat_s", [128, 128], BF16)
    masks_s = A("masks_s", [128, n_c * QW], BF16)
    bq_s = A("bq_s", [128, NH], F32)
    bk_s = A("bk_s", [128, NG], F32)
    bo_s = A("bo_s", [128, NH], F32)
    bv_pad = A("bv_pad", [128, vwb], BF16)
    qns_s = A("qns_s", [128, 1], F32)
    kns_s = A("kns_s", [128, 1], F32)
    ebias = A("ebias", [128, 1], F32)
    eps5 = A("eps5", [128, 1], F32)
    eps6 = A("eps6", [128, 1], F32)

    PT = [nc.alloc_psum_tensor(f"PT{i}", [128, 1024], F32) for i in range(4)]

    def bank(b, n=512):
        return PT[b // 2][:, (b % 2) * 512:(b % 2) * 512 + n]

    PS = [bank(i) for i in range(8)]

    P = Prog()

    def dma(eng, dst_ap, src_ap, reads, writes):
        P.op(eng, lambda p, d=dst_ap, s=src_ap: p.dma_start(out=d, in_=s),
             reads=reads, writes=writes)

    def mmN(bi, n, lhsT, rhs, reads, start, stop, inc=True):
        P.op("T", lambda p, l=lhsT, r=rhs, st=start, sp=stop, b=bi, nn=n:
             p.matmul(bank(b, nn), l, r, start=st, stop=sp),
             reads=reads, writes=[f"P{bi}"], inc=inc)

    def mm2(bi, lhsT, rhs, reads, start, stop, inc=True):
        mmN(bi, QW, lhsT, rhs, reads, start, stop, inc)

    P.op("V", lambda p: p.memset(allones[:], 1.0), writes=["allones"])
    P.op("V", lambda p: p.memset(eps5[:], 1e-5), writes=["eps5"])
    P.op("V", lambda p: p.memset(eps6[:], 1e-6), writes=["eps6"])
    P.op("V", lambda p: p.memset(bv_pad[:], 0.0), writes=["bv_pad"])
    dma("Q", ident_s[:], d_ident[:, :], [], ["ident_s"])
    dma("Q", identb_s[:], d_identb[:, :], [], ["identb_s"])
    dma("Q", masks_s[:], d_masks[:, :], [], ["masks_s"])
    dma("G", Smat_s[:], d_S[:, :], [], ["Smat_s"])
    dma("G", cos_s[:], d_cosA[:, :], [], ["cos_s"])
    dma("G", sin_s[:], d_sinA[:, :], [], ["sin_s"])
    dma("G", bq_s[:], d_bq[:, :], [], ["bq_s"])
    dma("G", bk_s[:], d_bk[:, :], [], ["bk_s"])
    dma("G", bo_s[:], d_bo[:, :], [], ["bo_s"])
    dma("G", qns_s[:], d_qns[:, :], [], ["qns_s"])
    dma("G", kns_s[:], d_kns[:, :], [], ["kns_s"])
    for i in range(NEC):
        dma("G", x_t[i][:], xin[i * 128:(i + 1) * 128, :], [], [f"x{i}"])

    phases = []

    def add_phase(dmas):
        phases.append(dmas)
        return len(phases) - 1

    emitted = set()

    def emit_phase(idx):
        if idx in emitted or idx >= len(phases):
            return
        emitted.add(idx)
        slot = idx % 2
        name = f"ws{slot}"
        for (off, width, src) in phases[idx]:
            dma("G", ws[slot][:, off:off + width], src, [name], [name])

    def slot_of(idx):
        return idx % 2, f"ws{idx % 2}"

    def layer_norm():
        steps = [(ec, tc) for ec in range(NEC) for tc in range(NQC)]

        def square(i):
            ec, tc = steps[i]
            sq, sqn = sqs[i % 2], f"sqs{i % 2}"
            sl = slice(tc * QW, (tc + 1) * QW)
            P.op("V", lambda p, e=ec, s=sl, q=sq: p.tensor_tensor(
                q[:], x_t[e][:, s], x_t[e][:, s], ALU.mult),
                reads=[f"x{ec}"], writes=[sqn])

        square(0)
        for i, (ec, tc) in enumerate(steps):
            if i + 1 < len(steps):
                square(i + 1)
            sqn = f"sqs{i % 2}"
            sl = slice(tc * QW, (tc + 1) * QW)
            mm2(tc, allones[:], x_t[ec][:, sl], ["allones", f"x{ec}"],
                start=(ec == 0), stop=(ec == NEC - 1))
            mm2(4 + tc, allones[:], sqs[i % 2][:], ["allones", sqn],
                start=(ec == 0), stop=(ec == NEC - 1))
        for tc in range(NQC):
            sl = slice(tc * QW, (tc + 1) * QW)
            P.op("S", lambda p, t=tc, s=sl: p.activation(
                mB[:, s], PS[t][:, 0:QW], AF.Identity, scale=1.0 / E),
                reads=[f"P{tc}"], writes=["mB"])
            P.op("S", lambda p, s=sl: p.activation(m2s[:], mB[:, s], AF.Square),
                 reads=["mB"], writes=["m2s"])
            P.op("V", lambda p, t=tc: p.scalar_tensor_tensor(
                vars_[:], PS[4 + t][:, 0:QW], 1.0 / E, m2s[:], ALU.mult, ALU.subtract),
                reads=[f"P{4 + tc}", "m2s"], writes=["vars"])
            P.op("S", lambda p: p.activation(vars_[:], vars_[:], AF.Ln, bias=eps5[:]),
                 reads=["vars", "eps5"], writes=["vars"])
            P.op("S", lambda p, s=sl: p.activation(rstdB[:, s], vars_[:], AF.Exp, scale=-0.5),
                 reads=["vars"], writes=["rstdB"])
            P.op("V", lambda p, s=sl: p.tensor_tensor(mB[:, s], mB[:, s], rstdB[:, s], ALU.mult),
                 reads=["rstdB"], writes=["mB"])
        for ec in range(NEC):
            P.op("V", lambda p, e=ec: p.tensor_tensor(y_t[e][:], x_t[e][:], rstdB[:], ALU.mult),
                 reads=[f"x{ec}", "rstdB"], writes=[f"y{ec}"])
            P.op("V", lambda p, e=ec: p.tensor_tensor(y_t[e][:], y_t[e][:], mB[:], ALU.subtract),
                 reads=["mB"], writes=[f"y{ec}"])

    def proj(dst, dstname, sname, slot_ap, woff, bias_ap, bias_name):
        for tc in range(NQC):
            b = 6 + (tc % 2)
            for ec in range(NEC):
                lhs = slot_ap[:, woff + ec * 128: woff + (ec + 1) * 128]
                mm2(b, lhs, y_t[ec][:, tc * QW:(tc + 1) * QW],
                    [sname, f"y{ec}"], start=(ec == 0), stop=(ec == NEC - 1),
                    inc=(ec == NEC - 1))
            if bias_ap is None:
                P.op("V", lambda p, bb=b, t=tc: p.tensor_copy(
                    dst[:, t * QW:(t + 1) * QW], PS[bb][:, 0:QW]),
                    reads=[f"P{b}"], writes=[f"{dstname}.{tc}"])
            else:
                P.op("S", lambda p, bb=b, t=tc, ba=bias_ap: p.activation(
                    dst[:, t * QW:(t + 1) * QW], PS[bb][:, 0:QW], AF.Identity, bias=ba),
                    reads=[f"P{b}", bias_name], writes=[f"{dstname}.{tc}"])

    def rope(buf, bufname, bank0):
        for tc in range(NQC):
            b = bank0 + (tc % 2)
            sl = slice(tc * QW, (tc + 1) * QW)
            nm = f"{bufname}.{tc}"
            mm2(b, Smat_s[:], buf[:, sl], ["Smat_s", nm], start=True, stop=True)
            P.op("V", lambda p, bb=b, s=sl: p.tensor_tensor(
                tmp1[:, s], PS[bb][:, 0:QW], sin_s[:, s], ALU.mult),
                reads=[f"P{b}", "sin_s"], writes=[f"tmp1.{tc}"])
            P.op("V", lambda p, s=sl: p.tensor_tensor(
                buf[:, s], buf[:, s], cos_s[:, s], ALU.mult),
                reads=["cos_s", nm], writes=[nm])
            P.op("V", lambda p, s=sl: p.tensor_tensor(
                buf[:, s], buf[:, s], tmp1[:, s], ALU.add),
                reads=[f"tmp1.{tc}"], writes=[nm])

    def rmsnorm(buf, bufname, scale_ap, scale_name, eps_ap, eps_name):
        allc = [f"{bufname}.{t}" for t in range(NQC)]
        for tc in range(NQC):
            sq, sqn = sqs[tc % 2], f"sqs{tc % 2}"
            b = 6 + (tc % 2)
            sl = slice(tc * QW, (tc + 1) * QW)
            P.op("V", lambda p, s=sl, q=sq: p.tensor_tensor(
                q[:], buf[:, s], buf[:, s], ALU.mult),
                reads=[f"{bufname}.{tc}"], writes=[sqn])
            mm2(b, allones[:], sq[:], ["allones", sqn], start=True, stop=True)
            P.op("S", lambda p, bb=b, ea=eps_ap: p.activation(vars_[:], PS[bb][:, 0:QW], AF.Ln,
                                                   scale=1.0 / 128.0, bias=ea),
                 reads=[f"P{b}", eps_name], writes=["vars"])
            P.op("S", lambda p, s=sl: p.activation(
                rstdB[:, s], vars_[:], AF.Exp, scale=-0.5),
                reads=["vars"], writes=["rstdB"])
        P.op("V", lambda p: p.tensor_tensor(buf[:], buf[:], rstdB[:], ALU.mult),
             reads=["rstdB"] + allc, writes=allc)
        P.op("V", lambda p, sa=scale_ap: p.tensor_scalar(
            buf[:], buf[:], sa, None, ALU.mult),
            reads=[scale_name] + allc, writes=allc)

    def fill_maxnorm2(buf, bufname, outcol, bank):
        for tc in range(NQC):
            sq, sqn = sqs[tc % 2], f"sqs{tc % 2}"
            sl = slice(tc * QW, (tc + 1) * QW)
            P.op("V", lambda p, s=sl, q=sq: p.tensor_tensor(
                q[:], buf[:, s], buf[:, s], ALU.mult),
                reads=[f"{bufname}.{tc}"], writes=[sqn])
            mm2(bank, allones[:], sq[:], ["allones", sqn], start=True, stop=True)
            P.op("V", lambda p, t=tc: p.reduce_max(
                smalls[:, 8 + t:9 + t], PS[bank][:, 0:QW], axis=AX.X),
                reads=[f"P{bank}"], writes=["smalls"])
        P.op("V", lambda p, o=outcol: p.reduce_max(
            smalls[:, o:o + 1], smalls[:, 8:8 + NQC], axis=AX.X),
            reads=["smalls"], writes=["smalls"])

    def exp_bias(e_scale):
        P.op("V", lambda p: p.tensor_tensor(smalls[:, 2:3], smalls[:, 0:1],
                                            smalls[:, 1:2], ALU.mult),
             reads=["smalls"], writes=["smalls"])
        P.op("S", lambda p: p.activation(smalls[:, 3:4], smalls[:, 2:3], AF.Ln),
             reads=["smalls"], writes=["smalls"])
        P.op("S", lambda p: p.activation(smalls[:, 3:4], smalls[:, 3:4], AF.Exp, scale=0.5),
             reads=["smalls"], writes=["smalls"])
        P.op("V", lambda p, es=e_scale: p.tensor_scalar(
            smalls[:, 4:5], smalls[:, 3:4], -es, 40.0, ALU.mult, ALU.add),
            reads=["smalls"], writes=["smalls"])
        P.op("V", lambda p: p.tensor_scalar(ebias[:], smalls[:, 4:5], 0.0, None, ALU.min),
             reads=["smalls"], writes=["ebias"])

    def attn_head(h, vslice, e_scale):
        for qc in range(NQC):
            qsl = slice(qc * QW, (qc + 1) * QW)
            nkc = (qc + 1) * n_c
            npair = nkc // 2

            def sc_exp(j):
                par = j % 2
                for half in range(2):
                    kc = 2 * j + half
                    cc = kc - qc * n_c
                    diag = cc >= 0
                    mm2(2 * par + half, k_sb[:, kc * KW:(kc + 1) * KW], q_sb[:, qsl],
                        [f"k_sb.{kc // n_c}", f"q_sb.{qc}"], start=True, stop=not diag,
                        inc=not diag)
                    if diag:
                        mm2(2 * par + half, identb_s[:], masks_s[:, cc * QW:(cc + 1) * QW],
                            ["identb_s", "masks_s"], start=False, stop=True)
                P.op("S", lambda p, pa=par: p.activation(
                    expT[pa][:], PT[pa][:, 0:2 * QW], AF.Exp, bias=ebias[:], scale=e_scale),
                    reads=[f"P{2 * par}", f"P{2 * par + 1}", "ebias"], writes=[f"expT{par}"])

            def zav(j):
                par = j % 2
                for half in range(2):
                    kc = 2 * j + half
                    e_ap = expT[par][:, half * QW:(half + 1) * QW]
                    mm2(4, allones[:], e_ap, ["allones", f"expT{par}"],
                        start=(kc == 0), stop=(kc == nkc - 1))
                    mm2(5, v_tm[kc][:, vslice], e_ap,
                        [f"vtm{kc}", f"expT{par}"],
                        start=(kc == 0), stop=(kc == nkc - 1))

            sc_exp(0)
            for j in range(1, npair):
                sc_exp(j)
                zav(j - 1)
            zav(npair - 1)
            P.op("S", lambda p: p.activation(vars_[:], PS[4][:, 0:QW], AF.Ln),
                 reads=["P4"], writes=["vars"])
            P.op("S", lambda p: p.activation(recipZ[:], vars_[:], AF.Exp, scale=-1.0),
                 reads=["vars"], writes=["recipZ"])
            P.op("V", lambda p: p.tensor_tensor(outn[:], PS[5][:, 0:QW], recipZ[:], ALU.mult),
                 reads=["P5", "recipZ"], writes=["outn"])
            dma("Q", osp[h * 128:(h + 1) * 128, qc * QW:(qc + 1) * QW], outn[:],
                ["outn"], [f"osp{h}"])

    def v_proj(sname, slot_ap, vwidth, bvslice=None):
        for tg in range((NTC + 3) // 4):
            tchs = list(range(tg * 4, min((tg + 1) * 4, NTC)))
            for ec in range(NEC):
                for i, t in enumerate(tchs):
                    bank = 4 + i
                    last = (ec == NEC - 1) and bvslice is None
                    mmN(bank, vwidth, y_t[ec][:, t * 128:(t + 1) * 128],
                        slot_ap[:, ec * vwidth:(ec + 1) * vwidth],
                        [f"y{ec}", sname], start=(ec == 0), stop=last,
                        inc=(ec == NEC - 1))
            if bvslice is not None:
                for i, t in enumerate(tchs):
                    mmN(4 + i, vwidth, allones[:], bv_pad[:, 0:vwidth],
                        ["allones", "bv_pad"], start=False, stop=True)
            for i, t in enumerate(tchs):
                bank = 4 + i
                P.op("V", lambda p, b=bank, tt=t, vw=vwidth: p.tensor_copy(
                    v_tm[tt][:, 0:vw], PS[b][:, 0:vw]),
                    reads=[f"P{bank}"], writes=[f"vtm{t}"])

    def wo_phase(phase_base, with_bo):
        for h in range(NH):
            dma("Q", y_t[h][:], osp[h * 128:(h + 1) * 128, :], [f"osp{h}"], [f"y{h}"])
        emit_phase(phase_base)
        for fc in range(NEC):
            if fc + 1 < NEC:
                emit_phase(phase_base + fc + 1)
            slot, sname = slot_of(phase_base + fc)
            for tc in range(NQC):
                b = tc % 2
                for h in range(NH):
                    mm2(b, ws[slot][:, h * 128:(h + 1) * 128],
                        y_t[h][:, tc * QW:(tc + 1) * QW],
                        [sname, f"y{h}"], start=(h == 0), stop=(h == NH - 1),
                        inc=(h == NH - 1))
                if with_bo:
                    P.op("V", lambda p, bb=b, f=fc, t=tc: p.scalar_tensor_tensor(
                        x_t[f][:, t * QW:(t + 1) * QW], PS[bb][:, 0:QW],
                        bo_s[:, f:f + 1], x_t[f][:, t * QW:(t + 1) * QW],
                        ALU.add, ALU.add),
                        reads=[f"P{b}", "bo_s", f"x{fc}"], writes=[f"x{fc}"])
                else:
                    P.op("V", lambda p, bb=b, f=fc, t=tc: p.tensor_tensor(
                        x_t[f][:, t * QW:(t + 1) * QW],
                        x_t[f][:, t * QW:(t + 1) * QW], PS[bb][:, 0:QW], ALU.add),
                        reads=[f"P{b}", f"x{fc}"], writes=[f"x{fc}"])

    def ffn(phase_idx, last=False):
        layer_norm()
        emit_phase(phase_idx)
        slot, sname = slot_of(phase_idx)
        droff = 0
        uoff = NEC * 136
        def gate_chain(t):
            b = t % 2
            par = t % 2
            P.op("V", lambda p, bb=b: p.reduce_max(
                smalls[:, 0:1], PS[bb][:, 128:136], axis=AX.X),
                reads=[f"P{b}"], writes=["smalls"])
            P.op("V", lambda p: p.tensor_scalar(
                smalls[:, 1:2], smalls[:, 0:1], -1.0, None, ALU.mult),
                reads=["smalls"], writes=["smalls"])
            P.op("S", lambda p, bb=b: p.activation(
                w8[:], PS[bb][:, 128:136], AF.Exp, bias=smalls[:, 1:2]),
                reads=[f"P{b}", "smalls"], writes=["w8"])
            P.op("V", lambda p: p.reduce_sum(smalls[:, 2:3], w8[:], axis=AX.X),
                 reads=["w8"], writes=["smalls"])
            P.op("V", lambda p: p.reciprocal(smalls[:, 3:4], smalls[:, 2:3]),
                 reads=["smalls"], writes=["smalls"])
            P.op("V", lambda p: p.tensor_scalar(
                g8[:, 0:8], w8[:], 1.0, -1e9, ALU.is_ge, ALU.mult),
                reads=["w8"], writes=["g8"])
            P.op("V", lambda p: p.tensor_tensor(g8[:, 8:16], w8[:], g8[:, 0:8], ALU.add),
                 reads=["w8"], writes=["g8"])
            P.op("V", lambda p: p.reduce_max(smalls[:, 4:5], g8[:, 8:16], axis=AX.X),
                 reads=["g8"], writes=["smalls"])
            P.op("V", lambda p: p.tensor_scalar(
                g8[:, 0:8], w8[:], smalls[:, 4:5], None, ALU.is_ge),
                reads=["w8", "smalls"], writes=["g8"])
            P.op("V", lambda p: p.tensor_tensor(w8[:], w8[:], g8[:, 0:8], ALU.mult),
                 reads=["g8"], writes=["w8"])
            P.op("V", lambda p: p.tensor_scalar(
                w8[:], w8[:], smalls[:, 3:4], None, ALU.mult),
                reads=["smalls"], writes=["w8"])
            P.op("S", lambda p, bb=b: p.activation(
                vars_[:, 0:128], PS[bb][:, 0:128], AF.Exp, scale=-1.0),
                reads=[f"P{b}"], writes=["vars"])
            P.op("V", lambda p: p.tensor_scalar(vars_[:, 0:128], vars_[:, 0:128], 1.0, None, ALU.add),
                 reads=["vars"], writes=["vars"])
            P.op("V", lambda p: p.reciprocal(vars_[:, 128:256], vars_[:, 0:128]),
                 reads=["vars"], writes=["vars"])
            P.op("V", lambda p, bb=b, pa=par: p.tensor_tensor(
                a_sb[pa][:], PS[bb][:, 0:128], vars_[:, 128:256], ALU.mult),
                reads=[f"P{b}", "vars"], writes=[f"a_sb{par}"])
            for ex in range(8):
                P.op("V", lambda p, e=ex, pa=par: p.tensor_scalar(
                    ag_sb[pa][:, e * 16:(e + 1) * 16], a_sb[pa][:, e * 16:(e + 1) * 16],
                    w8[:, e:e + 1], None, ALU.mult),
                    reads=[f"a_sb{par}", "w8"], writes=[f"ag_sb{par}"], inc=(ex == 7))

        def transp(t):
            par = t % 2
            tb = 2 + (t % 2)
            P.op("T", lambda p, bb=tb, pa=par: p.transpose(
                PS[bb][:, 0:128], ag_sb[pa][:], ident_s[:]),
                reads=[f"ag_sb{par}", "ident_s"], writes=[f"P{tb}"])
            P.op("S", lambda p, bb=tb, tt=t: p.activation(
                agT[:, tt * 128:(tt + 1) * 128], PS[bb][:, 0:128], AF.Identity),
                reads=[f"P{tb}"], writes=["agT"])

        for t in range(NTC):
            b = t % 2
            for ec in range(NEC):
                mmN(b, 136, y_t[ec][:, t * 128:(t + 1) * 128],
                    ws[slot][:, droff + ec * 136: droff + (ec + 1) * 136],
                    [f"y{ec}", sname], start=(ec == 0), stop=(ec == NEC - 1),
                    inc=(ec == NEC - 1))
            if t > 0:
                transp(t - 1)
            gate_chain(t)
        transp(NTC - 1)
        stage3 = [stage[0][:], stage[1][:], vars_[:, 0:QW]]
        for fc in range(NEC):
            for tc in range(NQC):
                idx = fc * NQC + tc
                b = 4 + (idx % 4)
                mm2(b, ws[slot][:, uoff + fc * 128: uoff + (fc + 1) * 128],
                    agT[:, tc * QW:(tc + 1) * QW], [sname, "agT"],
                    start=True, stop=True)
                if last:
                    par = idx % 3
                    sname3 = ["stage0", "stage1", "vars"][par]
                    sap = stage3[par]
                    P.op("V", lambda p, bb=b, f=fc, t=tc, sa=sap: p.tensor_tensor(
                        sa, x_t[f][:, t * QW:(t + 1) * QW],
                        PS[bb][:, 0:QW], ALU.add),
                        reads=[f"P{b}", f"x{fc}"], writes=[sname3])
                    dma("Q", out[fc * 128:(fc + 1) * 128, tc * QW:(tc + 1) * QW],
                        sap, [sname3], ["outhbm"])
                else:
                    P.op("V", lambda p, bb=b, f=fc, t=tc: p.tensor_tensor(
                        x_t[f][:, t * QW:(t + 1) * QW],
                        x_t[f][:, t * QW:(t + 1) * QW], PS[bb][:, 0:QW], ALU.add),
                        reads=[f"P{b}", f"x{fc}"], writes=[f"x{fc}"])

    gpv = NG // nvp
    ph_wva, ph_wqk = {}, {}
    for pair in range(NH // 2):
        ph_wva[pair] = add_phase([(0, NEC * 256, d_wva[pair, :, :])])
        for hh in range(2):
            h = pair * 2 + hh
            ph_wqk[h] = add_phase([(0, NEC * 128, d_wqa[h, :, :]),
                                   (NEC * 128, NEC * 128, d_wka[h, :, :])])
    ph_woa = [add_phase([(0, NH * 128, d_woa[fc, :, :])]) for fc in range(NEC)]
    PH_FFN1 = add_phase([(0, NEC * 136, d_dr[:, :]), (NEC * 136, E, d_u[:, :])])
    ph_wvb, ph_wkb, ph_wqb = {}, {}, {}
    for vp in range(nvp):
        ph_wvb[vp] = add_phase([(0, NEC * vwb, d_wvb[vp, :, :])])
        for gg in range(gpv):
            g = vp * gpv + gg
            ph_wkb[g] = add_phase([(0, NEC * 128, d_wkb[g, :, :])])
            for r in range(rep):
                h = g * rep + r
                ph_wqb[h] = add_phase([(0, NEC * 128, d_wqb[h, :, :])])
    ph_wob = [add_phase([(0, NH * 128, d_wob[fc, :, :])]) for fc in range(NEC)]
    PH_FFN2 = add_phase([(0, NEC * 136, d_dr[:, :]), (NEC * 136, E, d_u[:, :])])

    layer_norm()
    emit_phase(0)
    for pair in range(NH // 2):
        emit_phase(ph_wva[pair])
        emit_phase(ph_wva[pair] + 1)
        vslot, vsname = slot_of(ph_wva[pair])
        v_proj(vsname, ws[vslot], 256)
        for hh in range(2):
            h = pair * 2 + hh
            phq = ph_wqk[h]
            emit_phase(phq)
            emit_phase(phq + 1)
            qslot, qsname = slot_of(phq)
            proj(q_sb, "q_sb", qsname, ws[qslot], 0, None, None)
            proj(k_sb, "k_sb", qsname, ws[qslot], NEC * 128, None, None)
            rope(q_sb, "q_sb", 6)
            rope(k_sb, "k_sb", 6)
            fill_maxnorm2(q_sb, "q_sb", 0, 6)
            fill_maxnorm2(k_sb, "k_sb", 1, 6)
            exp_bias(scale_a)
            attn_head(h, slice(hh * 128, (hh + 1) * 128), scale_a)
    wo_phase(ph_woa[0], with_bo=False)

    ffn(PH_FFN1)

    layer_norm()
    dma("G", cos_s[:], d_cosB[:, :], ["cos_s"], ["cos_s"])
    dma("G", sin_s[:], d_sinB[:, :], ["sin_s"], ["sin_s"])
    emit_phase(ph_wvb[0])
    for vp in range(nvp):
        emit_phase(ph_wvb[vp])
        emit_phase(ph_wvb[vp] + 1)
        vslot, vsname = slot_of(ph_wvb[vp])
        dma("G", bv_pad[0:1, :], d_bv[:, vp * vwb:(vp + 1) * vwb], ["bv_pad"], ["bv_pad"])
        v_proj(vsname, ws[vslot], vwb, bvslice=slice(vp * vwb, (vp + 1) * vwb))
        for gg in range(gpv):
            g = vp * gpv + gg
            phk = ph_wkb[g]
            emit_phase(phk)
            emit_phase(phk + 1)
            kslot, ksname = slot_of(phk)
            proj(k_sb, "k_sb", ksname, ws[kslot], 0, bk_s[:, g:g + 1], "bk_s")
            rmsnorm(k_sb, "k_sb", kns_s[:], "kns_s", eps6[:], "eps6")
            rope(k_sb, "k_sb", 6)
            fill_maxnorm2(k_sb, "k_sb", 1, 6)
            for r in range(rep):
                h = g * rep + r
                phq = ph_wqb[h]
                emit_phase(phq)
                emit_phase(phq + 1)
                qslot, qsname = slot_of(phq)
                proj(q_sb, "q_sb", qsname, ws[qslot], 0, bq_s[:, h:h + 1], "bq_s")
                rmsnorm(q_sb, "q_sb", qns_s[:], "qns_s", eps6[:], "eps6")
                rope(q_sb, "q_sb", 6)
                fill_maxnorm2(q_sb, "q_sb", 0, 6)
                exp_bias(scale_b)
                attn_head(h, slice(gg * 128, (gg + 1) * 128), scale_b)
    wo_phase(ph_wob[0], with_bo=True)

    ffn(PH_FFN2, last=True)

    P.final_wait("Q", "Q")
    P.final_wait("G", "G")
    assert len(emitted) == len(phases), (len(emitted), len(phases))

    import contextlib
    with contextlib.ExitStack() as stack:
        sems = {e: stack.enter_context(nc.semaphore(f"sem{e}")) for e in Prog.ENGS}
        block = stack.enter_context(nc.Block())

        @block.tensor
        def _(t):
            for f in P.ops["T"]:
                f(t, sems)

        @block.scalar
        def _(s):
            for f in P.ops["S"]:
                f(s, sems)

        @block.vector
        def _(v):
            for f in P.ops["V"]:
                f(v, sems)

        @block.sync
        def _(q):
            for f in P.ops["Q"]:
                f(q, sems)

        @block.gpsimd
        def _(g):
            for f in P.ops["G"]:
                f(g, sems)

    return nc


def make_in_maps(inputs):
    x = np.asarray(inputs["L0_attn_a_x"], np.float32)
    B, T, E = x.shape
    NG = np.asarray(inputs["L0_attn_b_wk"]).shape[1] // 128
    consts = host_constants(T, E)
    w = layout_weights(inputs, T, E)
    shared = {
        "cosA": consts["cosA"], "sinA": consts["sinA"],
        "cosB": consts["cosB"], "sinB": consts["sinB"],
        "Smat": consts["Smat"], "ident": consts["ident"], "identb": consts["identb"], "masks": consts["masks"],
        "wqa": w["wqa"], "wka": w["wka"], "wva": w["wva"], "woa": w["woa"],
        "wqb": w["wqb"], "wkb": w["wkb"], "wvb": w["wvb"], "wob": w["wob"],
        "dr": w["dr"], "u": w["u"],
        "bq": w["bq"], "bk": w["bk"], "bo": w["bo"], "bv": w["bv"],
        "qns": w["qns"], "kns": w["kns"],
    }
    in_maps = []
    for b in range(B):
        m = dict(shared)
        m["xin"] = np.ascontiguousarray(x[b].T)
        in_maps.append(m)
    return in_maps, T, E, NG


def kernel(**inputs):
    x = np.asarray(inputs["L0_attn_a_x"], np.float32)
    B, T, E = x.shape
    assert B == N_CORES, (B, N_CORES)
    in_maps, T, E, NG = make_in_maps(inputs)
    nc = build_program(T, E, NG)
    res = run_bass_kernel_spmd(nc, in_maps, core_ids=list(range(N_CORES)))
    outs = [np.ascontiguousarray(res.results[b]["out"].T) for b in range(B)]
    return np.stack(outs).astype(np.float32)
